# version 13
# baseline (speedup 1.0000x reference)
import numpy as np
import ml_dtypes
from contextlib import ExitStack

import concourse.bass as bass
import concourse.mybir as mybir
from concourse.bass_utils import run_bass_kernel_spmd

F32 = mybir.dt.float32
BF16 = mybir.dt.bfloat16
AF = mybir.ActivationFunctionType
ALU = mybir.AluOpType
AX = mybir.AxisListType

B, S, D, NH, DH = 4, 4096, 1024, 16, 64
TOK = 2048
NE, M = 16, 512
NSLOT, NBLK, BLK = 8192, 32, 256
EPS = 1e-5

IDENT = 0
ONES = 128
WO_C = 256
GWT = 1280
BQ, BK, BV, BQS = 1408, 1412, 1416, 1420
NEGC = 1424
NRM = 1426
STAT = 1434
IOTA = 1466
IOTA16B = 1482
TILEST = 1498
THRESH = 1530
BASE_TM = 1538
PBASE = 1554
PER_END = 1590

RA = PER_END
RB = RA + 16384
RC = RB + 16384
SP = RC + 12288
INDP = SP
RT_T = SP + 2048
RTS = SP + 2360
RTR = SP + 2616
HMID = SP + 2872
RC_XR = HMID
SGQ = HMID + 2048
NCOLS = SGQ + 512 + 24

RA_K = RA
RA_Q = RA + 4096
RA_V = RA + 8192
RA_XN2 = RA

RB_A = RB
RB_VF = RB + 3072
RB_BC = RB + 3584
RB_ZS = RB + 4608
RB_ZTMP = RB + 8704
RB_RD = RB + 9216
RB_ACC = RB

RC_X = RC
RC_XN = RC + 2048
RC_W = RC
RC_XNC = RC + 6144
RC_R1 = RC + 9216
RC_ZH = RC + 11264
RC_ZF = RC + 11520
RC_ZT = RC + 11776
RC_WG = RC
RC_WU = RC + 4096
RC_WD = RC + 8192

RT_SIG = 0
RT_SCT = 128
RT_M8 = 144
RT_DEN = 152
RT_RDN = 153
RT_IND = 160
RT_OH0 = 176
RT_OH1 = 192
RT_TMP = 208
RT_V = 224
RT_VAL = 288
RT_P = 292
RT_TID = 300
XG_TM = 0
XG_FM = 4096
EOB = 6144
GB = 8192


class Sem:
    def __init__(self, h):
        self.h = h
        self.n = 0


class Prog:
    ENGINES = ("sync", "tensor", "vector", "scalar", "gpsimd")

    def __init__(self):
        self.ops = {e: [] for e in self.ENGINES}

    def op(self, eng, fn, waits=(), inc=None, k=1):
        tgts = tuple((s.h, int(v)) for (s, v) in waits if v > 0)
        if inc is not None:
            inc.n += k
        inch = inc.h if inc is not None else None

        def run(E, fn=fn, tgts=tgts, inch=inch, k=k):
            for h, v in tgts:
                E.wait_ge(h, v)
            ins = fn(E)
            if inch is not None:
                ins.then_inc(inch, k)
        self.ops[eng].append(run)
        return inc.n if inc is not None else None

    def wait(self, eng, waits):
        tgts = tuple((s.h, int(v)) for (s, v) in waits if v > 0)
        if not tgts:
            return

        def run(E, tgts=tgts):
            for h, v in tgts:
                E.wait_ge(h, v)
        self.ops[eng].append(run)


def build_program():
    nc = bass.Bass("TRN2", target_bir_lowering=False)

    x_d = nc.dram_tensor("x", [S, D], F32, kind="ExternalInput")
    xres_d = nc.dram_tensor("xres", [TOK, D], F32, kind="ExternalInput")
    wq_d = nc.dram_tensor("wq", [D, 512], F32, kind="ExternalInput")
    wk_d = nc.dram_tensor("wk", [D, 512], F32, kind="ExternalInput")
    wv_d = nc.dram_tensor("wv", [D, 512], F32, kind="ExternalInput")
    bq_d = nc.dram_tensor("bq", [128, 4], F32, kind="ExternalInput")
    bk_d = nc.dram_tensor("bk", [128, 4], F32, kind="ExternalInput")
    bv_d = nc.dram_tensor("bv", [128, 4], F32, kind="ExternalInput")
    wo_d = nc.dram_tensor("wo", [64, 1024], F32, kind="ExternalInput")
    bo_d = nc.dram_tensor("bo", [1, 1024], F32, kind="ExternalInput")
    gwt_d = nc.dram_tensor("gwt", [D, NE], F32, kind="ExternalInput")
    wsg_d = nc.dram_tensor("wsg", [D, M], F32, kind="ExternalInput")
    wsu_d = nc.dram_tensor("wsu", [D, M], F32, kind="ExternalInput")
    wsd_d = nc.dram_tensor("wsd", [M, D], F32, kind="ExternalInput")
    wgu_d = nc.dram_tensor("wgu", [NE * 2 * D * M], F32, kind="ExternalInput")
    wed_d = nc.dram_tensor("wed", [NE * M, D], F32, kind="ExternalInput")
    ident_d = nc.dram_tensor("ident", [128, 128], F32, kind="ExternalInput")
    iota_d = nc.dram_tensor("iota", [128, 16], F32, kind="ExternalInput")
    iota16_d = nc.dram_tensor("iota16", [128, 16], F32, kind="ExternalInput")
    tilest_d = nc.dram_tensor("tilest", [128, 32], F32, kind="ExternalInput")
    thresh_d = nc.dram_tensor("thresh", [128, 8], F32, kind="ExternalInput")
    masks_d = nc.dram_tensor("masks", [4, 128, 512], BF16, kind="ExternalInput")
    out_d = nc.dram_tensor("out", [TOK, D], F32, kind="ExternalOutput")

    xn1_d = nc.dram_tensor("xn1", [D, S], F32)
    ztm_d = nc.dram_tensor("ztm", [S, 64], F32)
    zhalf_d = nc.dram_tensor("zhalf", [TOK, 64], F32)
    xn2tm_d = nc.dram_tensor("xn2tm", [TOK, D], F32)
    tok_d = nc.dram_tensor("tok", [NSLOT, 1], F32)
    eo_d = nc.dram_tensor("eo", [NSLOT, D], F32)

    with ExitStack() as ctx:
        PB = nc.alloc_sbuf_tensor("PB", [128, NCOLS], F32)
        MB = nc.alloc_sbuf_tensor("MB", [128, 2048], BF16)
        U32 = nc.alloc_sbuf_tensor("U32", [128, 80], mybir.dt.uint32)
        pb = [nc.alloc_psum_tensor(f"pb{i}", [128, 512], F32) for i in range(8)]

        def mks(name):
            return Sem(ctx.enter_context(nc.semaphore(name)))

        s_pe, s_vec, s_act, s_cc = mks("s_pe"), mks("s_vec"), mks("s_act"), mks("s_cc")
        s_lc = mks("s_lc")
        s_sm = mks("s_sm")
        s_lx = [mks("s_lx0"), mks("s_lx1")]
        s_s1 = [mks("s_s10"), mks("s_s11")]
        s_lw = [mks("s_lw0"), mks("s_lw1")]
        s_lxn = [mks("s_lxn0"), mks("s_lxn1")]
        s_lzh = [mks("s_lzh0"), mks("s_lzh1")]
        s_lxr = [mks("s_lxr0"), mks("s_lxr1")]
        s_szt = [mks("s_szt0"), mks("s_szt1")]
        s_lwg, s_lwu, s_lwd = mks("s_lwg"), mks("s_lwu"), mks("s_lwd")
        s_wdg = mks("s_wdg")
        s_wgb = [mks(f"s_wgb{b}") for b in range(NBLK)]
        s_so = mks("s_so")
        s_sxn2 = [mks("s_sxn20"), mks("s_sxn21")]
        s_stk = mks("s_stk")
        s_ltk = [mks("s_ltk0"), mks("s_ltk1")]
        s_lxg = [mks("s_lxg0"), mks("s_lxg1")]
        s_seo = [mks(f"s_seo{i}") for i in range(4)]
        s_lgb = [mks("s_lgb0"), mks("s_lgb1")]
        block = ctx.enter_context(nc.Block())

        P = Prog()

        def col(base, a, b, r0=0, r1=128):
            return PB[r0:r1, base + a: base + b]

        P.op("sync", lambda E: E.dma_start(out=col(IDENT, 0, 128), in_=ident_d[:]), inc=s_lc, k=16)
        P.op("sync", lambda E: E.dma_start(out=col(WO_C, 0, 1024, 0, 64), in_=wo_d[:]), inc=s_lc, k=16)
        P.op("sync", lambda E: E.dma_start(out=col(WO_C, 0, 1024, 64, 65), in_=bo_d[:]), inc=s_lc, k=16)
        P.op("sync", lambda E: E.dma_start(
            out=bass.AP(PB, GWT, [[NCOLS, 128], [16, 8], [1, 16]]),
            in_=bass.AP(gwt_d, 0, [[16, 128], [2048, 8], [1, 16]])), inc=s_lc, k=16)
        for base, dd in ((IOTA, iota_d), (IOTA16B, iota16_d)):
            P.op("sync", lambda E, dd=dd, base=base: E.dma_start(out=col(base, 0, 16), in_=dd[:]),
                 inc=s_lc, k=16)
        P.op("sync", lambda E: E.dma_start(out=col(TILEST, 0, 32), in_=tilest_d[:]), inc=s_lc, k=16)
        P.op("sync", lambda E: E.dma_start(out=col(THRESH, 0, 8), in_=thresh_d[:]), inc=s_lc, k=16)
        for base, dd in ((BQ, bq_d), (BK, bk_d), (BV, bv_d)):
            P.op("sync", lambda E, dd=dd, base=base: E.dma_start(out=col(base, 0, 4), in_=dd[:]),
                 inc=s_lc, k=16)
        for r in range(4):
            P.op("gpsimd", lambda E, r=r: E.dma_start(out=MB[:, 512 * r: 512 * r + 512], in_=masks_d[r]),
                 inc=s_sm, k=16)
        ld_const = s_lc.n
        mask_ld = s_sm.n
        P.op("vector", lambda E: E.memset(col(ONES, 0, 128), 1.0), inc=s_vec)
        P.op("vector", lambda E: E.memset(
            bass.AP(PB, RA_V + 64, [[NCOLS, 128], [65, 64]]), 1.0), inc=s_vec)
        vones_set = s_vec.n
        P.op("vector", lambda E: E.memset(col(RC_ZF, 0, 256, 64, 65), 1.0), inc=s_vec)
        P.op("vector", lambda E: E.tensor_scalar(col(BQS, 0, 4), col(BQ, 0, 4), 0.125, None, ALU.mult),
             waits=[(s_lc, ld_const)], inc=s_vec)
        P.op("vector", lambda E: E.tensor_scalar(col(WO_C, 0, 1024, 64, 65), col(WO_C, 0, 1024, 64, 65),
                                                 16.0, None, ALU.mult),
             waits=[(s_lc, ld_const)], inc=s_vec)
        vec_const = s_vec.n

        def emit_ln(x_fn, dump_fn, xn_fn, sb, W, x_ready, dump_free=(), xn_free=()):
            st_ = lambda i: col(STAT, sb + i, sb + i + 1)
            v_rs = P.op("vector", lambda E: E.reduce_sum(st_(0), x_fn(), axis=AX.X),
                        waits=list(x_ready) + [(s_vec, s_vec.n)], inc=s_vec)
            a_sq = P.op("scalar", lambda E: E.activation(dump_fn(), x_fn(), AF.Square,
                                                         accum_out=st_(1)),
                        waits=list(x_ready) + list(dump_free) + [(s_act, s_act.n)], inc=s_act)
            v_m = P.op("vector", lambda E: E.tensor_scalar(st_(2), st_(0), 1.0 / W, None, ALU.mult),
                       waits=[(s_vec, v_rs)], inc=s_vec)
            v_e2 = P.op("vector", lambda E: E.tensor_scalar(st_(3), st_(1), 1.0 / W, EPS,
                                                            ALU.mult, ALU.add),
                        waits=[(s_act, a_sq)], inc=s_vec)
            v_ms = P.op("vector", lambda E: E.tensor_scalar(st_(4), st_(2), st_(2), None, ALU.mult),
                        waits=[(s_vec, v_m)], inc=s_vec)
            v_var = P.op("vector", lambda E: E.tensor_sub(st_(3), st_(3), st_(4)),
                         waits=[(s_vec, v_ms), (s_vec, v_e2)], inc=s_vec)
            a_sd = P.op("scalar", lambda E: E.activation(st_(5), st_(3), AF.Sqrt),
                        waits=[(s_vec, v_var)], inc=s_act)
            v_rd = P.op("vector", lambda E: E.reciprocal(st_(6), st_(5)),
                        waits=[(s_act, a_sd)], inc=s_vec)
            v_xn = P.op("vector", lambda E: E.tensor_scalar(xn_fn(), x_fn(), st_(2), st_(6),
                                                            ALU.subtract, ALU.mult),
                        waits=[(s_vec, v_rd), (s_act, a_sq)] + list(xn_free), inc=s_vec)
            return v_xn, a_sq

        blk_store_done = [0, 0]
        xtile_free = [0, 0]
        xntile_free = [0, 0]
        trb_free = [0, 0]
        ph1_tb = []
        for tb in range(8):
            rpar = tb % 2
            blk = RA + 4096 * rpar
            last_copy = 0
            for tt in range(4):
                t128 = tb * 4 + tt
                par = t128 % 2
                xap = lambda par=par: col(RC_X, 1024 * par, 1024 * par + 1024)
                xnap = lambda par=par: col(RC_XN, 1024 * par, 1024 * par + 1024)
                ldv = P.op("sync", lambda E, t128=t128, par=par: E.dma_start(
                    out=col(RC_X, 1024 * par, 1024 * par + 1024),
                    in_=x_d[128 * t128: 128 * t128 + 128, :]),
                    waits=[(s_vec, xtile_free[par])], inc=s_lx[par], k=16)
                v_xn, _ = emit_ln(xap, xnap, xnap, 16 * par, 1024,
                                  [(s_lx[par], ldv)],
                                  dump_free=[(s_pe, xntile_free[par])],
                                  xn_free=[(s_pe, xntile_free[par])])
                xtile_free[par] = v_xn
                for dch in range(8):
                    bk = dch % 2
                    pt = P.op("tensor", lambda E, par=par, dch=dch, bk=bk: E.transpose(
                        pb[bk][:, 0:128],
                        col(RC_XN, 1024 * par + 128 * dch, 1024 * par + 128 * dch + 128),
                        col(IDENT, 0, 128)),
                        waits=[(s_vec, v_xn), (s_vec, trb_free[bk]), (s_lc, ld_const),
                               (s_s1[rpar], blk_store_done[rpar])],
                        inc=s_pe)
                    vc = P.op("vector", lambda E, blk=blk, dch=dch, tt=tt, bk=bk: E.tensor_copy(
                        col(0, blk + 512 * dch + 128 * tt, blk + 512 * dch + 128 * tt + 128),
                        pb[bk][:, 0:128]),
                        waits=[(s_pe, pt)], inc=s_vec)
                    trb_free[bk] = vc
                    last_copy = vc
                xntile_free[par] = s_pe.n
            for dch in range(8):
                P.op("gpsimd", lambda E, blk=blk, dch=dch, tb=tb: E.dma_start(
                    out=xn1_d[128 * dch: 128 * dch + 128, 512 * tb: 512 * tb + 512],
                    in_=col(0, blk + 512 * dch, blk + 512 * dch + 512)),
                    waits=[(s_vec, last_copy)], inc=s_s1[rpar], k=16)
            blk_store_done[rpar] = s_s1[rpar].n
            ph1_tb.append((s_s1[rpar], s_s1[rpar].n))
        ph1_st = [(s_s1[0], s_s1[0].n), (s_s1[1], s_s1[1].n)]
        ph1_vec = s_vec.n

        wload_done = [0, 0, 0, 0]
        proj_done = [0, 0, 0, 0]

        def load_wset(p):
            for mi, wd in enumerate((wq_d, wk_d, wv_d)):
                for dch in range(8):
                    P.op("sync", lambda E, wd=wd, mi=mi, p=p, dch=dch: E.dma_start(
                        out=col(RC_W + 3072 * (p % 2) + 1024 * mi,
                                128 * dch, 128 * dch + 128),
                        in_=wd[128 * dch: 128 * dch + 128, 128 * p: 128 * p + 128]),
                        waits=([(s_vec, ph1_vec if p < 2 else 0),
                                (s_pe, proj_done[p - 2] if p >= 2 else 0)]
                               if dch == 0 and mi == 0 else []),
                        inc=s_lw[p % 2], k=16)
            wload_done[p] = s_lw[p % 2].n

        load_wset(0)
        load_wset(1)

        xnc_free = [0, 0]
        bank_r = {2: 0, 3: 0, 6: 0, 7: 0}
        exp_free = [[0, 0], [0, 0]]
        av_free3 = [[0, 0, 0], [0, 0, 0]]
        z_cons = [0, 0]
        bc_cons = [0, 0]
        rd_free = [0, 0]
        sgq_free = [0, 0]
        vf_free = 0

        negc_done = 0
        zt_free = [0, 0]
        zt_vfree = [0, 0]
        NRM_ACC = ((0, 1), (5, 6))
        NRM_SCR = (2, 7)

        def emit_norm_step(ch, h, nco, base, first):
            r0, r1 = 64 * h, 64 * h + 64
            a_sq = P.op("scalar", lambda E, base=base, ch=ch, r0=r0, r1=r1: E.activation(
                PB[r0:r1, SGQ: SGQ + 512],
                PB[r0:r1, base + 512 * ch: base + 512 * ch + 512], AF.Square),
                waits=[(s_pe, sgq_free[h]), (s_act, s_act.n)], inc=s_act)
            p_n = P.op("tensor", lambda E, r0=r0, r1=r1, h=h: E.matmul(
                pb[6 + h][0:1, 0:512],
                PB[r0:r1, ONES: ONES + 1], PB[r0:r1, SGQ: SGQ + 512],
                start=True, stop=True),
                waits=[(s_act, a_sq), (s_vec, bank_r[6 + h])], inc=s_pe)
            sgq_free[h] = p_n
            sc = NRM_SCR[h]
            v_mx = P.op("vector", lambda E, sc=sc, h=h: E.reduce_max(
                col(NRM, sc, sc + 1, 0, 1), pb[6 + h][0:1, 0:512], axis=AX.X),
                waits=[(s_pe, p_n), (s_vec, s_vec.n)], inc=s_vec)
            ac = NRM_ACC[h][nco]
            if first:
                v_a = P.op("vector", lambda E, ac=ac, sc=sc: E.tensor_copy(
                    col(NRM, ac, ac + 1, 0, 1), col(NRM, sc, sc + 1, 0, 1)),
                    waits=[(s_vec, v_mx)], inc=s_vec)
            else:
                v_a = P.op("vector", lambda E, ac=ac, sc=sc: E.tensor_tensor(
                    col(NRM, ac, ac + 1, 0, 1), col(NRM, ac, ac + 1, 0, 1),
                    col(NRM, sc, sc + 1, 0, 1), ALU.max),
                    waits=[(s_vec, v_mx)], inc=s_vec)
            bank_r[6 + h] = v_mx

        for p in range(4):
            pair_guard = (([(s_act, v) for hh in exp_free for v in hh] +
                           [(s_vec, bank_r[2]), (s_vec, bank_r[3])])
                          if p > 0 else [])
            evac_k = [0, 0]
            evac_v = [0, 0]
            evac_q = [0, 0]
            for tc in range(8):
                par = tc % 2
                pe_projs = 0
                wq_off = RC_W + 3072 * (p % 2)
                for dch in range(8):
                    ci = tc * 8 + dch
                    cpar = ci % 2
                    ldv = P.op("sync", lambda E, dch=dch, tc=tc, cpar=cpar: E.dma_start(
                        out=col(RC_XNC, 512 * cpar, 512 * cpar + 512),
                        in_=xn1_d[128 * dch: 128 * dch + 128, 512 * tc: 512 * tc + 512]),
                        waits=ph1_st + [(s_pe, xnc_free[cpar]),
                                        (s_lw[p % 2], wload_done[p])],
                        inc=s_lxn[cpar], k=16)
                    mv = lambda cpar=cpar: col(RC_XNC, 512 * cpar, 512 * cpar + 512)
                    first, last = dch == 0, dch == 7
                    w8 = [(s_lxn[cpar], ldv)]
                    if first:
                        w8 += [(s_act, evac_k[par]), (s_act, evac_v[par]), (s_act, evac_q[par])]
                        if tc < 2:
                            w8 += pair_guard
                    P.op("tensor", lambda E, wq_off=wq_off, dch=dch, par=par, mv=mv, first=first, last=last: E.matmul(
                        pb[0 + par][:], col(wq_off + 1024, 128 * dch, 128 * dch + 128), mv(),
                        start=first, stop=last), waits=w8)
                    P.op("tensor", lambda E, wq_off=wq_off, dch=dch, par=par, mv=mv, first=first, last=last: E.matmul(
                        pb[2 + par][:], col(wq_off + 2048, 128 * dch, 128 * dch + 128), mv(),
                        start=first, stop=last))
                    pe_projs = P.op("tensor", lambda E, wq_off=wq_off, dch=dch, par=par, mv=mv, first=first, last=last: E.matmul(
                        pb[4 + par][:], col(wq_off, 128 * dch, 128 * dch + 128), mv(),
                        start=first, stop=last), inc=s_pe)
                    xnc_free[cpar] = pe_projs
                evac_k[par] = P.op("scalar", lambda E, tc=tc, par=par, p=p: E.activation(
                    col(RA_K, 512 * tc, 512 * tc + 512), pb[0 + par][:],
                    AF.Identity, bias=col(BK, p, p + 1)),
                    waits=[(s_pe, pe_projs), (s_lc, ld_const), (s_act, s_act.n)], inc=s_act)
                evac_v[par] = P.op("scalar", lambda E, par=par, p=p: E.activation(
                    col(RB_VF, 0, 512), pb[2 + par][:],
                    AF.Identity, bias=col(BV, p, p + 1)),
                    waits=[(s_pe, pe_projs), (s_pe, vf_free)], inc=s_act)
                evac_q[par] = P.op("scalar", lambda E, tc=tc, par=par, p=p: E.activation(
                    col(RA_Q, 512 * tc, 512 * tc + 512), pb[4 + par][:],
                    AF.Identity, bias=col(BQS, p, p + 1), scale=0.125),
                    waits=[(s_pe, pe_projs)], inc=s_act)
                for j in range(4):
                    bk = 6 + j % 2
                    k4 = tc * 4 + j
                    pt = P.op("tensor", lambda E, j=j, bk=bk: E.transpose(
                        pb[bk][:, 0:128], col(RB_VF, 128 * j, 128 * j + 128), col(IDENT, 0, 128)),
                        waits=[(s_act, evac_v[par]), (s_vec, bank_r[bk])],
                        inc=s_pe)
                    P.op("vector", lambda E, k4=k4, bk=bk: E.tensor_copy(
                        col(RA_V, 130 * k4, 130 * k4 + 64), pb[bk][:, 0:64]),
                        waits=[(s_pe, pt), (s_vec, vones_set)], inc=s_vec)
                    bank_r[bk] = P.op("vector", lambda E, k4=k4, bk=bk: E.tensor_copy(
                        col(RA_V, 130 * k4 + 65, 130 * k4 + 129), pb[bk][:, 64:128]),
                        inc=s_vec)
                vf_free = s_pe.n
                if tc > 0:
                    for nco, base in ((0, RA_Q), (1, RA_K)):
                        for h in range(2):
                            emit_norm_step(tc - 1, h, nco, base, tc == 1)
            proj_done[p] = s_pe.n
            evac_all = [(s_act, s_act.n), (s_vec, s_vec.n)]
            if p + 2 < 4:
                load_wset(p + 2)

            for nco, base in ((0, RA_Q), (1, RA_K)):
                for h in range(2):
                    emit_norm_step(7, h, nco, base, False)
            for h in range(2):
                a1 = P.op("scalar", lambda E, h=h: E.activation(
                    col(NRM, 3, 4, 0, 1), col(NRM, NRM_ACC[h][0], NRM_ACC[h][0] + 1, 0, 1),
                    AF.Sqrt),
                    waits=[(s_vec, s_vec.n), (s_act, s_act.n)], inc=s_act)
                a2 = P.op("scalar", lambda E, h=h: E.activation(
                    col(NRM, 4, 5, 0, 1), col(NRM, NRM_ACC[h][1], NRM_ACC[h][1] + 1, 0, 1),
                    AF.Sqrt),
                    inc=s_act)
                v1 = P.op("vector", lambda E: E.tensor_tensor(
                    col(NRM, 3, 4, 0, 1), col(NRM, 3, 4, 0, 1), col(NRM, 4, 5, 0, 1), ALU.mult),
                    waits=[(s_act, a2), (s_vec, s_vec.n)], inc=s_vec)
                v2 = P.op("vector", lambda E: E.tensor_scalar(
                    col(NRM, 3, 4, 0, 1), col(NRM, 3, 4, 0, 1), -1.0, None, ALU.mult),
                    waits=[(s_vec, v1)], inc=s_vec)
                p_bc = P.op("tensor", lambda E, h=h: E.matmul(
                    pb[6 + h][:, 0:1], PB[0:1, ONES: ONES + 128], col(NRM, 3, 4, 0, 1),
                    start=True, stop=True),
                    waits=[(s_vec, v2), (s_vec, bank_r[6 + h])], inc=s_pe)
                v_nc = P.op("vector", lambda E, h=h: E.tensor_copy(
                    col(NEGC, h, h + 1), pb[6 + h][:, 0:1]),
                    waits=[(s_pe, p_bc), (s_act, s_act.n), (s_vec, s_vec.n)], inc=s_vec)
                bank_r[6 + h] = v_nc
            negc_done = s_vec.n

            for qb in range(8):
                nkc = 4 * qb + 4
                qp = qb % 2
                avb = 2 if qp == 0 else 6
                pbb = 6 if qp == 0 else 2
                e_slot = [[0, 0, 0], [0, 0, 0]]
                av_ab = [0, 0]
                for kc in range(nkc + 2):
                    if kc < nkc:
                        pr = kc % 2
                        sl = kc % 3
                        sb = (0, 1) if pr == 0 else (4, 5)
                        pe_ss = [0, 0]
                        for h in range(2):
                            r0, r1 = 64 * h, 64 * h + 64
                            w_s = [(s_act, exp_free[h][pr]), (s_vec, negc_done)]
                            pe_ss[h] = P.op("tensor", lambda E, r0=r0, r1=r1, kc=kc, qb=qb, b=sb[h]: E.matmul(
                                pb[b][:], PB[r0:r1, RA_K + 128 * kc: RA_K + 128 * kc + 128],
                                PB[r0:r1, RA_Q + 512 * qb: RA_Q + 512 * qb + 512],
                                start=True, stop=True),
                                waits=w_s, inc=s_pe)
                        diag = kc >= 4 * qb
                        for h in range(2):
                            if diag:
                                r = kc - 4 * qb
                                v_m = P.op("vector", lambda E, b=sb[h], r=r: E.tensor_add(
                                    pb[b][:], pb[b][:], MB[:, 512 * r: 512 * r + 512]),
                                    waits=[(s_pe, pe_ss[h]), (s_sm, mask_ld), (s_vec, s_vec.n)],
                                    inc=s_vec)
                                w_exp = [(s_vec, v_m)]
                            else:
                                w_exp = [(s_pe, pe_ss[h])]
                            e_slot[h][sl] = P.op("scalar", lambda E, h=h, sl=sl, b=sb[h]: E.activation(
                                col(RB_A, 512 * (3 * h + sl), 512 * (3 * h + sl) + 512), pb[b][:],
                                AF.Exp, bias=col(NEGC, h, h + 1)),
                                waits=w_exp + [(s_pe, av_free3[h][sl])], inc=s_act)
                            exp_free[h][pr] = e_slot[h][sl]
                    if kc >= 2:
                        kp = kc - 2
                        slp = kp % 3
                        last = kp == nkc - 1
                        for h in range(2):
                            w_a = [(s_act, e_slot[h][slp])]
                            if kp == 0:
                                w_a += [(s_vec, bank_r[avb + h]), (s_vec, negc_done)]
                            av_ab[h] = P.op("tensor", lambda E, kp=kp, h=h, slp=slp, avb=avb, last=last: E.matmul(
                                pb[avb + h][0:65, :],
                                col(RA_V, 130 * kp + 65 * h, 130 * kp + 65 * h + 65),
                                col(RB_A, 512 * (3 * h + slp), 512 * (3 * h + slp) + 512),
                                start=(kp == 0), stop=last),
                                waits=w_a, inc=s_pe)
                            av_free3[h][slp] = av_ab[h]
                for h in range(2):
                    v_rd = P.op("vector", lambda E, h=h, avb=avb: E.reciprocal(
                        col(RB_RD, 512 * h, 512 * h + 512, 64, 65), pb[avb + h][64:65, :]),
                        waits=[(s_pe, av_ab[h]), (s_pe, rd_free[h]), (s_vec, s_vec.n)],
                        inc=s_vec)
                    p_b = P.op("tensor", lambda E, h=h, pbb=pbb: E.matmul(
                        pb[pbb + h][0:64, :], PB[64:65, ONES: ONES + 64],
                        col(RB_RD, 512 * h, 512 * h + 512, 64, 65),
                        start=True, stop=True),
                        waits=[(s_vec, v_rd), (s_vec, bank_r[pbb + h])], inc=s_pe)
                    rd_free[h] = p_b
                    v_bc = P.op("vector", lambda E, h=h, pbb=pbb: E.tensor_copy(
                        col(RB_BC, 512 * h, 512 * h + 512, 0, 64), pb[pbb + h][0:64, :]),
                        waits=[(s_pe, p_b), (s_vec, s_vec.n)], inc=s_vec)
                    bc_cons[h] = v_bc
                    bank_r[pbb + h] = v_bc
                    v_zt = P.op("vector", lambda E, h=h, avb=avb: E.tensor_tensor(
                        col(RB_ZTMP, 0, 512, 0, 64), pb[avb + h][0:64, :],
                        col(RB_BC, 512 * h, 512 * h + 512, 0, 64), ALU.mult),
                        waits=[(s_vec, s_vec.n)], inc=s_vec)
                    bank_r[avb + h] = v_zt
                    if p == 0 and h == 0:
                        z_cons[h] = P.op("vector", lambda E, qb=qb: E.tensor_copy(
                            col(RB_ZS, 512 * qb, 512 * qb + 512, 0, 64),
                            col(RB_ZTMP, 0, 512, 0, 64)),
                            waits=[(s_vec, v_zt)], inc=s_vec)
                    else:
                        z_cons[h] = P.op("vector", lambda E, qb=qb: E.tensor_add(
                            col(RB_ZS, 512 * qb, 512 * qb + 512, 0, 64),
                            col(RB_ZS, 512 * qb, 512 * qb + 512, 0, 64),
                            col(RB_ZTMP, 0, 512, 0, 64)),
                            waits=[(s_vec, v_zt), (s_vec, s_vec.n)], inc=s_vec)
                if p == 3:
                    for jj in range(4):
                        j = 4 * qb + jj
                        jp = jj % 2
                        ptz = P.op("tensor", lambda E, j=j, bk6=6 + jp: E.transpose(
                            pb[bk6][:, 0:64], col(RB_ZS, 128 * j, 128 * j + 128, 0, 64),
                            col(IDENT, 0, 64, 0, 64)),
                            waits=[(s_vec, z_cons[1]), (s_vec, bank_r[6 + jp]),
                                   (s_szt[jp], zt_free[jp])], inc=s_pe)
                        vcz = P.op("vector", lambda E, jp=jp, bk6=6 + jp: E.tensor_copy(
                            col(RC_ZT, 64 * jp, 64 * jp + 64), pb[bk6][:, 0:64]),
                            waits=[(s_pe, ptz), (s_szt[jp], zt_free[jp]),
                                   (s_vec, s_vec.n)], inc=s_vec)
                        bank_r[6 + jp] = vcz
                        zt_free[jp] = P.op("gpsimd", lambda E, j=j, jp=jp: E.dma_start(
                            out=ztm_d[128 * j: 128 * j + 128, :],
                            in_=col(RC_ZT, 64 * jp, 64 * jp + 64)),
                            waits=[(s_vec, vcz)], inc=s_szt[jp], k=16)
        zsum_done = s_vec.n

        pbf = {i: ("vec", 0) for i in range(8)}

        def bw(i):
            eng, v = pbf[i]
            return (s_vec if eng == "vec" else (s_act if eng == "act" else s_pe), v)

        rt = lambda a, b, r0=0, r1=128: col(RT_T, a, b, r0, r1)
        pbf[6] = ("vec", bank_r[6])
        pbf[7] = ("vec", bank_r[7])

        def emit_routing_chunk(chn):
            pe_sc = 0
            for dch in range(8):
                first, last = dch == 0, dch == 7
                w0_ = ([(s_vec, s_vec.n), bw(7), (s_lc, ld_const)] if first else [])
                pe_sc = P.op("tensor", lambda E, chn=chn, dch=dch, first=first, last=last: E.matmul(
                    pb[7][0:16, 0:128], col(GWT, 16 * dch, 16 * dch + 16),
                    col(RA_XN2, 2048 * dch + 128 * chn, 2048 * dch + 128 * chn + 128),
                    start=first, stop=last),
                    waits=w0_, inc=s_pe if last else None)
            a_sig = P.op("scalar", lambda E: E.activation(
                rt(RT_SIG, RT_SIG + 128, 0, 16), pb[7][0:16, 0:128], AF.Sigmoid),
                waits=[(s_pe, pe_sc), (s_act, s_act.n)], inc=s_act)
            pt = P.op("tensor", lambda E: E.transpose(
                pb[7][:, 0:16], rt(RT_SIG, RT_SIG + 128, 0, 16), col(IDENT, 0, 16, 0, 16)),
                waits=[(s_act, a_sig)], inc=s_pe)
            v_sct = P.op("vector", lambda E, chn=chn: E.tensor_copy(
                col(RTS, 16 * chn, 16 * chn + 16), pb[7][:, 0:16]),
                waits=[(s_pe, pt), (s_vec, s_vec.n)], inc=s_vec)
            P.op("vector", lambda E, chn=chn: E.max(out=rt(RT_M8, RT_M8 + 8),
                                                    in_=col(RTS, 16 * chn, 16 * chn + 16)),
                 waits=[(s_vec, v_sct)], inc=s_vec)
            P.op("vector", lambda E: E.scalar_tensor_tensor(
                rt(RT_DEN, RT_DEN + 1), rt(RT_M8, RT_M8 + 1), 1e-20,
                rt(RT_M8 + 1, RT_M8 + 2), ALU.add, ALU.add),
                waits=[(s_vec, s_vec.n)], inc=s_vec)
            P.op("vector", lambda E: E.reciprocal(rt(RT_RDN, RT_RDN + 1), rt(RT_DEN, RT_DEN + 1)),
                 waits=[(s_vec, s_vec.n)], inc=s_vec)
            P.op("vector", lambda E, chn=chn: E.tensor_scalar(
                rt(RT_V + 4 * chn, RT_V + 4 * chn + 2), rt(RT_M8, RT_M8 + 2),
                rt(RT_RDN, RT_RDN + 1), 2.5, ALU.mult, ALU.mult),
                waits=[(s_vec, s_vec.n)], inc=s_vec)
            P.op("vector", lambda E, chn=chn: E.tensor_scalar(
                rt(RT_IND, RT_IND + 16), col(RTS, 16 * chn, 16 * chn + 16),
                rt(RT_M8 + 1, RT_M8 + 2), None, ALU.is_ge),
                waits=[(s_vec, s_vec.n)], inc=s_vec)
            pt2 = P.op("tensor", lambda E: E.transpose(
                pb[7][0:16, 0:128], rt(RT_IND, RT_IND + 16), col(IDENT, 0, 128)),
                waits=[(s_vec, s_vec.n)], inc=s_pe)
            v_ifm = P.op("vector", lambda E, chn=chn: E.tensor_copy(
                col(INDP, 128 * chn, 128 * chn + 128, 0, 16), pb[7][0:16, 0:128]),
                waits=[(s_pe, pt2), (s_vec, s_vec.n)], inc=s_vec)
            pbf[7] = ("vec", v_ifm)

        ztm_stores = [(s_szt[0], s_szt[0].n), (s_szt[1], s_szt[1].n)]
        v_z = P.op("vector", lambda E: E.memset(col(RC_XNC, 0, NSLOT // 128), 0.0),
                   waits=[(s_pe, proj_done[3]), (s_vec, s_vec.n)], inc=s_vec)
        P.op("gpsimd", lambda E: E.dma_start(out=tok_d[:], in_=col(RC_XNC, 0, NSLOT // 128)),
             waits=[(s_vec, v_z)], inc=s_stk, k=16)
        stk_zero = s_stk.n
        P.op("gpsimd", lambda E: E.collective_compute(
            "ReduceScatter", ALU.add,
            replica_groups=[[0, 1], [2, 3], [4, 5], [6, 7]],
            ins=[ztm_d[:]], outs=[zhalf_d[:]]),
            waits=ztm_stores, inc=s_cc)

        epi_zh_free = [0, 0]
        epi_xr_free = [[], []]
        epi_r1_free = [[], []]
        epi_zf_free = [0, 0]
        bank_free_v = {2: 0, 3: 0, 4: 0, 5: 0}
        tr01_free = [0, 0]
        for j in range(16):
            par = j % 2
            ld_zh = P.op("sync", lambda E, j=j, par=par: E.dma_start(
                out=col(RC_ZH, 64 * par, 64 * par + 64), in_=zhalf_d[128 * j: 128 * j + 128, :]),
                waits=[(s_cc, 1), (s_pe, epi_zh_free[par])], inc=s_lzh[par], k=16)
            ld_xr = P.op("sync", lambda E, j=j, par=par: E.dma_start(
                out=col(RC_XR, 1024 * par, 1024 * par + 1024),
                in_=xres_d[128 * j: 128 * j + 128, :]),
                waits=epi_xr_free[par], inc=s_lxr[par], k=16)
            pt = P.op("tensor", lambda E, par=par: E.transpose(
                pb[par][0:64, 0:128], col(RC_ZH, 64 * par, 64 * par + 64), col(IDENT, 0, 128)),
                waits=[(s_lzh[par], ld_zh), (s_vec, tr01_free[par]),
                       (s_act, max(exp_free[0][par], exp_free[1][par]))],
                inc=s_pe)
            epi_zh_free[par] = pt
            v_zf = P.op("vector", lambda E, par=par: E.tensor_copy(
                col(RC_ZF, 128 * par, 128 * par + 128, 0, 64), pb[par][0:64, 0:128]),
                waits=[(s_pe, pt), (s_pe, epi_zf_free[par]), (s_vec, s_vec.n)], inc=s_vec)
            tr01_free[par] = v_zf
            pe_o = []
            for half in range(2):
                bk = 2 + half
                pe_o.append(P.op("tensor", lambda E, par=par, half=half, bk=bk: E.matmul(
                    pb[bk][:], col(RC_ZF, 128 * par, 128 * par + 128, 0, 65),
                    col(WO_C, 512 * half, 512 * half + 512, 0, 65),
                    start=True, stop=True),
                    waits=[(s_vec, v_zf), (s_vec, bank_free_v[bk]), (s_vec, vec_const)],
                    inc=s_pe))
            epi_zf_free[par] = pe_o[-1]
            v_r1 = []
            for half in range(2):
                v_r1.append(P.op("vector", lambda E, par=par, half=half, bk=2 + half: E.tensor_add(
                    col(RC_R1, 1024 * par + 512 * half, 1024 * par + 512 * half + 512),
                    pb[bk][:],
                    col(RC_XR, 1024 * par + 512 * half, 1024 * par + 512 * half + 512)),
                    waits=[(s_pe, pe_o[half]), (s_lxr[par], ld_xr),
                           (s_vec, s_vec.n)] + epi_r1_free[par],
                    inc=s_vec))
            bank_free_v[2] = v_r1[0]
            bank_free_v[3] = v_r1[1]
            v_acc = P.op("vector", lambda E, j=j, par=par: E.tensor_copy(
                col(RB_ACC, 1024 * j, 1024 * j + 1024),
                col(RC_R1, 1024 * par, 1024 * par + 1024)),
                waits=[(s_vec, v_r1[1])], inc=s_vec)
            v_xn2, a_sq2 = emit_ln(
                lambda par=par: col(RC_R1, 1024 * par, 1024 * par + 1024),
                lambda par=par: col(RC_XR, 1024 * par, 1024 * par + 1024),
                lambda par=par: col(RC_R1, 1024 * par, 1024 * par + 1024),
                16 * par, 1024,
                [(s_vec, v_r1[1])],
                dump_free=[(s_vec, v_r1[1])],
                xn_free=[(s_vec, v_acc)])
            epi_xr_free[par] = [(s_act, a_sq2), (s_vec, v_r1[1])]
            st_x2 = P.op("gpsimd", lambda E, j=j, par=par: E.dma_start(
                out=xn2tm_d[128 * j: 128 * j + 128, :],
                in_=col(RC_R1, 1024 * par, 1024 * par + 1024)),
                waits=[(s_vec, v_xn2)], inc=s_sxn2[par], k=16)
            pt3 = 0
            for dch in range(8):
                bk = 4 + dch % 2
                pt3 = P.op("tensor", lambda E, par=par, dch=dch, bk=bk: E.transpose(
                    pb[bk][:, 0:128],
                    col(RC_R1, 1024 * par + 128 * dch, 1024 * par + 128 * dch + 128),
                    col(IDENT, 0, 128)),
                    waits=[(s_vec, v_xn2), (s_vec, bank_free_v[bk])], inc=s_pe)
                vc3 = P.op("vector", lambda E, j=j, dch=dch, bk=bk: E.tensor_copy(
                    col(RA_XN2, 2048 * dch + 128 * j, 2048 * dch + 128 * j + 128),
                    pb[bk][:, 0:128]),
                    waits=[(s_pe, pt3), (s_vec, s_vec.n)], inc=s_vec)
                bank_free_v[bk] = vc3
            epi_r1_free[par] = [(s_pe, pt3), (s_sxn2[par], st_x2)]
            emit_routing_chunk(j)
        xn2_done = s_vec.n
        epi_pe = s_pe.n
        xn2_st = [(s_sxn2[0], s_sxn2[0].n), (s_sxn2[1], s_sxn2[1].n)]

        epi_clear = [(s_vec, xn2_done), (s_pe, epi_pe)] + xn2_st
        wg_ld = wu_ld = wd_ld = 0

        def load_w(gt, ut, dt_, e, w_g, w_u, w_d):
            nonlocal wg_ld, wu_ld, wd_ld
            for dch in range(8):
                wg_ld = P.op("sync", lambda E, gt=gt, e=e, dch=dch: E.dma_start(
                    out=col(RC_WG, 512 * dch, 512 * dch + 512),
                    in_=(gt[128 * dch: 128 * dch + 128, :] if e is None
                         else gt[e, 128 * dch: 128 * dch + 128, :])),
                    waits=w_g if dch == 0 else [],
                    inc=s_lwg, k=16)
            for dch in range(8):
                wu_ld = P.op("sync", lambda E, ut=ut, e=e, dch=dch: E.dma_start(
                    out=col(RC_WU, 512 * dch, 512 * dch + 512),
                    in_=(ut[128 * dch: 128 * dch + 128, :] if e is None
                         else ut[e, 128 * dch: 128 * dch + 128, :])),
                    waits=w_u if dch == 0 else [],
                    inc=s_lwu, k=16)
            for mc in range(4):
                wd_ld = P.op("sync", lambda E, dt_=dt_, e=e, mc=mc: E.dma_start(
                    out=col(RC_WD, 1024 * mc, 1024 * mc + 1024),
                    in_=(dt_[128 * mc: 128 * mc + 128, :] if e is None
                         else dt_[e, 128 * mc: 128 * mc + 128, :])),
                    waits=w_d if mc == 0 else [],
                    inc=s_lwd, k=16)

        load_w(wsg_d, wsu_d, wsd_d, None,
               [(s_pe, proj_done[3])],
               [(s_pe, proj_done[3]), (s_stk, stk_zero)],
               epi_clear + ztm_stores)

        v_scan = P.op("vector", lambda E: E.tensor_tensor_scan(
            col(INDP, 0, 2048, 0, 16), col(INDP, 0, 2048, 0, 16),
            col(INDP, 0, 2048, 0, 16), 0.0, ALU.add, ALU.bypass),
            waits=[(s_vec, s_vec.n)], inc=s_vec)

        P.op("vector", lambda E: E.tensor_scalar(
            col(PBASE, 2, 10, 0, 16), col(THRESH, 0, 8, 0, 16),
            col(INDP, 2047, 2048, 0, 16), None, ALU.is_lt),
            waits=[(s_vec, v_scan), (s_lc, ld_const)], inc=s_vec)
        P.op("vector", lambda E: E.reduce_sum(col(PBASE, 0, 1, 0, 16),
                                              col(PBASE, 2, 10, 0, 16), axis=AX.X),
             waits=[(s_vec, s_vec.n)], inc=s_vec)
        v_pad = P.op("vector", lambda E: E.tensor_scalar(
            col(PBASE, 1, 2, 0, 16), col(PBASE, 0, 1, 0, 16), float(BLK), None, ALU.mult),
            waits=[(s_vec, s_vec.n)], inc=s_vec)
        p_tp = P.op("tensor", lambda E: E.transpose(
            pb[6][0:1, 0:16], col(PBASE, 1, 2, 0, 16), col(IDENT, 0, 16, 0, 16)),
            waits=[(s_vec, v_pad), bw(6)], inc=s_pe)
        v_pr = P.op("vector", lambda E: E.tensor_copy(col(PBASE, 18, 34, 0, 1), pb[6][0:1, 0:16]),
                    waits=[(s_pe, p_tp), (s_vec, s_vec.n)], inc=s_vec)
        pbf[6] = ("vec", v_pr)
        v_in = P.op("vector", lambda E: E.tensor_tensor_scan(
            col(PBASE, 2, 18, 0, 1), col(PBASE, 18, 34, 0, 1), col(PBASE, 18, 34, 0, 1),
            0.0, ALU.add, ALU.bypass),
            waits=[(s_vec, v_pr)], inc=s_vec)
        v_ex = P.op("vector", lambda E: E.tensor_sub(
            col(PBASE, 2, 18, 0, 1), col(PBASE, 2, 18, 0, 1), col(PBASE, 18, 34, 0, 1)),
            waits=[(s_vec, v_in)], inc=s_vec)
        p_bc2 = P.op("tensor", lambda E: E.matmul(
            pb[6][:, 0:16], PB[0:1, ONES: ONES + 128], col(PBASE, 2, 18, 0, 1),
            start=True, stop=True),
            waits=[(s_vec, v_ex), bw(6)], inc=s_pe)
        v_btm = P.op("vector", lambda E: E.tensor_copy(col(BASE_TM, 0, 16), pb[6][:, 0:16]),
                     waits=[(s_pe, p_bc2), (s_vec, s_vec.n)], inc=s_vec)
        pbf[6] = ("vec", v_btm)
        p_tb = P.op("tensor", lambda E: E.transpose(
            pb[6][0:16, 0:1], col(PBASE, 2, 18, 0, 1), col(IDENT, 0, 1, 0, 1)),
            waits=[(s_vec, v_btm)], inc=s_pe)
        v_bc3 = P.op("vector", lambda E: E.tensor_copy(col(PBASE, 34, 35, 0, 16), pb[6][0:16, 0:1]),
                     waits=[(s_pe, p_tb), (s_vec, s_vec.n)], inc=s_vec)
        pbf[6] = ("vec", v_bc3)
        v_m = P.op("vector", lambda E: E.tensor_scalar(
            col(PBASE, 2, 34, 0, 16), col(TILEST, 0, 32, 0, 16),
            col(PBASE, 34, 35, 0, 16), None, ALU.is_ge),
            waits=[(s_vec, v_bc3), (s_lc, ld_const)], inc=s_vec)
        p_te = P.op("tensor", lambda E: E.matmul(
            pb[6][0:1, 0:32], PB[0:16, ONES: ONES + 1], col(PBASE, 2, 34, 0, 16),
            start=True, stop=True),
            waits=[(s_vec, v_m)], inc=s_pe)
        v_te = P.op("vector", lambda E: E.tensor_copy(col(PBASE, 2, 34, 0, 1), pb[6][0:1, 0:32]),
                    waits=[(s_pe, p_te), (s_vec, s_vec.n)], inc=s_vec)
        pbf[6] = ("vec", v_te)
        v_te2 = P.op("vector", lambda E: E.tensor_scalar(
            col(PBASE, 2, 34, 0, 1), col(PBASE, 2, 34, 0, 1), -1.0, None, ALU.add),
            waits=[(s_vec, v_te)], inc=s_vec)
        tilexp_v = P.op("vector", lambda E: E.tensor_copy(
            U32[0:1, 48:80], col(PBASE, 2, 34, 0, 1)),
            waits=[(s_vec, v_te2)], inc=s_vec)
        p_eb = P.op("tensor", lambda E: E.matmul(
            pb[6][:, 0:32], PB[0:1, ONES: ONES + 128], col(PBASE, 2, 34, 0, 1),
            start=True, stop=True),
            waits=[(s_vec, v_te2)], inc=s_pe)
        v_ebs2 = P.op("vector", lambda E: E.tensor_scalar(
            col(GWT, 0, 32), pb[6][:, 0:32], 512.0, None, ALU.mult),
            waits=[(s_pe, p_eb), (s_vec, s_vec.n)], inc=s_vec)
        pbf[6] = ("vec", v_ebs2)

        pt_all = 0
        for chn in range(16):
            pt_all = P.op("tensor", lambda E, chn=chn: E.transpose(
                pb[7][:, 16 * chn: 16 * chn + 16],
                col(INDP, 128 * chn, 128 * chn + 128, 0, 16),
                col(IDENT, 0, 16, 0, 16)),
                waits=([(s_vec, v_scan), bw(7)] if chn == 0 else []),
                inc=s_pe if chn == 15 else None)
        v_pos = P.op("vector", lambda E: E.tensor_copy(col(RTR, 0, 256), pb[7][:, 0:256]),
                     waits=[(s_pe, pt_all), (s_vec, s_vec.n)], inc=s_vec)
        pbf[7] = ("vec", v_pos)

        def emit_pass2_chunk(chn):
            P.op("vector", lambda E, chn=chn: E.tensor_add(
                rt(RT_SCT, RT_SCT + 16), col(RTR, 16 * chn, 16 * chn + 16),
                col(BASE_TM, 0, 16)),
                waits=[(s_vec, s_vec.n)], inc=s_vec)
            P.op("vector", lambda E, chn=chn: E.max(out=rt(RT_M8, RT_M8 + 8),
                                                    in_=col(RTS, 16 * chn, 16 * chn + 16)),
                 waits=[(s_vec, s_vec.n)], inc=s_vec)
            P.op("vector", lambda E, chn=chn: E.tensor_scalar(
                rt(RT_OH0, RT_OH0 + 16), col(RTS, 16 * chn, 16 * chn + 16),
                rt(RT_M8, RT_M8 + 1), None, ALU.is_ge),
                waits=[(s_vec, s_vec.n)], inc=s_vec)
            P.op("vector", lambda E, chn=chn: E.tensor_scalar(
                rt(RT_IND, RT_IND + 16), col(RTS, 16 * chn, 16 * chn + 16),
                rt(RT_M8 + 1, RT_M8 + 2), None, ALU.is_ge),
                waits=[(s_vec, s_vec.n)], inc=s_vec)
            P.op("vector", lambda E: E.tensor_sub(rt(RT_OH1, RT_OH1 + 16),
                                                  rt(RT_IND, RT_IND + 16),
                                                  rt(RT_OH0, RT_OH0 + 16)),
                 waits=[(s_vec, s_vec.n)], inc=s_vec)
            u32v = 0
            for r, oh in ((0, RT_OH0), (1, RT_OH1)):
                P.op("vector", lambda E, oh=oh: E.tensor_mul(
                    rt(RT_TMP, RT_TMP + 16), rt(oh, oh + 16), rt(RT_SCT, RT_SCT + 16)),
                    waits=[(s_vec, s_vec.n)], inc=s_vec)
                P.op("vector", lambda E, r=r: E.reduce_sum(
                    rt(RT_P + r, RT_P + r + 1), rt(RT_TMP, RT_TMP + 16), axis=AX.X),
                    waits=[(s_vec, s_vec.n)], inc=s_vec)
                u32v = P.op("vector", lambda E, chn=chn, r=r: E.tensor_scalar(
                    U32[:, 2 * chn + r: 2 * chn + r + 1], rt(RT_P + r, RT_P + r + 1),
                    -1.0, None, ALU.add),
                    waits=[(s_vec, s_vec.n)], inc=s_vec)
            v_val = P.op("vector", lambda E, chn=chn: E.tensor_copy(
                rt(RT_VAL, RT_VAL + 1), col(IOTA, chn, chn + 1)),
                waits=[(s_vec, s_vec.n), (s_lc, ld_const), (s_stk, s_stk.n)], inc=s_vec)
            for r in range(2):
                P.op("gpsimd", lambda E, chn=chn, r=r: E.indirect_dma_start(
                    out=tok_d[:],
                    out_offset=bass.IndirectOffsetOnAxis(U32[:, 2 * chn + r: 2 * chn + r + 1], 0),
                    in_=rt(RT_VAL, RT_VAL + 1), in_offset=None),
                    waits=[(s_vec, u32v), (s_vec, v_val), (s_stk, stk_zero)],
                    inc=s_stk, k=16)
        accv = [[xn2_done, xn2_done] for _ in range(16)]
        down_prev = 0
        glast = ulast = dlast = 0
        for t4 in range(4):
            for mc in range(4):
                emit_pass2_chunk(4 * t4 + mc)
                gbk, ubk = mc % 2, 2 + mc % 2
                pe_g = pe_u = 0
                for dch in range(8):
                    first, last = dch == 0, dch == 7
                    wgu = ([(s_lwg, wg_ld), (s_vec, xn2_done), bw(gbk)] if first else [])
                    pe_g = P.op("tensor", lambda E, mc=mc, dch=dch, t4=t4, gbk=gbk, first=first, last=last: E.matmul(
                        pb[gbk][:],
                        col(RC_WG, 512 * dch + 128 * mc, 512 * dch + 128 * mc + 128),
                        col(RA_XN2, 2048 * dch + 512 * t4, 2048 * dch + 512 * t4 + 512),
                        start=first, stop=last),
                        waits=wgu, inc=s_pe if last else None)
                for dch in range(8):
                    first, last = dch == 0, dch == 7
                    wuu = ([(s_lwu, wu_ld), bw(ubk)] if first else [])
                    pe_u = P.op("tensor", lambda E, mc=mc, dch=dch, t4=t4, ubk=ubk, first=first, last=last: E.matmul(
                        pb[ubk][:],
                        col(RC_WU, 512 * dch + 128 * mc, 512 * dch + 128 * mc + 128),
                        col(RA_XN2, 2048 * dch + 512 * t4, 2048 * dch + 512 * t4 + 512),
                        start=first, stop=last),
                        waits=wuu, inc=s_pe if last else None)
                glast, ulast = pe_g, pe_u
                a_sg = P.op("scalar", lambda E, gbk=gbk: E.activation(
                    col(SGQ, 0, 512), pb[gbk][:], AF.Silu),
                    waits=[(s_pe, pe_g), (s_act, s_act.n), (s_vec, s_vec.n)], inc=s_act)
                pbf[gbk] = ("act", a_sg)
                v_hm = P.op("vector", lambda E, mc=mc, ubk=ubk: E.tensor_tensor(
                    col(HMID, 512 * mc, 512 * mc + 512), col(SGQ, 0, 512), pb[ubk][:],
                    ALU.mult),
                    waits=[(s_act, a_sg), (s_pe, pe_u), (s_pe, down_prev), (s_vec, s_vec.n)],
                    inc=s_vec)
                pbf[ubk] = ("vec", v_hm)
            hm_all = s_vec.n
            for tt in range(4):
                jg = 4 * t4 + tt
                for half in range(2):
                    bk = 4 + (tt * 2 + half) % 4
                    pe_d = 0
                    for mc in range(4):
                        first, last = mc == 0, mc == 3
                        wdd = ([(s_lwd, wd_ld), (s_vec, hm_all), bw(bk)] if first else [])
                        pe_d = P.op("tensor", lambda E, tt=tt, half=half, mc=mc, bk=bk, first=first, last=last: E.matmul(
                            pb[bk][:],
                            PB[:, HMID + 512 * mc + 128 * tt: HMID + 512 * mc + 128 * tt + 128],
                            col(RC_WD, 1024 * mc + 512 * half, 1024 * mc + 512 * half + 512),
                            start=first, stop=last),
                            waits=wdd, inc=s_pe if last else None)
                    dlast = pe_d
                    v_add = P.op("vector", lambda E, jg=jg, half=half, bk=bk: E.tensor_add(
                        col(RB_ACC, 1024 * jg + 512 * half, 1024 * jg + 512 * half + 512),
                        col(RB_ACC, 1024 * jg + 512 * half, 1024 * jg + 512 * half + 512),
                        pb[bk][:]),
                        waits=[(s_pe, pe_d), (s_vec, accv[jg][half]), (s_vec, s_vec.n)],
                        inc=s_vec)
                    accv[jg][half] = v_add
                    pbf[bk] = ("vec", v_add)
            down_prev = dlast
        routing_v = s_vec.n
        stk_all = s_stk.n

        sh_pe = s_pe.n
        wregs = {}

        def wload(b, free_waits):
            sem = s_wgb[b]
            eng = "sync" if b % 2 == 0 else "scalar"

            def f(E, b=b, sem=sem, eng=eng):
                re = wregs.get(eng)
                if re is None:
                    re = wregs[eng] = E.alloc_register(f"rw_{eng}")
                E.reg_load(re, U32[0:1, 48 + b: 49 + b])
                E.reg_mul(re, re, 2 * D * M)
                E.dma_start(
                    out=bass.AP(PB, RC_WG, [[NCOLS, 128], [512, 16], [1, 512]]),
                    in_=bass.AP(wgu_d, re, [[512, 128], [65536, 16], [1, 512]]),
                ).then_inc(sem.h, 16)
            P.op(eng, f, waits=free_waits + [(s_vec, tilexp_v)], inc=None)
            sem.n += 16
            return sem

        wdu_free = [0, 0]

        def wd_load(b, free_waits):
            par = b % 2
            P.op("vector", lambda E, b=b: E.tensor_scalar(
                rt(RT_P + 4, RT_P + 8), col(IOTA, 0, 4), col(GWT, b, b + 1),
                None, ALU.add),
                waits=[(s_vec, s_vec.n)], inc=s_vec)
            v_wdu = P.op("vector", lambda E, par=par: E.tensor_copy(
                U32[:, 40 + 4 * par: 44 + 4 * par], rt(RT_P + 4, RT_P + 8)),
                waits=[(s_vec, s_vec.n), (s_wdg, wdu_free[par])], inc=s_vec)
            ld = 0
            for mc in range(4):
                ld = P.op("gpsimd", lambda E, par=par, mc=mc: E.indirect_dma_start(
                    out=col(RC_WD, 1024 * mc, 1024 * mc + 1024),
                    out_offset=None, in_=wed_d[:],
                    in_offset=bass.IndirectOffsetOnAxis(
                        U32[:, 40 + 4 * par + mc: 41 + 4 * par + mc], 0)),
                    waits=(free_waits + [(s_vec, v_wdu)]) if mc == 0 else [],
                    inc=s_wdg, k=16)
            wdu_free[par] = ld
            return ld

        tid_free = [0, 0]
        xgtm_free = [sh_pe, sh_pe]
        st_xg = {}

        def stage_tok(b, first=False):
            par = b % 2
            ld_tid = 0
            for t in range(2):
                ld_tid = P.op("sync", lambda E, b=b, t=t, par=par: E.dma_start(
                    out=rt(RT_TID + 4 * par + t, RT_TID + 4 * par + t + 1),
                    in_=bass.AP(tok_d, BLK * b + 128 * t, [[1, 128], [1, 1]])),
                    waits=([(s_stk, stk_all), (s_vec, tid_free[par])] if t == 0 else []),
                    inc=s_ltk[par], k=16)
            v_u32 = P.op("vector", lambda E, par=par: E.tensor_copy(
                U32[:, 32 + 4 * par: 34 + 4 * par],
                rt(RT_TID + 4 * par, RT_TID + 4 * par + 2)),
                waits=[(s_ltk[par], ld_tid), (s_vec, s_vec.n)], inc=s_vec)
            tid_free[par] = v_u32
            xg = 0
            for t in range(2):
                w_g = [(s_vec, v_u32), (s_pe, xgtm_free[par])] if t == 0 else []
                if t == 0 and first:
                    w_g = w_g + xn2_st
                xg = P.op("gpsimd", lambda E, par=par, t=t: E.indirect_dma_start(
                    out=col(RA, XG_TM + 2048 * par + 1024 * t,
                            XG_TM + 2048 * par + 1024 * t + 1024),
                    out_offset=None, in_=xn2tm_d[:],
                    in_offset=bass.IndirectOffsetOnAxis(
                        U32[:, 32 + 4 * par + t: 33 + 4 * par + t], 0)),
                    waits=w_g, inc=s_lxg[par], k=16)
            st_xg[b] = xg

        wgu_cur = wload(0, [(s_pe, ulast)])
        stage_tok(0, first=True)
        wd_cur = wd_load(0, [(s_pe, dlast)])
        gu_prev = sh_pe
        seo_last = [0, 0, 0, 0]
        for b in range(NBLK):
            par = b % 2
            wgl, wdl = wgu_cur, wd_cur
            if b + 1 < NBLK:
                stage_tok(b + 1)
            cnt = 0
            pt_last = 0
            for t in range(2):
                for dch in range(8):
                    bk = 6 + cnt % 2
                    cnt += 1
                    pt_last = P.op("tensor", lambda E, par=par, t=t, dch=dch, bk=bk: E.transpose(
                        pb[bk][:, 0:128],
                        col(RA, XG_TM + 2048 * par + 1024 * t + 128 * dch,
                            XG_TM + 2048 * par + 1024 * t + 128 * dch + 128),
                        col(IDENT, 0, 128)),
                        waits=[(s_lxg[par], st_xg[b]), bw(bk)], inc=s_pe)
                    vcp = P.op("vector", lambda E, t=t, dch=dch, bk=bk: E.tensor_copy(
                        col(RA, XG_FM + 256 * dch + 128 * t,
                            XG_FM + 256 * dch + 128 * t + 128),
                        pb[bk][:, 0:128]),
                        waits=[(s_pe, pt_last), (s_pe, gu_prev), (s_vec, s_vec.n)], inc=s_vec)
                    pbf[bk] = ("vec", vcp)
            xgtm_free[par] = pt_last
            xgfm_v = s_vec.n
            for mc in range(4):
                gbk, ubk = mc % 2, 2 + mc % 2
                pe_g = pe_u = 0
                for dch in range(8):
                    first, last = dch == 0, dch == 7
                    wgu = ([(wgl, 16), (s_vec, xgfm_v), bw(gbk)] if first else [])
                    pe_g = P.op("tensor", lambda E, mc=mc, dch=dch, gbk=gbk, first=first, last=last: E.matmul(
                        pb[gbk][:, 0:256],
                        col(RC_WG, 512 * dch + 128 * mc, 512 * dch + 128 * mc + 128),
                        col(RA, XG_FM + 256 * dch, XG_FM + 256 * dch + 256),
                        start=first, stop=last),
                        waits=wgu, inc=s_pe if last else None)
                for dch in range(8):
                    first, last = dch == 0, dch == 7
                    wuu = ([(wgl, 16), bw(ubk)] if first else [])
                    pe_u = P.op("tensor", lambda E, mc=mc, dch=dch, ubk=ubk, first=first, last=last: E.matmul(
                        pb[ubk][:, 0:256],
                        col(RC_WU, 512 * dch + 128 * mc, 512 * dch + 128 * mc + 128),
                        col(RA, XG_FM + 256 * dch, XG_FM + 256 * dch + 256),
                        start=first, stop=last),
                        waits=wuu, inc=s_pe if last else None)
                glast, ulast = pe_g, pe_u
                a_sg = P.op("scalar", lambda E, gbk=gbk: E.activation(
                    col(SGQ, 0, 256), pb[gbk][:, 0:256], AF.Silu),
                    waits=[(s_pe, pe_g), (s_act, s_act.n), (s_vec, s_vec.n)], inc=s_act)
                pbf[gbk] = ("act", a_sg)
                v_hm = P.op("vector", lambda E, mc=mc, ubk=ubk: E.tensor_tensor(
                    col(HMID, 256 * mc, 256 * mc + 256), col(SGQ, 0, 256),
                    pb[ubk][:, 0:256], ALU.mult),
                    waits=[(s_act, a_sg), (s_pe, pe_u), (s_pe, down_prev), (s_vec, s_vec.n)],
                    inc=s_vec)
                pbf[ubk] = ("vec", v_hm)
            gu_prev = s_pe.n
            hm_all = s_vec.n
            if b + 1 < NBLK:
                wgu_cur = wload(b + 1, [(s_pe, ulast)])
            for sc in range(2):
                for half in range(2):
                    slot = sc * 2 + half
                    bk = 4 + slot
                    pe_d = 0
                    for mc in range(4):
                        first, last = mc == 0, mc == 3
                        wdd = ([(s_wdg, wdl), (s_vec, hm_all), bw(bk)] if first else [])
                        pe_d = P.op("tensor", lambda E, sc=sc, half=half, mc=mc, bk=bk, first=first, last=last: E.matmul(
                            pb[bk][:],
                            PB[:, HMID + 256 * mc + 128 * sc: HMID + 256 * mc + 128 * sc + 128],
                            col(RC_WD, 1024 * mc + 512 * half, 1024 * mc + 512 * half + 512),
                            start=first, stop=last),
                            waits=wdd, inc=s_pe if last else None)
                    dlast = pe_d
                    vcp2 = P.op("vector", lambda E, slot=slot, bk=bk: E.tensor_copy(
                        col(RA, EOB + 512 * slot, EOB + 512 * slot + 512), pb[bk][:]),
                        waits=[(s_pe, pe_d), (s_seo[slot], seo_last[slot]), (s_vec, s_vec.n)],
                        inc=s_vec)
                    seo_last[slot] = P.op("gpsimd", lambda E, b=b, sc=sc, half=half, slot=slot: E.dma_start(
                        out=eo_d[BLK * b + 128 * sc: BLK * b + 128 * sc + 128,
                                 512 * half: 512 * half + 512],
                        in_=col(RA, EOB + 512 * slot, EOB + 512 * slot + 512)),
                        waits=[(s_vec, vcp2)], inc=s_seo[slot], k=16)
                    pbf[bk] = ("vec", vcp2)
            down_prev = dlast
            if b + 1 < NBLK:
                wd_cur = wd_load(b + 1, [(s_pe, dlast)])

        eo_all = [(s_seo[i], s_seo[i].n) for i in range(4)]
        gb_free = [0, 0]
        for chn in range(16):
            for r in range(2):
                par = (2 * chn + r) % 2
                g_v = P.op("gpsimd", lambda E, chn=chn, r=r, par=par: E.indirect_dma_start(
                    out=col(RA, GB + 1024 * par, GB + 1024 * par + 1024),
                    out_offset=None, in_=eo_d[:],
                    in_offset=bass.IndirectOffsetOnAxis(
                        U32[:, 2 * chn + r: 2 * chn + r + 1], 0)),
                    waits=eo_all + [(s_vec, routing_v), (s_vec, gb_free[par])],
                    inc=s_lgb[par], k=16)
                v_a = P.op("vector", lambda E, chn=chn, r=r, par=par: E.scalar_tensor_tensor(
                    col(RB_ACC, 1024 * chn, 1024 * chn + 1024),
                    col(RA, GB + 1024 * par, GB + 1024 * par + 1024),
                    rt(RT_V + 4 * chn + r, RT_V + 4 * chn + r + 1),
                    col(RB_ACC, 1024 * chn, 1024 * chn + 1024),
                    ALU.mult, ALU.add),
                    waits=[(s_lgb[par], g_v), (s_vec, accv[chn][0]), (s_vec, accv[chn][1]),
                           (s_vec, s_vec.n)],
                    inc=s_vec)
                gb_free[par] = v_a
                accv[chn][0] = v_a
                accv[chn][1] = v_a
        for j in range(16):
            P.op("gpsimd", lambda E, j=j: E.dma_start(
                out=out_d[128 * j: 128 * j + 128, :],
                in_=col(RB_ACC, 1024 * j, 1024 * j + 1024)),
                waits=[(s_vec, accv[j][0])], inc=s_so, k=16)
        P.wait("gpsimd", [(s_so, s_so.n)])

        @block.sync
        def _(E):
            for fn in P.ops["sync"]:
                fn(E)

        @block.tensor
        def _(E):
            for fn in P.ops["tensor"]:
                fn(E)

        @block.vector
        def _(E):
            for fn in P.ops["vector"]:
                fn(E)

        @block.scalar
        def _(E):
            for fn in P.ops["scalar"]:
                fn(E)

        @block.gpsimd
        def _(E):
            for fn in P.ops["gpsimd"]:
                fn(E)

    return nc


_NC_CACHE = None


def _get_program():
    global _NC_CACHE
    if _NC_CACHE is None:
        _NC_CACHE = build_program()
    return _NC_CACHE


def make_in_maps(inputs):
    x = np.asarray(inputs["L0_attn_x"], dtype=np.float32)
    WQ = np.asarray(inputs["L0_attn_WQ"], dtype=np.float32)
    bQ = np.asarray(inputs["L0_attn_bQ"], dtype=np.float32)
    WK = np.asarray(inputs["L0_attn_WK"], dtype=np.float32)
    bK = np.asarray(inputs["L0_attn_bK"], dtype=np.float32)
    WV = np.asarray(inputs["L0_attn_WV"], dtype=np.float32)
    bV = np.asarray(inputs["L0_attn_bV"], dtype=np.float32)
    WO = np.asarray(inputs["L0_attn_WO"], dtype=np.float32)
    bO = np.asarray(inputs["L0_attn_bO"], dtype=np.float32)
    gw = np.asarray(inputs["L0_ffn_gate_w"], dtype=np.float32)
    weg = np.asarray(inputs["L0_ffn_w_gate"], dtype=np.float32)
    weu = np.asarray(inputs["L0_ffn_w_up"], dtype=np.float32)
    wed = np.asarray(inputs["L0_ffn_w_down"], dtype=np.float32)
    wsg = np.asarray(inputs["L0_ffn_ws_gate"], dtype=np.float32)
    wsu = np.asarray(inputs["L0_ffn_ws_up"], dtype=np.float32)
    wsd = np.asarray(inputs["L0_ffn_ws_down"], dtype=np.float32)

    wgu_f = np.ascontiguousarray(np.concatenate(
        [weg.reshape(NE, D * M), weu.reshape(NE, D * M)], axis=1).ravel())
    wed_f = np.ascontiguousarray(wed.reshape(NE * M, D))
    ident = np.eye(128, dtype=np.float32)
    iota = (np.arange(128)[:, None] + 128 * np.arange(16)[None, :]).astype(np.float32)
    iota16 = np.broadcast_to(np.arange(16, dtype=np.float32), (128, 16)).copy()
    tilest = np.broadcast_to(256.0 * np.arange(32, dtype=np.float32), (128, 32)).copy()
    thresh = np.broadcast_to(256.0 * np.arange(8, dtype=np.float32), (128, 8)).copy()
    masks = np.stack([
        np.where(np.arange(128)[:, None] + 128 * r <= np.arange(512)[None, :],
                 0.0, -100000.0)
        for r in range(4)]).astype(ml_dtypes.bfloat16)
    in_maps = []
    for c in range(8):
        b, g = c // 2, c % 2
        hs = slice(512 * g, 512 * (g + 1))
        in_maps.append({
            "x": np.ascontiguousarray(x[b]),
            "xres": np.ascontiguousarray(x[b, TOK * g: TOK * (g + 1)]),
            "wq": np.ascontiguousarray(WQ[:, hs]),
            "wk": np.ascontiguousarray(WK[:, hs]),
            "wv": np.ascontiguousarray(WV[:, hs]),
            "bq": np.ascontiguousarray(bQ[hs].reshape(4, 128).T),
            "bk": np.ascontiguousarray(bK[hs].reshape(4, 128).T),
            "bv": np.ascontiguousarray(bV[hs].reshape(4, 128).T),
            "wo": WO,
            "bo": np.ascontiguousarray(bO.reshape(1, 1024)),
            "gwt": np.ascontiguousarray(gw.T),
            "wsg": wsg, "wsu": wsu, "wsd": wsd,
            "wgu": wgu_f, "wed": wed_f,
            "ident": ident, "iota": iota, "iota16": iota16, "masks": masks,
            "tilest": tilest, "thresh": thresh,
        })
    return in_maps


def kernel(**inputs):
    nc = _get_program()
    in_maps = make_in_maps(inputs)
    res = run_bass_kernel_spmd(nc, in_maps, core_ids=list(range(8)))
    out = np.empty((B, S, D), dtype=np.float32)
    for c in range(8):
        b, g = c // 2, c % 2
        out[b, TOK * g: TOK * (g + 1), :] = np.asarray(res.results[c]["out"])
    return out


# version 14
# speedup vs baseline: 1.0817x; 1.0817x over previous
import numpy as np
import ml_dtypes
from contextlib import ExitStack

import concourse.bass as bass
import concourse.mybir as mybir
from concourse.bass_utils import run_bass_kernel_spmd

F32 = mybir.dt.float32
BF16 = mybir.dt.bfloat16
AF = mybir.ActivationFunctionType
ALU = mybir.AluOpType
AX = mybir.AxisListType

B, S, D, NH, DH = 4, 4096, 1024, 16, 64
TOK = 2048
NE, M = 16, 512
NSLOT, NBLK, BLK = 8192, 32, 256
EPS = 1e-5

IDENT = 0
ONES = 128
WO_C = 256
GWT = 1280
BQ, BK, BV, BQS = 1408, 1412, 1416, 1420
NEGC = 1424
NRM = 1426
STAT = 1434
IOTA = 1466
IOTA16B = 1482
TILEST = 1498
THRESH = 1530
BASE_TM = 1538
PBASE = 1554
PER_END = 1590

RA = PER_END
RB = RA + 16384
RC = RB + 16384
SP = RC + 12288
INDP = SP
RT_T = SP + 2048
RTS = SP + 2360
RTR = SP + 2616
HMID = SP + 2872
RC_XR = HMID
SGQ = HMID + 2048
NCOLS = SGQ + 512 + 24

RA_K = RA
RA_Q = RA + 4096
RA_V = RA + 8192
RA_XN2 = RA

RB_A = RB
RB_VF = RB + 3072
RB_BC = RB + 3584
RB_ZS = RB + 4608
RB_ZTMP = RB + 8704
RB_RD = RB + 9728
RB_ACC = RB

RC_X = RC
RC_XN = RC + 2048
RC_W = RC
RC_XNC = RC + 6144
RC_R1 = RC + 9216
RC_ZH = RC + 11264
RC_ZF = RC + 11520
RC_ZT = RC + 11776
RC_WG = RC
RC_WU = RC + 4096
RC_WD = RC + 8192

RT_SIG = 0
RT_SCT = 128
RT_M8 = 144
RT_DEN = 152
RT_RDN = 153
RT_IND = 160
RT_OH0 = 176
RT_OH1 = 192
RT_TMP = 208
RT_V = 224
RT_VAL = 288
RT_P = 292
RT_TID = 300
XG_TM = 0
XG_FM = 4096
EOB = 6144
GB = 8192


class Sem:
    def __init__(self, h):
        self.h = h
        self.n = 0


class Prog:
    ENGINES = ("sync", "tensor", "vector", "scalar", "gpsimd")

    def __init__(self):
        self.ops = {e: [] for e in self.ENGINES}

    def op(self, eng, fn, waits=(), inc=None, k=1):
        tgts = tuple((s.h, int(v)) for (s, v) in waits if v > 0)
        if inc is not None:
            inc.n += k
        inch = inc.h if inc is not None else None

        def run(E, fn=fn, tgts=tgts, inch=inch, k=k):
            for h, v in tgts:
                E.wait_ge(h, v)
            ins = fn(E)
            if inch is not None:
                ins.then_inc(inch, k)
        self.ops[eng].append(run)
        return inc.n if inc is not None else None

    def wait(self, eng, waits):
        tgts = tuple((s.h, int(v)) for (s, v) in waits if v > 0)
        if not tgts:
            return

        def run(E, tgts=tgts):
            for h, v in tgts:
                E.wait_ge(h, v)
        self.ops[eng].append(run)


def build_program():
    nc = bass.Bass("TRN2", target_bir_lowering=False)

    x_d = nc.dram_tensor("x", [S, D], F32, kind="ExternalInput")
    xres_d = nc.dram_tensor("xres", [TOK, D], F32, kind="ExternalInput")
    wq_d = nc.dram_tensor("wq", [D, 512], F32, kind="ExternalInput")
    wk_d = nc.dram_tensor("wk", [D, 512], F32, kind="ExternalInput")
    wv_d = nc.dram_tensor("wv", [D, 512], F32, kind="ExternalInput")
    bq_d = nc.dram_tensor("bq", [128, 4], F32, kind="ExternalInput")
    bk_d = nc.dram_tensor("bk", [128, 4], F32, kind="ExternalInput")
    bv_d = nc.dram_tensor("bv", [128, 4], F32, kind="ExternalInput")
    wo_d = nc.dram_tensor("wo", [64, 1024], F32, kind="ExternalInput")
    bo_d = nc.dram_tensor("bo", [1, 1024], F32, kind="ExternalInput")
    gwt_d = nc.dram_tensor("gwt", [D, NE], F32, kind="ExternalInput")
    wsg_d = nc.dram_tensor("wsg", [D, M], F32, kind="ExternalInput")
    wsu_d = nc.dram_tensor("wsu", [D, M], F32, kind="ExternalInput")
    wsd_d = nc.dram_tensor("wsd", [M, D], F32, kind="ExternalInput")
    wgu_d = nc.dram_tensor("wgu", [NE * 2 * D * M], F32, kind="ExternalInput")
    wed_d = nc.dram_tensor("wed", [NE * M, D], F32, kind="ExternalInput")
    ident_d = nc.dram_tensor("ident", [128, 128], F32, kind="ExternalInput")
    iota_d = nc.dram_tensor("iota", [128, 16], F32, kind="ExternalInput")
    iota16_d = nc.dram_tensor("iota16", [128, 16], F32, kind="ExternalInput")
    tilest_d = nc.dram_tensor("tilest", [128, 32], F32, kind="ExternalInput")
    thresh_d = nc.dram_tensor("thresh", [128, 8], F32, kind="ExternalInput")
    masks_d = nc.dram_tensor("masks", [4, 128, 512], BF16, kind="ExternalInput")
    out_d = nc.dram_tensor("out", [TOK, D], F32, kind="ExternalOutput")

    xn1_d = nc.dram_tensor("xn1", [D, S], F32)
    ztm_d = nc.dram_tensor("ztm", [S, 64], F32)
    zhalf_d = nc.dram_tensor("zhalf", [TOK, 64], F32)
    xn2tm_d = nc.dram_tensor("xn2tm", [TOK, D], F32)
    tok_d = nc.dram_tensor("tok", [NSLOT, 1], F32)
    eo_d = nc.dram_tensor("eo", [NSLOT, D], F32)

    with ExitStack() as ctx:
        PB = nc.alloc_sbuf_tensor("PB", [128, NCOLS], F32)
        MB = nc.alloc_sbuf_tensor("MB", [128, 2048], BF16)
        U32 = nc.alloc_sbuf_tensor("U32", [128, 80], mybir.dt.uint32)
        pb = [nc.alloc_psum_tensor(f"pb{i}", [128, 512], F32) for i in range(8)]

        def mks(name):
            return Sem(ctx.enter_context(nc.semaphore(name)))

        s_pe, s_vec, s_act, s_cc = mks("s_pe"), mks("s_vec"), mks("s_act"), mks("s_cc")
        s_lc = mks("s_lc")
        s_sm = mks("s_sm")
        s_lx = [mks("s_lx0"), mks("s_lx1")]
        s_s1 = [mks("s_s10"), mks("s_s11")]
        s_lw = [mks("s_lw0"), mks("s_lw1")]
        s_lxn = [mks("s_lxn0"), mks("s_lxn1")]
        s_lzh = [mks("s_lzh0"), mks("s_lzh1")]
        s_lxr = [mks("s_lxr0"), mks("s_lxr1")]
        s_szt = [mks("s_szt0"), mks("s_szt1")]
        s_lwg, s_lwu, s_lwd = mks("s_lwg"), mks("s_lwu"), mks("s_lwd")
        s_wdg = mks("s_wdg")
        s_wgb = [mks(f"s_wgb{b}") for b in range(NBLK)]
        s_so = mks("s_so")
        s_sxn2 = [mks("s_sxn20"), mks("s_sxn21")]
        s_stk = mks("s_stk")
        s_ltk = [mks("s_ltk0"), mks("s_ltk1")]
        s_lxg = [mks("s_lxg0"), mks("s_lxg1")]
        s_seo = [mks(f"s_seo{i}") for i in range(4)]
        s_lgb = [mks("s_lgb0"), mks("s_lgb1")]
        block = ctx.enter_context(nc.Block())

        P = Prog()

        def col(base, a, b, r0=0, r1=128):
            return PB[r0:r1, base + a: base + b]

        P.op("sync", lambda E: E.dma_start(out=col(IDENT, 0, 128), in_=ident_d[:]), inc=s_lc, k=16)
        P.op("sync", lambda E: E.dma_start(out=col(WO_C, 0, 1024, 0, 64), in_=wo_d[:]), inc=s_lc, k=16)
        P.op("sync", lambda E: E.dma_start(out=col(WO_C, 0, 1024, 64, 65), in_=bo_d[:]), inc=s_lc, k=16)
        P.op("sync", lambda E: E.dma_start(
            out=bass.AP(PB, GWT, [[NCOLS, 128], [16, 8], [1, 16]]),
            in_=bass.AP(gwt_d, 0, [[16, 128], [2048, 8], [1, 16]])), inc=s_lc, k=16)
        for base, dd in ((IOTA, iota_d), (IOTA16B, iota16_d)):
            P.op("sync", lambda E, dd=dd, base=base: E.dma_start(out=col(base, 0, 16), in_=dd[:]),
                 inc=s_lc, k=16)
        P.op("sync", lambda E: E.dma_start(out=col(TILEST, 0, 32), in_=tilest_d[:]), inc=s_lc, k=16)
        P.op("sync", lambda E: E.dma_start(out=col(THRESH, 0, 8), in_=thresh_d[:]), inc=s_lc, k=16)
        for base, dd in ((BQ, bq_d), (BK, bk_d), (BV, bv_d)):
            P.op("sync", lambda E, dd=dd, base=base: E.dma_start(out=col(base, 0, 4), in_=dd[:]),
                 inc=s_lc, k=16)
        for r in range(4):
            P.op("gpsimd", lambda E, r=r: E.dma_start(out=MB[:, 512 * r: 512 * r + 512], in_=masks_d[r]),
                 inc=s_sm, k=16)
        ld_const = s_lc.n
        mask_ld = s_sm.n
        P.op("vector", lambda E: E.memset(col(ONES, 0, 128), 1.0), inc=s_vec)
        P.op("vector", lambda E: E.memset(
            bass.AP(PB, RA_V + 64, [[NCOLS, 128], [65, 64]]), 1.0), inc=s_vec)
        vones_set = s_vec.n
        P.op("vector", lambda E: E.memset(col(RC_ZF, 0, 256, 64, 65), 1.0), inc=s_vec)
        P.op("vector", lambda E: E.tensor_scalar(col(BQS, 0, 4), col(BQ, 0, 4), 0.125, None, ALU.mult),
             waits=[(s_lc, ld_const)], inc=s_vec)
        P.op("vector", lambda E: E.tensor_scalar(col(WO_C, 0, 1024, 64, 65), col(WO_C, 0, 1024, 64, 65),
                                                 16.0, None, ALU.mult),
             waits=[(s_lc, ld_const)], inc=s_vec)
        vec_const = s_vec.n

        def emit_ln(x_fn, dump_fn, xn_fn, sb, W, x_ready, dump_free=(), xn_free=()):
            st_ = lambda i: col(STAT, sb + i, sb + i + 1)
            v_rs = P.op("vector", lambda E: E.reduce_sum(st_(0), x_fn(), axis=AX.X),
                        waits=list(x_ready) + [(s_vec, s_vec.n)], inc=s_vec)
            a_sq = P.op("scalar", lambda E: E.activation(dump_fn(), x_fn(), AF.Square,
                                                         accum_out=st_(1)),
                        waits=list(x_ready) + list(dump_free) + [(s_act, s_act.n)], inc=s_act)
            v_m = P.op("vector", lambda E: E.tensor_scalar(st_(2), st_(0), 1.0 / W, None, ALU.mult),
                       waits=[(s_vec, v_rs)], inc=s_vec)
            v_e2 = P.op("vector", lambda E: E.tensor_scalar(st_(3), st_(1), 1.0 / W, EPS,
                                                            ALU.mult, ALU.add),
                        waits=[(s_act, a_sq)], inc=s_vec)
            v_ms = P.op("vector", lambda E: E.tensor_scalar(st_(4), st_(2), st_(2), None, ALU.mult),
                        waits=[(s_vec, v_m)], inc=s_vec)
            v_var = P.op("vector", lambda E: E.tensor_sub(st_(3), st_(3), st_(4)),
                         waits=[(s_vec, v_ms), (s_vec, v_e2)], inc=s_vec)
            a_sd = P.op("scalar", lambda E: E.activation(st_(5), st_(3), AF.Sqrt),
                        waits=[(s_vec, v_var)], inc=s_act)
            v_rd = P.op("vector", lambda E: E.reciprocal(st_(6), st_(5)),
                        waits=[(s_act, a_sd)], inc=s_vec)
            v_xn = P.op("vector", lambda E: E.tensor_scalar(xn_fn(), x_fn(), st_(2), st_(6),
                                                            ALU.subtract, ALU.mult),
                        waits=[(s_vec, v_rd), (s_act, a_sq)] + list(xn_free), inc=s_vec)
            return v_xn, a_sq

        blk_store_done = [0, 0]
        xtile_free = [0, 0]
        xntile_free = [0, 0]
        trb_free = [0, 0]
        for tb in range(8):
            rpar = tb % 2
            blk = RA + 4096 * rpar
            last_copy = 0
            for tt in range(4):
                t128 = tb * 4 + tt
                par = t128 % 2
                xap = lambda par=par: col(RC_X, 1024 * par, 1024 * par + 1024)
                xnap = lambda par=par: col(RC_XN, 1024 * par, 1024 * par + 1024)
                ldv = P.op("sync", lambda E, t128=t128, par=par: E.dma_start(
                    out=col(RC_X, 1024 * par, 1024 * par + 1024),
                    in_=x_d[128 * t128: 128 * t128 + 128, :]),
                    waits=[(s_vec, xtile_free[par])], inc=s_lx[par], k=16)
                v_xn, _ = emit_ln(xap, xnap, xnap, 16 * par, 1024,
                                  [(s_lx[par], ldv)],
                                  dump_free=[(s_pe, xntile_free[par])],
                                  xn_free=[(s_pe, xntile_free[par])])
                xtile_free[par] = v_xn
                for dch in range(8):
                    bk = dch % 2
                    pt = P.op("tensor", lambda E, par=par, dch=dch, bk=bk: E.transpose(
                        pb[bk][:, 0:128],
                        col(RC_XN, 1024 * par + 128 * dch, 1024 * par + 128 * dch + 128),
                        col(IDENT, 0, 128)),
                        waits=[(s_vec, v_xn), (s_vec, trb_free[bk]), (s_lc, ld_const),
                               (s_s1[rpar], blk_store_done[rpar])],
                        inc=s_pe)
                    vc = P.op("vector", lambda E, blk=blk, dch=dch, tt=tt, bk=bk: E.tensor_copy(
                        col(0, blk + 512 * dch + 128 * tt, blk + 512 * dch + 128 * tt + 128),
                        pb[bk][:, 0:128]),
                        waits=[(s_pe, pt)], inc=s_vec)
                    trb_free[bk] = vc
                    last_copy = vc
                xntile_free[par] = s_pe.n
            for dch in range(8):
                P.op("gpsimd", lambda E, blk=blk, dch=dch, tb=tb: E.dma_start(
                    out=xn1_d[128 * dch: 128 * dch + 128, 512 * tb: 512 * tb + 512],
                    in_=col(0, blk + 512 * dch, blk + 512 * dch + 512)),
                    waits=[(s_vec, last_copy)], inc=s_s1[rpar], k=16)
            blk_store_done[rpar] = s_s1[rpar].n
        ph1_st = [(s_s1[0], s_s1[0].n), (s_s1[1], s_s1[1].n)]
        ph1_vec = s_vec.n

        wload_done = [0, 0, 0, 0]
        proj_done = [0, 0, 0, 0]

        def load_wset(p):
            for mi, wd in enumerate((wq_d, wk_d, wv_d)):
                for dch in range(8):
                    P.op("sync", lambda E, wd=wd, mi=mi, p=p, dch=dch: E.dma_start(
                        out=col(RC_W + 3072 * (p % 2) + 1024 * mi,
                                128 * dch, 128 * dch + 128),
                        in_=wd[128 * dch: 128 * dch + 128, 128 * p: 128 * p + 128]),
                        waits=([(s_vec, ph1_vec if p < 2 else 0),
                                (s_pe, proj_done[p - 2] if p >= 2 else 0)]
                               if dch == 0 and mi == 0 else []),
                        inc=s_lw[p % 2], k=16)
            wload_done[p] = s_lw[p % 2].n

        load_wset(0)
        load_wset(1)

        xnc_free = [0, 0]
        vt_bank_free = [0, 0]
        exp_free = [[0, 0], [0, 0]]
        av_free3 = [[0, 0, 0], [0, 0, 0]]
        av_guard = [0, 0]
        z_cons = [0, 0]
        bc_cons = [0, 0]
        rd_free = [0, 0]
        sgq_free = [0, 0]
        vf_free = 0

        negc_done = 0
        zt_free = [0, 0]
        zt_vfree = [0, 0]
        NRM_ACC = ((0, 1), (5, 6))
        NRM_SCR = (2, 7)

        def emit_norm_step(ch, h, nco, base, first):
            r0, r1 = 64 * h, 64 * h + 64
            a_sq = P.op("scalar", lambda E, base=base, ch=ch, r0=r0, r1=r1: E.activation(
                PB[r0:r1, SGQ: SGQ + 512],
                PB[r0:r1, base + 512 * ch: base + 512 * ch + 512], AF.Square),
                waits=[(s_pe, sgq_free[h]), (s_act, s_act.n)], inc=s_act)
            p_n = P.op("tensor", lambda E, r0=r0, r1=r1, h=h: E.matmul(
                pb[6 + h][0:1, 0:512],
                PB[r0:r1, ONES: ONES + 1], PB[r0:r1, SGQ: SGQ + 512],
                start=True, stop=True),
                waits=[(s_act, a_sq), (s_vec, vt_bank_free[h])], inc=s_pe)
            sgq_free[h] = p_n
            sc = NRM_SCR[h]
            v_mx = P.op("vector", lambda E, sc=sc, h=h: E.reduce_max(
                col(NRM, sc, sc + 1, 0, 1), pb[6 + h][0:1, 0:512], axis=AX.X),
                waits=[(s_pe, p_n), (s_vec, s_vec.n)], inc=s_vec)
            ac = NRM_ACC[h][nco]
            if first:
                v_a = P.op("vector", lambda E, ac=ac, sc=sc: E.tensor_copy(
                    col(NRM, ac, ac + 1, 0, 1), col(NRM, sc, sc + 1, 0, 1)),
                    waits=[(s_vec, v_mx)], inc=s_vec)
            else:
                v_a = P.op("vector", lambda E, ac=ac, sc=sc: E.tensor_tensor(
                    col(NRM, ac, ac + 1, 0, 1), col(NRM, ac, ac + 1, 0, 1),
                    col(NRM, sc, sc + 1, 0, 1), ALU.max),
                    waits=[(s_vec, v_mx)], inc=s_vec)
            vt_bank_free[h] = v_a

        for p in range(4):
            pair_guard = (([(s_act, v) for hh in exp_free for v in hh] +
                           [(s_vec, av_guard[0]), (s_vec, av_guard[1]),
                            (s_vec, bc_cons[0]), (s_vec, bc_cons[1])])
                          if p > 0 else [])
            pg67 = [zt_vfree[0], zt_vfree[1]]
            evac_k = [0, 0]
            evac_v = [0, 0]
            evac_q = [0, 0]
            for tc in range(8):
                par = tc % 2
                pe_projs = 0
                wq_off = RC_W + 3072 * (p % 2)
                for dch in range(8):
                    ci = tc * 8 + dch
                    cpar = ci % 2
                    ldv = P.op("sync", lambda E, dch=dch, tc=tc, cpar=cpar: E.dma_start(
                        out=col(RC_XNC, 512 * cpar, 512 * cpar + 512),
                        in_=xn1_d[128 * dch: 128 * dch + 128, 512 * tc: 512 * tc + 512]),
                        waits=ph1_st + [(s_pe, xnc_free[cpar]),
                                        (s_lw[p % 2], wload_done[p])],
                        inc=s_lxn[cpar], k=16)
                    mv = lambda cpar=cpar: col(RC_XNC, 512 * cpar, 512 * cpar + 512)
                    first, last = dch == 0, dch == 7
                    w8 = [(s_lxn[cpar], ldv)]
                    if first:
                        w8 += [(s_act, evac_k[par]), (s_act, evac_v[par]), (s_act, evac_q[par])]
                        if tc < 2:
                            w8 += pair_guard
                    P.op("tensor", lambda E, wq_off=wq_off, dch=dch, par=par, mv=mv, first=first, last=last: E.matmul(
                        pb[0 + par][:], col(wq_off + 1024, 128 * dch, 128 * dch + 128), mv(),
                        start=first, stop=last), waits=w8)
                    P.op("tensor", lambda E, wq_off=wq_off, dch=dch, par=par, mv=mv, first=first, last=last: E.matmul(
                        pb[2 + par][:], col(wq_off + 2048, 128 * dch, 128 * dch + 128), mv(),
                        start=first, stop=last))
                    pe_projs = P.op("tensor", lambda E, wq_off=wq_off, dch=dch, par=par, mv=mv, first=first, last=last: E.matmul(
                        pb[4 + par][:], col(wq_off, 128 * dch, 128 * dch + 128), mv(),
                        start=first, stop=last), inc=s_pe)
                    xnc_free[cpar] = pe_projs
                evac_k[par] = P.op("scalar", lambda E, tc=tc, par=par, p=p: E.activation(
                    col(RA_K, 512 * tc, 512 * tc + 512), pb[0 + par][:],
                    AF.Identity, bias=col(BK, p, p + 1)),
                    waits=[(s_pe, pe_projs), (s_lc, ld_const), (s_act, s_act.n)], inc=s_act)
                evac_v[par] = P.op("scalar", lambda E, par=par, p=p: E.activation(
                    col(RB_VF, 0, 512), pb[2 + par][:],
                    AF.Identity, bias=col(BV, p, p + 1)),
                    waits=[(s_pe, pe_projs), (s_pe, vf_free)], inc=s_act)
                evac_q[par] = P.op("scalar", lambda E, tc=tc, par=par, p=p: E.activation(
                    col(RA_Q, 512 * tc, 512 * tc + 512), pb[4 + par][:],
                    AF.Identity, bias=col(BQS, p, p + 1), scale=0.125),
                    waits=[(s_pe, pe_projs)], inc=s_act)
                for j in range(4):
                    bk = 6 + j % 2
                    k4 = tc * 4 + j
                    pt = P.op("tensor", lambda E, j=j, bk=bk: E.transpose(
                        pb[bk][:, 0:128], col(RB_VF, 128 * j, 128 * j + 128), col(IDENT, 0, 128)),
                        waits=[(s_act, evac_v[par]), (s_vec, vt_bank_free[j % 2]),
                               (s_vec, pg67[j % 2])],
                        inc=s_pe)
                    P.op("vector", lambda E, k4=k4, bk=bk: E.tensor_copy(
                        col(RA_V, 130 * k4, 130 * k4 + 64), pb[bk][:, 0:64]),
                        waits=[(s_pe, pt), (s_vec, vones_set)], inc=s_vec)
                    vt_bank_free[j % 2] = P.op("vector", lambda E, k4=k4, bk=bk: E.tensor_copy(
                        col(RA_V, 130 * k4 + 65, 130 * k4 + 129), pb[bk][:, 64:128]),
                        inc=s_vec)
                vf_free = s_pe.n
                if tc > 0:
                    for nco, base in ((0, RA_Q), (1, RA_K)):
                        for h in range(2):
                            emit_norm_step(tc - 1, h, nco, base, tc == 1)
            proj_done[p] = s_pe.n
            evac_all = [(s_act, s_act.n), (s_vec, s_vec.n)]
            if p + 2 < 4:
                load_wset(p + 2)

            for nco, base in ((0, RA_Q), (1, RA_K)):
                for h in range(2):
                    emit_norm_step(7, h, nco, base, False)
            for h in range(2):
                a1 = P.op("scalar", lambda E, h=h: E.activation(
                    col(NRM, 3, 4, 0, 1), col(NRM, NRM_ACC[h][0], NRM_ACC[h][0] + 1, 0, 1),
                    AF.Sqrt),
                    waits=[(s_vec, s_vec.n), (s_act, s_act.n)], inc=s_act)
                a2 = P.op("scalar", lambda E, h=h: E.activation(
                    col(NRM, 4, 5, 0, 1), col(NRM, NRM_ACC[h][1], NRM_ACC[h][1] + 1, 0, 1),
                    AF.Sqrt),
                    inc=s_act)
                v1 = P.op("vector", lambda E: E.tensor_tensor(
                    col(NRM, 3, 4, 0, 1), col(NRM, 3, 4, 0, 1), col(NRM, 4, 5, 0, 1), ALU.mult),
                    waits=[(s_act, a2), (s_vec, s_vec.n)], inc=s_vec)
                v2 = P.op("vector", lambda E: E.tensor_scalar(
                    col(NRM, 3, 4, 0, 1), col(NRM, 3, 4, 0, 1), -1.0, None, ALU.mult),
                    waits=[(s_vec, v1)], inc=s_vec)
                p_bc = P.op("tensor", lambda E, h=h: E.matmul(
                    pb[6 + h][:, 0:1], PB[0:1, ONES: ONES + 128], col(NRM, 3, 4, 0, 1),
                    start=True, stop=True),
                    waits=[(s_vec, v2), (s_vec, vt_bank_free[h])], inc=s_pe)
                v_nc = P.op("vector", lambda E, h=h: E.tensor_copy(
                    col(NEGC, h, h + 1), pb[6 + h][:, 0:1]),
                    waits=[(s_pe, p_bc), (s_act, s_act.n), (s_vec, s_vec.n)], inc=s_vec)
                vt_bank_free[h] = v_nc
            negc_done = s_vec.n

            defer = None
            rd_ready = [0, 0]

            def emit_defer():
                zqb = defer
                for h in range(2):
                    p_b = P.op("tensor", lambda E, h=h: E.matmul(
                        pb[6 + h][0:64, :], PB[64:65, ONES: ONES + 64],
                        col(RB_RD, 512 * h, 512 * h + 512, 64, 65),
                        start=True, stop=True),
                        waits=[(s_vec, rd_ready[h]), (s_vec, vt_bank_free[h]),
                               (s_vec, zt_vfree[h])], inc=s_pe)
                    rd_free[h] = p_b
                    v_bc = P.op("vector", lambda E, h=h: E.tensor_copy(
                        col(RB_BC, 512 * h, 512 * h + 512, 0, 64), pb[6 + h][0:64, :]),
                        waits=[(s_pe, p_b), (s_vec, s_vec.n)], inc=s_vec)
                    bc_cons[h] = v_bc
                    vt_bank_free[h] = v_bc
                    v_zt = P.op("vector", lambda E, h=h: E.tensor_mul(
                        col(RB_ZTMP, 512 * h, 512 * h + 512, 0, 64),
                        col(RB_ZTMP, 512 * h, 512 * h + 512, 0, 64),
                        col(RB_BC, 512 * h, 512 * h + 512, 0, 64)),
                        waits=[(s_vec, s_vec.n)], inc=s_vec)
                    if p == 0 and h == 0:
                        z_cons[h] = P.op("vector", lambda E, zqb=zqb, h=h: E.tensor_copy(
                            col(RB_ZS, 512 * zqb, 512 * zqb + 512, 0, 64),
                            col(RB_ZTMP, 512 * h, 512 * h + 512, 0, 64)),
                            waits=[(s_vec, v_zt)], inc=s_vec)
                    else:
                        z_cons[h] = P.op("vector", lambda E, zqb=zqb, h=h: E.tensor_add(
                            col(RB_ZS, 512 * zqb, 512 * zqb + 512, 0, 64),
                            col(RB_ZS, 512 * zqb, 512 * zqb + 512, 0, 64),
                            col(RB_ZTMP, 512 * h, 512 * h + 512, 0, 64)),
                            waits=[(s_vec, v_zt), (s_vec, s_vec.n)], inc=s_vec)
                if p == 3:
                    for jj in range(4):
                        j = 4 * zqb + jj
                        jp = jj % 2
                        ptz = P.op("tensor", lambda E, j=j, bk6=6 + jp: E.transpose(
                            pb[bk6][:, 0:64], col(RB_ZS, 128 * j, 128 * j + 128, 0, 64),
                            col(IDENT, 0, 64, 0, 64)),
                            waits=[(s_vec, z_cons[1]), (s_vec, zt_vfree[jp]),
                                   (s_vec, vt_bank_free[jp]),
                                   (s_szt[jp], zt_free[jp])], inc=s_pe)
                        vcz = P.op("vector", lambda E, jp=jp, bk6=6 + jp: E.tensor_copy(
                            col(RC_ZT, 64 * jp, 64 * jp + 64), pb[bk6][:, 0:64]),
                            waits=[(s_pe, ptz), (s_szt[jp], zt_free[jp]),
                                   (s_vec, s_vec.n)], inc=s_vec)
                        zt_vfree[jp] = vcz
                        zt_free[jp] = P.op("gpsimd", lambda E, j=j, jp=jp: E.dma_start(
                            out=ztm_d[128 * j: 128 * j + 128, :],
                            in_=col(RC_ZT, 64 * jp, 64 * jp + 64)),
                            waits=[(s_vec, vcz)], inc=s_szt[jp], k=16)

            for qb in range(8):
                nkc = 4 * qb + 4
                e_slot = [[0, 0, 0], [0, 0, 0]]
                av_ab = [0, 0]
                for kc in range(nkc + 2):
                    if kc == 3 and defer is not None:
                        emit_defer()
                        defer = None
                    if kc < nkc:
                        pr = kc % 2
                        sl = kc % 3
                        sb = (0, 1) if pr == 0 else (4, 5)
                        pe_ss = [0, 0]
                        for h in range(2):
                            r0, r1 = 64 * h, 64 * h + 64
                            w_s = [(s_act, exp_free[h][pr]), (s_vec, negc_done)]
                            pe_ss[h] = P.op("tensor", lambda E, r0=r0, r1=r1, kc=kc, qb=qb, b=sb[h]: E.matmul(
                                pb[b][:], PB[r0:r1, RA_K + 128 * kc: RA_K + 128 * kc + 128],
                                PB[r0:r1, RA_Q + 512 * qb: RA_Q + 512 * qb + 512],
                                start=True, stop=True),
                                waits=w_s, inc=s_pe)
                        diag = kc >= 4 * qb
                        for h in range(2):
                            if diag:
                                r = kc - 4 * qb
                                v_m = P.op("vector", lambda E, b=sb[h], r=r: E.tensor_add(
                                    pb[b][:], pb[b][:], MB[:, 512 * r: 512 * r + 512]),
                                    waits=[(s_pe, pe_ss[h]), (s_sm, mask_ld), (s_vec, s_vec.n)],
                                    inc=s_vec)
                                w_exp = [(s_vec, v_m)]
                            else:
                                w_exp = [(s_pe, pe_ss[h])]
                            e_slot[h][sl] = P.op("scalar", lambda E, h=h, sl=sl, b=sb[h]: E.activation(
                                col(RB_A, 512 * (3 * h + sl), 512 * (3 * h + sl) + 512), pb[b][:],
                                AF.Exp, bias=col(NEGC, h, h + 1)),
                                waits=w_exp + [(s_pe, av_free3[h][sl])], inc=s_act)
                            exp_free[h][pr] = e_slot[h][sl]
                    if kc >= 2:
                        kp = kc - 2
                        slp = kp % 3
                        last = kp == nkc - 1
                        for h in range(2):
                            w_a = [(s_act, e_slot[h][slp])]
                            if kp == 0:
                                w_a += [(s_vec, av_guard[h]), (s_vec, negc_done)]
                            av_ab[h] = P.op("tensor", lambda E, kp=kp, h=h, slp=slp, last=last: E.matmul(
                                pb[2 + h][0:65, :],
                                col(RA_V, 130 * kp + 65 * h, 130 * kp + 65 * h + 65),
                                col(RB_A, 512 * (3 * h + slp), 512 * (3 * h + slp) + 512),
                                start=(kp == 0), stop=last),
                                waits=w_a, inc=s_pe)
                            av_free3[h][slp] = av_ab[h]
                for h in range(2):
                    v_zraw = P.op("vector", lambda E, h=h: E.tensor_copy(
                        col(RB_ZTMP, 512 * h, 512 * h + 512, 0, 64), pb[2 + h][0:64, :]),
                        waits=[(s_pe, av_ab[h]), (s_vec, s_vec.n)], inc=s_vec)
                    v_dcp = P.op("vector", lambda E, h=h: E.tensor_copy(
                        col(RB_RD, 512 * h, 512 * h + 512, 64, 65), pb[2 + h][64:65, :]),
                        waits=[(s_pe, rd_free[h]), (s_vec, s_vec.n)], inc=s_vec)
                    av_guard[h] = v_dcp
                    rd_ready[h] = P.op("vector", lambda E, h=h: E.reciprocal(
                        col(RB_RD, 512 * h, 512 * h + 512, 64, 65),
                        col(RB_RD, 512 * h, 512 * h + 512, 64, 65)),
                        waits=[(s_vec, s_vec.n)], inc=s_vec)
                defer = qb
            emit_defer()
            defer = None
        zsum_done = s_vec.n

        pbf = {i: ("vec", 0) for i in range(8)}

        def bw(i):
            eng, v = pbf[i]
            return (s_vec if eng == "vec" else (s_act if eng == "act" else s_pe), v)

        rt = lambda a, b, r0=0, r1=128: col(RT_T, a, b, r0, r1)
        pbf[6] = ("vec", zt_vfree[0])
        pbf[7] = ("vec", zt_vfree[1])

        def emit_routing_chunk(chn):
            pe_sc = 0
            for dch in range(8):
                first, last = dch == 0, dch == 7
                w0_ = ([(s_vec, s_vec.n), bw(7), (s_lc, ld_const)] if first else [])
                pe_sc = P.op("tensor", lambda E, chn=chn, dch=dch, first=first, last=last: E.matmul(
                    pb[7][0:16, 0:128], col(GWT, 16 * dch, 16 * dch + 16),
                    col(RA_XN2, 2048 * dch + 128 * chn, 2048 * dch + 128 * chn + 128),
                    start=first, stop=last),
                    waits=w0_, inc=s_pe if last else None)
            a_sig = P.op("scalar", lambda E: E.activation(
                rt(RT_SIG, RT_SIG + 128, 0, 16), pb[7][0:16, 0:128], AF.Sigmoid),
                waits=[(s_pe, pe_sc), (s_act, s_act.n)], inc=s_act)
            pt = P.op("tensor", lambda E: E.transpose(
                pb[7][:, 0:16], rt(RT_SIG, RT_SIG + 128, 0, 16), col(IDENT, 0, 16, 0, 16)),
                waits=[(s_act, a_sig)], inc=s_pe)
            v_sct = P.op("vector", lambda E, chn=chn: E.tensor_copy(
                col(RTS, 16 * chn, 16 * chn + 16), pb[7][:, 0:16]),
                waits=[(s_pe, pt), (s_vec, s_vec.n)], inc=s_vec)
            P.op("vector", lambda E, chn=chn: E.max(out=rt(RT_M8, RT_M8 + 8),
                                                    in_=col(RTS, 16 * chn, 16 * chn + 16)),
                 waits=[(s_vec, v_sct)], inc=s_vec)
            P.op("vector", lambda E: E.scalar_tensor_tensor(
                rt(RT_DEN, RT_DEN + 1), rt(RT_M8, RT_M8 + 1), 1e-20,
                rt(RT_M8 + 1, RT_M8 + 2), ALU.add, ALU.add),
                waits=[(s_vec, s_vec.n)], inc=s_vec)
            P.op("vector", lambda E: E.reciprocal(rt(RT_RDN, RT_RDN + 1), rt(RT_DEN, RT_DEN + 1)),
                 waits=[(s_vec, s_vec.n)], inc=s_vec)
            P.op("vector", lambda E, chn=chn: E.tensor_scalar(
                rt(RT_V + 4 * chn, RT_V + 4 * chn + 2), rt(RT_M8, RT_M8 + 2),
                rt(RT_RDN, RT_RDN + 1), 2.5, ALU.mult, ALU.mult),
                waits=[(s_vec, s_vec.n)], inc=s_vec)
            P.op("vector", lambda E, chn=chn: E.tensor_scalar(
                rt(RT_IND, RT_IND + 16), col(RTS, 16 * chn, 16 * chn + 16),
                rt(RT_M8 + 1, RT_M8 + 2), None, ALU.is_ge),
                waits=[(s_vec, s_vec.n)], inc=s_vec)
            pt2 = P.op("tensor", lambda E: E.transpose(
                pb[7][0:16, 0:128], rt(RT_IND, RT_IND + 16), col(IDENT, 0, 128)),
                waits=[(s_vec, s_vec.n)], inc=s_pe)
            v_ifm = P.op("vector", lambda E, chn=chn: E.tensor_copy(
                col(INDP, 128 * chn, 128 * chn + 128, 0, 16), pb[7][0:16, 0:128]),
                waits=[(s_pe, pt2), (s_vec, s_vec.n)], inc=s_vec)
            pbf[7] = ("vec", v_ifm)

        ztm_stores = [(s_szt[0], s_szt[0].n), (s_szt[1], s_szt[1].n)]
        v_z = P.op("vector", lambda E: E.memset(col(RC_XNC, 0, NSLOT // 128), 0.0),
                   waits=[(s_pe, proj_done[3]), (s_vec, s_vec.n)], inc=s_vec)
        P.op("gpsimd", lambda E: E.dma_start(out=tok_d[:], in_=col(RC_XNC, 0, NSLOT // 128)),
             waits=[(s_vec, v_z)], inc=s_stk, k=16)
        stk_zero = s_stk.n
        P.op("gpsimd", lambda E: E.collective_compute(
            "ReduceScatter", ALU.add,
            replica_groups=[[0, 1], [2, 3], [4, 5], [6, 7]],
            ins=[ztm_d[:]], outs=[zhalf_d[:]]),
            waits=ztm_stores, inc=s_cc)

        epi_zh_free = [0, 0]
        epi_xr_free = [[], []]
        epi_r1_free = [[], []]
        epi_zf_free = [0, 0]
        bank_free_v = {2: 0, 3: 0, 4: 0, 5: 0}
        tr01_free = [0, 0]
        for j in range(16):
            par = j % 2
            ld_zh = P.op("sync", lambda E, j=j, par=par: E.dma_start(
                out=col(RC_ZH, 64 * par, 64 * par + 64), in_=zhalf_d[128 * j: 128 * j + 128, :]),
                waits=[(s_cc, 1), (s_pe, epi_zh_free[par])], inc=s_lzh[par], k=16)
            ld_xr = P.op("sync", lambda E, j=j, par=par: E.dma_start(
                out=col(RC_XR, 1024 * par, 1024 * par + 1024),
                in_=xres_d[128 * j: 128 * j + 128, :]),
                waits=epi_xr_free[par], inc=s_lxr[par], k=16)
            pt = P.op("tensor", lambda E, par=par: E.transpose(
                pb[par][0:64, 0:128], col(RC_ZH, 64 * par, 64 * par + 64), col(IDENT, 0, 128)),
                waits=[(s_lzh[par], ld_zh), (s_vec, tr01_free[par]),
                       (s_act, max(exp_free[0][par], exp_free[1][par]))],
                inc=s_pe)
            epi_zh_free[par] = pt
            v_zf = P.op("vector", lambda E, par=par: E.tensor_copy(
                col(RC_ZF, 128 * par, 128 * par + 128, 0, 64), pb[par][0:64, 0:128]),
                waits=[(s_pe, pt), (s_pe, epi_zf_free[par]), (s_vec, s_vec.n)], inc=s_vec)
            tr01_free[par] = v_zf
            pe_o = []
            for half in range(2):
                bk = 2 + half
                pe_o.append(P.op("tensor", lambda E, par=par, half=half, bk=bk: E.matmul(
                    pb[bk][:], col(RC_ZF, 128 * par, 128 * par + 128, 0, 65),
                    col(WO_C, 512 * half, 512 * half + 512, 0, 65),
                    start=True, stop=True),
                    waits=[(s_vec, v_zf), (s_vec, bank_free_v[bk]), (s_vec, vec_const)],
                    inc=s_pe))
            epi_zf_free[par] = pe_o[-1]
            v_r1 = []
            for half in range(2):
                v_r1.append(P.op("vector", lambda E, par=par, half=half, bk=2 + half: E.tensor_add(
                    col(RC_R1, 1024 * par + 512 * half, 1024 * par + 512 * half + 512),
                    pb[bk][:],
                    col(RC_XR, 1024 * par + 512 * half, 1024 * par + 512 * half + 512)),
                    waits=[(s_pe, pe_o[half]), (s_lxr[par], ld_xr),
                           (s_vec, s_vec.n)] + epi_r1_free[par],
                    inc=s_vec))
            bank_free_v[2] = v_r1[0]
            bank_free_v[3] = v_r1[1]
            v_acc = P.op("vector", lambda E, j=j, par=par: E.tensor_copy(
                col(RB_ACC, 1024 * j, 1024 * j + 1024),
                col(RC_R1, 1024 * par, 1024 * par + 1024)),
                waits=[(s_vec, v_r1[1])], inc=s_vec)
            v_xn2, a_sq2 = emit_ln(
                lambda par=par: col(RC_R1, 1024 * par, 1024 * par + 1024),
                lambda par=par: col(RC_XR, 1024 * par, 1024 * par + 1024),
                lambda par=par: col(RC_R1, 1024 * par, 1024 * par + 1024),
                16 * par, 1024,
                [(s_vec, v_r1[1])],
                dump_free=[(s_vec, v_r1[1])],
                xn_free=[(s_vec, v_acc)])
            epi_xr_free[par] = [(s_act, a_sq2), (s_vec, v_r1[1])]
            st_x2 = P.op("gpsimd", lambda E, j=j, par=par: E.dma_start(
                out=xn2tm_d[128 * j: 128 * j + 128, :],
                in_=col(RC_R1, 1024 * par, 1024 * par + 1024)),
                waits=[(s_vec, v_xn2)], inc=s_sxn2[par], k=16)
            pt3 = 0
            for dch in range(8):
                bk = 4 + dch % 2
                pt3 = P.op("tensor", lambda E, par=par, dch=dch, bk=bk: E.transpose(
                    pb[bk][:, 0:128],
                    col(RC_R1, 1024 * par + 128 * dch, 1024 * par + 128 * dch + 128),
                    col(IDENT, 0, 128)),
                    waits=[(s_vec, v_xn2), (s_vec, bank_free_v[bk])], inc=s_pe)
                vc3 = P.op("vector", lambda E, j=j, dch=dch, bk=bk: E.tensor_copy(
                    col(RA_XN2, 2048 * dch + 128 * j, 2048 * dch + 128 * j + 128),
                    pb[bk][:, 0:128]),
                    waits=[(s_pe, pt3), (s_vec, s_vec.n)], inc=s_vec)
                bank_free_v[bk] = vc3
            epi_r1_free[par] = [(s_pe, pt3), (s_sxn2[par], st_x2)]
            emit_routing_chunk(j)
        xn2_done = s_vec.n
        epi_pe = s_pe.n
        xn2_st = [(s_sxn2[0], s_sxn2[0].n), (s_sxn2[1], s_sxn2[1].n)]

        epi_clear = [(s_vec, xn2_done), (s_pe, epi_pe)] + xn2_st
        wg_ld = wu_ld = wd_ld = 0

        def load_w(gt, ut, dt_, e, w_g, w_u, w_d):
            nonlocal wg_ld, wu_ld, wd_ld
            for dch in range(8):
                wg_ld = P.op("sync", lambda E, gt=gt, e=e, dch=dch: E.dma_start(
                    out=col(RC_WG, 512 * dch, 512 * dch + 512),
                    in_=(gt[128 * dch: 128 * dch + 128, :] if e is None
                         else gt[e, 128 * dch: 128 * dch + 128, :])),
                    waits=w_g if dch == 0 else [],
                    inc=s_lwg, k=16)
            for dch in range(8):
                wu_ld = P.op("sync", lambda E, ut=ut, e=e, dch=dch: E.dma_start(
                    out=col(RC_WU, 512 * dch, 512 * dch + 512),
                    in_=(ut[128 * dch: 128 * dch + 128, :] if e is None
                         else ut[e, 128 * dch: 128 * dch + 128, :])),
                    waits=w_u if dch == 0 else [],
                    inc=s_lwu, k=16)
            for mc in range(4):
                wd_ld = P.op("sync", lambda E, dt_=dt_, e=e, mc=mc: E.dma_start(
                    out=col(RC_WD, 1024 * mc, 1024 * mc + 1024),
                    in_=(dt_[128 * mc: 128 * mc + 128, :] if e is None
                         else dt_[e, 128 * mc: 128 * mc + 128, :])),
                    waits=w_d if mc == 0 else [],
                    inc=s_lwd, k=16)

        load_w(wsg_d, wsu_d, wsd_d, None,
               [(s_pe, proj_done[3])],
               [(s_pe, proj_done[3]), (s_stk, stk_zero)],
               epi_clear + ztm_stores)

        v_scan = P.op("vector", lambda E: E.tensor_tensor_scan(
            col(INDP, 0, 2048, 0, 16), col(INDP, 0, 2048, 0, 16),
            col(INDP, 0, 2048, 0, 16), 0.0, ALU.add, ALU.bypass),
            waits=[(s_vec, s_vec.n)], inc=s_vec)

        P.op("vector", lambda E: E.tensor_scalar(
            col(PBASE, 2, 10, 0, 16), col(THRESH, 0, 8, 0, 16),
            col(INDP, 2047, 2048, 0, 16), None, ALU.is_lt),
            waits=[(s_vec, v_scan), (s_lc, ld_const)], inc=s_vec)
        P.op("vector", lambda E: E.reduce_sum(col(PBASE, 0, 1, 0, 16),
                                              col(PBASE, 2, 10, 0, 16), axis=AX.X),
             waits=[(s_vec, s_vec.n)], inc=s_vec)
        v_pad = P.op("vector", lambda E: E.tensor_scalar(
            col(PBASE, 1, 2, 0, 16), col(PBASE, 0, 1, 0, 16), float(BLK), None, ALU.mult),
            waits=[(s_vec, s_vec.n)], inc=s_vec)
        p_tp = P.op("tensor", lambda E: E.transpose(
            pb[6][0:1, 0:16], col(PBASE, 1, 2, 0, 16), col(IDENT, 0, 16, 0, 16)),
            waits=[(s_vec, v_pad), bw(6)], inc=s_pe)
        v_pr = P.op("vector", lambda E: E.tensor_copy(col(PBASE, 18, 34, 0, 1), pb[6][0:1, 0:16]),
                    waits=[(s_pe, p_tp), (s_vec, s_vec.n)], inc=s_vec)
        pbf[6] = ("vec", v_pr)
        v_in = P.op("vector", lambda E: E.tensor_tensor_scan(
            col(PBASE, 2, 18, 0, 1), col(PBASE, 18, 34, 0, 1), col(PBASE, 18, 34, 0, 1),
            0.0, ALU.add, ALU.bypass),
            waits=[(s_vec, v_pr)], inc=s_vec)
        v_ex = P.op("vector", lambda E: E.tensor_sub(
            col(PBASE, 2, 18, 0, 1), col(PBASE, 2, 18, 0, 1), col(PBASE, 18, 34, 0, 1)),
            waits=[(s_vec, v_in)], inc=s_vec)
        p_bc2 = P.op("tensor", lambda E: E.matmul(
            pb[6][:, 0:16], PB[0:1, ONES: ONES + 128], col(PBASE, 2, 18, 0, 1),
            start=True, stop=True),
            waits=[(s_vec, v_ex), bw(6)], inc=s_pe)
        v_btm = P.op("vector", lambda E: E.tensor_copy(col(BASE_TM, 0, 16), pb[6][:, 0:16]),
                     waits=[(s_pe, p_bc2), (s_vec, s_vec.n)], inc=s_vec)
        pbf[6] = ("vec", v_btm)
        p_tb = P.op("tensor", lambda E: E.transpose(
            pb[6][0:16, 0:1], col(PBASE, 2, 18, 0, 1), col(IDENT, 0, 1, 0, 1)),
            waits=[(s_vec, v_btm)], inc=s_pe)
        v_bc3 = P.op("vector", lambda E: E.tensor_copy(col(PBASE, 34, 35, 0, 16), pb[6][0:16, 0:1]),
                     waits=[(s_pe, p_tb), (s_vec, s_vec.n)], inc=s_vec)
        pbf[6] = ("vec", v_bc3)
        v_m = P.op("vector", lambda E: E.tensor_scalar(
            col(PBASE, 2, 34, 0, 16), col(TILEST, 0, 32, 0, 16),
            col(PBASE, 34, 35, 0, 16), None, ALU.is_ge),
            waits=[(s_vec, v_bc3), (s_lc, ld_const)], inc=s_vec)
        p_te = P.op("tensor", lambda E: E.matmul(
            pb[6][0:1, 0:32], PB[0:16, ONES: ONES + 1], col(PBASE, 2, 34, 0, 16),
            start=True, stop=True),
            waits=[(s_vec, v_m)], inc=s_pe)
        v_te = P.op("vector", lambda E: E.tensor_copy(col(PBASE, 2, 34, 0, 1), pb[6][0:1, 0:32]),
                    waits=[(s_pe, p_te), (s_vec, s_vec.n)], inc=s_vec)
        pbf[6] = ("vec", v_te)
        v_te2 = P.op("vector", lambda E: E.tensor_scalar(
            col(PBASE, 2, 34, 0, 1), col(PBASE, 2, 34, 0, 1), -1.0, None, ALU.add),
            waits=[(s_vec, v_te)], inc=s_vec)
        tilexp_v = P.op("vector", lambda E: E.tensor_copy(
            U32[0:1, 48:80], col(PBASE, 2, 34, 0, 1)),
            waits=[(s_vec, v_te2)], inc=s_vec)
        p_eb = P.op("tensor", lambda E: E.matmul(
            pb[6][:, 0:32], PB[0:1, ONES: ONES + 128], col(PBASE, 2, 34, 0, 1),
            start=True, stop=True),
            waits=[(s_vec, v_te2)], inc=s_pe)
        v_ebs2 = P.op("vector", lambda E: E.tensor_scalar(
            col(GWT, 0, 32), pb[6][:, 0:32], 512.0, None, ALU.mult),
            waits=[(s_pe, p_eb), (s_vec, s_vec.n)], inc=s_vec)
        pbf[6] = ("vec", v_ebs2)

        pt_all = 0
        for chn in range(16):
            pt_all = P.op("tensor", lambda E, chn=chn: E.transpose(
                pb[7][:, 16 * chn: 16 * chn + 16],
                col(INDP, 128 * chn, 128 * chn + 128, 0, 16),
                col(IDENT, 0, 16, 0, 16)),
                waits=([(s_vec, v_scan), bw(7)] if chn == 0 else []),
                inc=s_pe if chn == 15 else None)
        v_pos = P.op("vector", lambda E: E.tensor_copy(col(RTR, 0, 256), pb[7][:, 0:256]),
                     waits=[(s_pe, pt_all), (s_vec, s_vec.n)], inc=s_vec)
        pbf[7] = ("vec", v_pos)

        def emit_pass2_chunk(chn):
            P.op("vector", lambda E, chn=chn: E.tensor_add(
                rt(RT_SCT, RT_SCT + 16), col(RTR, 16 * chn, 16 * chn + 16),
                col(BASE_TM, 0, 16)),
                waits=[(s_vec, s_vec.n)], inc=s_vec)
            P.op("vector", lambda E, chn=chn: E.max(out=rt(RT_M8, RT_M8 + 8),
                                                    in_=col(RTS, 16 * chn, 16 * chn + 16)),
                 waits=[(s_vec, s_vec.n)], inc=s_vec)
            P.op("vector", lambda E, chn=chn: E.tensor_scalar(
                rt(RT_OH0, RT_OH0 + 16), col(RTS, 16 * chn, 16 * chn + 16),
                rt(RT_M8, RT_M8 + 1), None, ALU.is_ge),
                waits=[(s_vec, s_vec.n)], inc=s_vec)
            P.op("vector", lambda E, chn=chn: E.tensor_scalar(
                rt(RT_IND, RT_IND + 16), col(RTS, 16 * chn, 16 * chn + 16),
                rt(RT_M8 + 1, RT_M8 + 2), None, ALU.is_ge),
                waits=[(s_vec, s_vec.n)], inc=s_vec)
            P.op("vector", lambda E: E.tensor_sub(rt(RT_OH1, RT_OH1 + 16),
                                                  rt(RT_IND, RT_IND + 16),
                                                  rt(RT_OH0, RT_OH0 + 16)),
                 waits=[(s_vec, s_vec.n)], inc=s_vec)
            u32v = 0
            for r, oh in ((0, RT_OH0), (1, RT_OH1)):
                P.op("vector", lambda E, oh=oh: E.tensor_mul(
                    rt(RT_TMP, RT_TMP + 16), rt(oh, oh + 16), rt(RT_SCT, RT_SCT + 16)),
                    waits=[(s_vec, s_vec.n)], inc=s_vec)
                P.op("vector", lambda E, r=r: E.reduce_sum(
                    rt(RT_P + r, RT_P + r + 1), rt(RT_TMP, RT_TMP + 16), axis=AX.X),
                    waits=[(s_vec, s_vec.n)], inc=s_vec)
                u32v = P.op("vector", lambda E, chn=chn, r=r: E.tensor_scalar(
                    U32[:, 2 * chn + r: 2 * chn + r + 1], rt(RT_P + r, RT_P + r + 1),
                    -1.0, None, ALU.add),
                    waits=[(s_vec, s_vec.n)], inc=s_vec)
            v_val = P.op("vector", lambda E, chn=chn: E.tensor_copy(
                rt(RT_VAL, RT_VAL + 1), col(IOTA, chn, chn + 1)),
                waits=[(s_vec, s_vec.n), (s_lc, ld_const), (s_stk, s_stk.n)], inc=s_vec)
            for r in range(2):
                P.op("gpsimd", lambda E, chn=chn, r=r: E.indirect_dma_start(
                    out=tok_d[:],
                    out_offset=bass.IndirectOffsetOnAxis(U32[:, 2 * chn + r: 2 * chn + r + 1], 0),
                    in_=rt(RT_VAL, RT_VAL + 1), in_offset=None),
                    waits=[(s_vec, u32v), (s_vec, v_val), (s_stk, stk_zero)],
                    inc=s_stk, k=16)
        accv = [[xn2_done, xn2_done] for _ in range(16)]
        down_prev = 0
        glast = ulast = dlast = 0
        for t4 in range(4):
            for mc in range(4):
                emit_pass2_chunk(4 * t4 + mc)
                gbk, ubk = mc % 2, 2 + mc % 2
                pe_g = pe_u = 0
                for dch in range(8):
                    first, last = dch == 0, dch == 7
                    wgu = ([(s_lwg, wg_ld), (s_vec, xn2_done), bw(gbk)] if first else [])
                    pe_g = P.op("tensor", lambda E, mc=mc, dch=dch, t4=t4, gbk=gbk, first=first, last=last: E.matmul(
                        pb[gbk][:],
                        col(RC_WG, 512 * dch + 128 * mc, 512 * dch + 128 * mc + 128),
                        col(RA_XN2, 2048 * dch + 512 * t4, 2048 * dch + 512 * t4 + 512),
                        start=first, stop=last),
                        waits=wgu, inc=s_pe if last else None)
                for dch in range(8):
                    first, last = dch == 0, dch == 7
                    wuu = ([(s_lwu, wu_ld), bw(ubk)] if first else [])
                    pe_u = P.op("tensor", lambda E, mc=mc, dch=dch, t4=t4, ubk=ubk, first=first, last=last: E.matmul(
                        pb[ubk][:],
                        col(RC_WU, 512 * dch + 128 * mc, 512 * dch + 128 * mc + 128),
                        col(RA_XN2, 2048 * dch + 512 * t4, 2048 * dch + 512 * t4 + 512),
                        start=first, stop=last),
                        waits=wuu, inc=s_pe if last else None)
                glast, ulast = pe_g, pe_u
                a_sg = P.op("scalar", lambda E, gbk=gbk: E.activation(
                    col(SGQ, 0, 512), pb[gbk][:], AF.Silu),
                    waits=[(s_pe, pe_g), (s_act, s_act.n), (s_vec, s_vec.n)], inc=s_act)
                pbf[gbk] = ("act", a_sg)
                v_hm = P.op("vector", lambda E, mc=mc, ubk=ubk: E.tensor_tensor(
                    col(HMID, 512 * mc, 512 * mc + 512), col(SGQ, 0, 512), pb[ubk][:],
                    ALU.mult),
                    waits=[(s_act, a_sg), (s_pe, pe_u), (s_pe, down_prev), (s_vec, s_vec.n)],
                    inc=s_vec)
                pbf[ubk] = ("vec", v_hm)
            hm_all = s_vec.n
            for tt in range(4):
                jg = 4 * t4 + tt
                for half in range(2):
                    bk = 4 + (tt * 2 + half) % 4
                    pe_d = 0
                    for mc in range(4):
                        first, last = mc == 0, mc == 3
                        wdd = ([(s_lwd, wd_ld), (s_vec, hm_all), bw(bk)] if first else [])
                        pe_d = P.op("tensor", lambda E, tt=tt, half=half, mc=mc, bk=bk, first=first, last=last: E.matmul(
                            pb[bk][:],
                            PB[:, HMID + 512 * mc + 128 * tt: HMID + 512 * mc + 128 * tt + 128],
                            col(RC_WD, 1024 * mc + 512 * half, 1024 * mc + 512 * half + 512),
                            start=first, stop=last),
                            waits=wdd, inc=s_pe if last else None)
                    dlast = pe_d
                    v_add = P.op("vector", lambda E, jg=jg, half=half, bk=bk: E.tensor_add(
                        col(RB_ACC, 1024 * jg + 512 * half, 1024 * jg + 512 * half + 512),
                        col(RB_ACC, 1024 * jg + 512 * half, 1024 * jg + 512 * half + 512),
                        pb[bk][:]),
                        waits=[(s_pe, pe_d), (s_vec, accv[jg][half]), (s_vec, s_vec.n)],
                        inc=s_vec)
                    accv[jg][half] = v_add
                    pbf[bk] = ("vec", v_add)
            down_prev = dlast
        routing_v = s_vec.n
        stk_all = s_stk.n

        sh_pe = s_pe.n
        wregs = {}

        def wload(b, free_waits):
            sem = s_wgb[b]
            eng = "sync" if b % 2 == 0 else "scalar"

            def f(E, b=b, sem=sem, eng=eng):
                re = wregs.get(eng)
                if re is None:
                    re = wregs[eng] = E.alloc_register(f"rw_{eng}")
                E.reg_load(re, U32[0:1, 48 + b: 49 + b])
                E.reg_mul(re, re, 2 * D * M)
                E.dma_start(
                    out=bass.AP(PB, RC_WG, [[NCOLS, 128], [512, 16], [1, 512]]),
                    in_=bass.AP(wgu_d, re, [[512, 128], [65536, 16], [1, 512]]),
                ).then_inc(sem.h, 16)
            P.op(eng, f, waits=free_waits + [(s_vec, tilexp_v)], inc=None)
            sem.n += 16
            return sem

        wdu_free = [0, 0]

        def wd_load(b, free_waits):
            par = b % 2
            P.op("vector", lambda E, b=b: E.tensor_scalar(
                rt(RT_P + 4, RT_P + 8), col(IOTA, 0, 4), col(GWT, b, b + 1),
                None, ALU.add),
                waits=[(s_vec, s_vec.n)], inc=s_vec)
            v_wdu = P.op("vector", lambda E, par=par: E.tensor_copy(
                U32[:, 40 + 4 * par: 44 + 4 * par], rt(RT_P + 4, RT_P + 8)),
                waits=[(s_vec, s_vec.n), (s_wdg, wdu_free[par])], inc=s_vec)
            ld = 0
            for mc in range(4):
                ld = P.op("gpsimd", lambda E, par=par, mc=mc: E.indirect_dma_start(
                    out=col(RC_WD, 1024 * mc, 1024 * mc + 1024),
                    out_offset=None, in_=wed_d[:],
                    in_offset=bass.IndirectOffsetOnAxis(
                        U32[:, 40 + 4 * par + mc: 41 + 4 * par + mc], 0)),
                    waits=(free_waits + [(s_vec, v_wdu)]) if mc == 0 else [],
                    inc=s_wdg, k=16)
            wdu_free[par] = ld
            return ld

        tid_free = [0, 0]
        xgtm_free = [sh_pe, sh_pe]
        st_xg = {}

        def stage_tok(b, first=False):
            par = b % 2
            ld_tid = 0
            for t in range(2):
                ld_tid = P.op("sync", lambda E, b=b, t=t, par=par: E.dma_start(
                    out=rt(RT_TID + 4 * par + t, RT_TID + 4 * par + t + 1),
                    in_=bass.AP(tok_d, BLK * b + 128 * t, [[1, 128], [1, 1]])),
                    waits=([(s_stk, stk_all), (s_vec, tid_free[par])] if t == 0 else []),
                    inc=s_ltk[par], k=16)
            v_u32 = P.op("vector", lambda E, par=par: E.tensor_copy(
                U32[:, 32 + 4 * par: 34 + 4 * par],
                rt(RT_TID + 4 * par, RT_TID + 4 * par + 2)),
                waits=[(s_ltk[par], ld_tid), (s_vec, s_vec.n)], inc=s_vec)
            tid_free[par] = v_u32
            xg = 0
            for t in range(2):
                w_g = [(s_vec, v_u32), (s_pe, xgtm_free[par])] if t == 0 else []
                if t == 0 and first:
                    w_g = w_g + xn2_st
                xg = P.op("gpsimd", lambda E, par=par, t=t: E.indirect_dma_start(
                    out=col(RA, XG_TM + 2048 * par + 1024 * t,
                            XG_TM + 2048 * par + 1024 * t + 1024),
                    out_offset=None, in_=xn2tm_d[:],
                    in_offset=bass.IndirectOffsetOnAxis(
                        U32[:, 32 + 4 * par + t: 33 + 4 * par + t], 0)),
                    waits=w_g, inc=s_lxg[par], k=16)
            st_xg[b] = xg

        wgu_cur = wload(0, [(s_pe, ulast)])
        stage_tok(0, first=True)
        wd_cur = wd_load(0, [(s_pe, dlast)])
        gu_prev = sh_pe
        seo_last = [0, 0, 0, 0]
        for b in range(NBLK):
            par = b % 2
            wgl, wdl = wgu_cur, wd_cur
            if b + 1 < NBLK:
                stage_tok(b + 1)
            cnt = 0
            pt_last = 0
            for t in range(2):
                for dch in range(8):
                    bk = 6 + cnt % 2
                    cnt += 1
                    pt_last = P.op("tensor", lambda E, par=par, t=t, dch=dch, bk=bk: E.transpose(
                        pb[bk][:, 0:128],
                        col(RA, XG_TM + 2048 * par + 1024 * t + 128 * dch,
                            XG_TM + 2048 * par + 1024 * t + 128 * dch + 128),
                        col(IDENT, 0, 128)),
                        waits=[(s_lxg[par], st_xg[b]), bw(bk)], inc=s_pe)
                    vcp = P.op("vector", lambda E, t=t, dch=dch, bk=bk: E.tensor_copy(
                        col(RA, XG_FM + 256 * dch + 128 * t,
                            XG_FM + 256 * dch + 128 * t + 128),
                        pb[bk][:, 0:128]),
                        waits=[(s_pe, pt_last), (s_pe, gu_prev), (s_vec, s_vec.n)], inc=s_vec)
                    pbf[bk] = ("vec", vcp)
            xgtm_free[par] = pt_last
            xgfm_v = s_vec.n
            for mc in range(4):
                gbk, ubk = mc % 2, 2 + mc % 2
                pe_g = pe_u = 0
                for dch in range(8):
                    first, last = dch == 0, dch == 7
                    wgu = ([(wgl, 16), (s_vec, xgfm_v), bw(gbk)] if first else [])
                    pe_g = P.op("tensor", lambda E, mc=mc, dch=dch, gbk=gbk, first=first, last=last: E.matmul(
                        pb[gbk][:, 0:256],
                        col(RC_WG, 512 * dch + 128 * mc, 512 * dch + 128 * mc + 128),
                        col(RA, XG_FM + 256 * dch, XG_FM + 256 * dch + 256),
                        start=first, stop=last),
                        waits=wgu, inc=s_pe if last else None)
                for dch in range(8):
                    first, last = dch == 0, dch == 7
                    wuu = ([(wgl, 16), bw(ubk)] if first else [])
                    pe_u = P.op("tensor", lambda E, mc=mc, dch=dch, ubk=ubk, first=first, last=last: E.matmul(
                        pb[ubk][:, 0:256],
                        col(RC_WU, 512 * dch + 128 * mc, 512 * dch + 128 * mc + 128),
                        col(RA, XG_FM + 256 * dch, XG_FM + 256 * dch + 256),
                        start=first, stop=last),
                        waits=wuu, inc=s_pe if last else None)
                glast, ulast = pe_g, pe_u
                a_sg = P.op("scalar", lambda E, gbk=gbk: E.activation(
                    col(SGQ, 0, 256), pb[gbk][:, 0:256], AF.Silu),
                    waits=[(s_pe, pe_g), (s_act, s_act.n), (s_vec, s_vec.n)], inc=s_act)
                pbf[gbk] = ("act", a_sg)
                v_hm = P.op("vector", lambda E, mc=mc, ubk=ubk: E.tensor_tensor(
                    col(HMID, 256 * mc, 256 * mc + 256), col(SGQ, 0, 256),
                    pb[ubk][:, 0:256], ALU.mult),
                    waits=[(s_act, a_sg), (s_pe, pe_u), (s_pe, down_prev), (s_vec, s_vec.n)],
                    inc=s_vec)
                pbf[ubk] = ("vec", v_hm)
            gu_prev = s_pe.n
            hm_all = s_vec.n
            if b + 1 < NBLK:
                wgu_cur = wload(b + 1, [(s_pe, ulast)])
            for sc in range(2):
                for half in range(2):
                    slot = sc * 2 + half
                    bk = 4 + slot
                    pe_d = 0
                    for mc in range(4):
                        first, last = mc == 0, mc == 3
                        wdd = ([(s_wdg, wdl), (s_vec, hm_all), bw(bk)] if first else [])
                        pe_d = P.op("tensor", lambda E, sc=sc, half=half, mc=mc, bk=bk, first=first, last=last: E.matmul(
                            pb[bk][:],
                            PB[:, HMID + 256 * mc + 128 * sc: HMID + 256 * mc + 128 * sc + 128],
                            col(RC_WD, 1024 * mc + 512 * half, 1024 * mc + 512 * half + 512),
                            start=first, stop=last),
                            waits=wdd, inc=s_pe if last else None)
                    dlast = pe_d
                    vcp2 = P.op("vector", lambda E, slot=slot, bk=bk: E.tensor_copy(
                        col(RA, EOB + 512 * slot, EOB + 512 * slot + 512), pb[bk][:]),
                        waits=[(s_pe, pe_d), (s_seo[slot], seo_last[slot]), (s_vec, s_vec.n)],
                        inc=s_vec)
                    seo_last[slot] = P.op("gpsimd", lambda E, b=b, sc=sc, half=half, slot=slot: E.dma_start(
                        out=eo_d[BLK * b + 128 * sc: BLK * b + 128 * sc + 128,
                                 512 * half: 512 * half + 512],
                        in_=col(RA, EOB + 512 * slot, EOB + 512 * slot + 512)),
                        waits=[(s_vec, vcp2)], inc=s_seo[slot], k=16)
                    pbf[bk] = ("vec", vcp2)
            down_prev = dlast
            if b + 1 < NBLK:
                wd_cur = wd_load(b + 1, [(s_pe, dlast)])

        eo_all = [(s_seo[i], s_seo[i].n) for i in range(4)]
        gb_free = [0, 0]
        for chn in range(16):
            for r in range(2):
                par = (2 * chn + r) % 2
                g_v = P.op("gpsimd", lambda E, chn=chn, r=r, par=par: E.indirect_dma_start(
                    out=col(RA, GB + 1024 * par, GB + 1024 * par + 1024),
                    out_offset=None, in_=eo_d[:],
                    in_offset=bass.IndirectOffsetOnAxis(
                        U32[:, 2 * chn + r: 2 * chn + r + 1], 0)),
                    waits=eo_all + [(s_vec, routing_v), (s_vec, gb_free[par])],
                    inc=s_lgb[par], k=16)
                v_a = P.op("vector", lambda E, chn=chn, r=r, par=par: E.scalar_tensor_tensor(
                    col(RB_ACC, 1024 * chn, 1024 * chn + 1024),
                    col(RA, GB + 1024 * par, GB + 1024 * par + 1024),
                    rt(RT_V + 4 * chn + r, RT_V + 4 * chn + r + 1),
                    col(RB_ACC, 1024 * chn, 1024 * chn + 1024),
                    ALU.mult, ALU.add),
                    waits=[(s_lgb[par], g_v), (s_vec, accv[chn][0]), (s_vec, accv[chn][1]),
                           (s_vec, s_vec.n)],
                    inc=s_vec)
                gb_free[par] = v_a
                accv[chn][0] = v_a
                accv[chn][1] = v_a
        for j in range(16):
            P.op("gpsimd", lambda E, j=j: E.dma_start(
                out=out_d[128 * j: 128 * j + 128, :],
                in_=col(RB_ACC, 1024 * j, 1024 * j + 1024)),
                waits=[(s_vec, accv[j][0])], inc=s_so, k=16)
        P.wait("gpsimd", [(s_so, s_so.n)])

        @block.sync
        def _(E):
            for fn in P.ops["sync"]:
                fn(E)

        @block.tensor
        def _(E):
            for fn in P.ops["tensor"]:
                fn(E)

        @block.vector
        def _(E):
            for fn in P.ops["vector"]:
                fn(E)

        @block.scalar
        def _(E):
            for fn in P.ops["scalar"]:
                fn(E)

        @block.gpsimd
        def _(E):
            for fn in P.ops["gpsimd"]:
                fn(E)

    return nc


_NC_CACHE = None


def _get_program():
    global _NC_CACHE
    if _NC_CACHE is None:
        _NC_CACHE = build_program()
    return _NC_CACHE


def make_in_maps(inputs):
    x = np.asarray(inputs["L0_attn_x"], dtype=np.float32)
    WQ = np.asarray(inputs["L0_attn_WQ"], dtype=np.float32)
    bQ = np.asarray(inputs["L0_attn_bQ"], dtype=np.float32)
    WK = np.asarray(inputs["L0_attn_WK"], dtype=np.float32)
    bK = np.asarray(inputs["L0_attn_bK"], dtype=np.float32)
    WV = np.asarray(inputs["L0_attn_WV"], dtype=np.float32)
    bV = np.asarray(inputs["L0_attn_bV"], dtype=np.float32)
    WO = np.asarray(inputs["L0_attn_WO"], dtype=np.float32)
    bO = np.asarray(inputs["L0_attn_bO"], dtype=np.float32)
    gw = np.asarray(inputs["L0_ffn_gate_w"], dtype=np.float32)
    weg = np.asarray(inputs["L0_ffn_w_gate"], dtype=np.float32)
    weu = np.asarray(inputs["L0_ffn_w_up"], dtype=np.float32)
    wed = np.asarray(inputs["L0_ffn_w_down"], dtype=np.float32)
    wsg = np.asarray(inputs["L0_ffn_ws_gate"], dtype=np.float32)
    wsu = np.asarray(inputs["L0_ffn_ws_up"], dtype=np.float32)
    wsd = np.asarray(inputs["L0_ffn_ws_down"], dtype=np.float32)

    wgu_f = np.ascontiguousarray(np.concatenate(
        [weg.reshape(NE, D * M), weu.reshape(NE, D * M)], axis=1).ravel())
    wed_f = np.ascontiguousarray(wed.reshape(NE * M, D))
    ident = np.eye(128, dtype=np.float32)
    iota = (np.arange(128)[:, None] + 128 * np.arange(16)[None, :]).astype(np.float32)
    iota16 = np.broadcast_to(np.arange(16, dtype=np.float32), (128, 16)).copy()
    tilest = np.broadcast_to(256.0 * np.arange(32, dtype=np.float32), (128, 32)).copy()
    thresh = np.broadcast_to(256.0 * np.arange(8, dtype=np.float32), (128, 8)).copy()
    masks = np.stack([
        np.where(np.arange(128)[:, None] + 128 * r <= np.arange(512)[None, :],
                 0.0, -100000.0)
        for r in range(4)]).astype(ml_dtypes.bfloat16)
    in_maps = []
    for c in range(8):
        b, g = c // 2, c % 2
        hs = slice(512 * g, 512 * (g + 1))
        in_maps.append({
            "x": np.ascontiguousarray(x[b]),
            "xres": np.ascontiguousarray(x[b, TOK * g: TOK * (g + 1)]),
            "wq": np.ascontiguousarray(WQ[:, hs]),
            "wk": np.ascontiguousarray(WK[:, hs]),
            "wv": np.ascontiguousarray(WV[:, hs]),
            "bq": np.ascontiguousarray(bQ[hs].reshape(4, 128).T),
            "bk": np.ascontiguousarray(bK[hs].reshape(4, 128).T),
            "bv": np.ascontiguousarray(bV[hs].reshape(4, 128).T),
            "wo": WO,
            "bo": np.ascontiguousarray(bO.reshape(1, 1024)),
            "gwt": np.ascontiguousarray(gw.T),
            "wsg": wsg, "wsu": wsu, "wsd": wsd,
            "wgu": wgu_f, "wed": wed_f,
            "ident": ident, "iota": iota, "iota16": iota16, "masks": masks,
            "tilest": tilest, "thresh": thresh,
        })
    return in_maps


def kernel(**inputs):
    nc = _get_program()
    in_maps = make_in_maps(inputs)
    res = run_bass_kernel_spmd(nc, in_maps, core_ids=list(range(8)))
    out = np.empty((B, S, D), dtype=np.float32)
    for c in range(8):
        b, g = c // 2, c % 2
        out[b, TOK * g: TOK * (g + 1), :] = np.asarray(res.results[c]["out"])
    return out


# version 15
# speedup vs baseline: 1.1226x; 1.0378x over previous
import numpy as np
import ml_dtypes
from contextlib import ExitStack

import concourse.bass as bass
import concourse.mybir as mybir
from concourse.bass_utils import run_bass_kernel_spmd

F32 = mybir.dt.float32
BF16 = mybir.dt.bfloat16
AF = mybir.ActivationFunctionType
ALU = mybir.AluOpType
AX = mybir.AxisListType

B, S, D, NH, DH = 4, 4096, 1024, 16, 64
TOK = 2048
NE, M = 16, 512
NSLOT, NBLK, BLK = 8192, 32, 256
EPS = 1e-5

IDENT = 0
ONES = 128
WO_C = 256
GWT = 1280
BQ, BK, BV, BQS = 1408, 1412, 1416, 1420
NEGC = 1424
NRM = 1426
STAT = 1434
IOTA = 1466
IOTA16B = 1482
TILEST = 1498
THRESH = 1530
BASE_TM = 1538
PBASE = 1554
PER_END = 1590

RA = PER_END
RB = RA + 16384
RC = RB + 16384
SP = RC + 12288
INDP = SP
RT_T = SP + 2048
RTS = SP + 2360
RTR = SP + 2616
HMID = SP + 2872
RC_XR = HMID
SGQ = HMID + 2048
NCOLS = SGQ + 512 + 24

RA_K = RA
RA_Q = RA + 4096
RA_V = RA + 8192
RA_XN2 = RA

RB_A = RB
RB_VF = RB + 3072
RB_BC = RB + 3584
RB_ZS = RB + 4608
RB_ZTMP = RB + 8704
RB_RD = RB + 9216
RB_ACC = RB

RC_X = RC
RC_XN = RC + 2048
RC_W = RC
RC_XNC = RC + 6144
RC_R1 = RC + 9216
RC_ZH = RC + 11264
RC_ZF = RC + 11520
RC_ZT = RC + 11776
RC_WG = RC
RC_WU = RC + 4096
RC_WD = RC + 8192

RT_SIG = 0
RT_SCT = 128
RT_M8 = 144
RT_DEN = 152
RT_RDN = 153
RT_IND = 160
RT_OH0 = 176
RT_OH1 = 192
RT_TMP = 208
RT_V = 224
RT_VAL = 288
RT_P = 292
RT_TID = 300
XG_TM = 0
XG_FM = 4096
EOB = 6144
GB = 8192


class Sem:
    def __init__(self, h):
        self.h = h
        self.n = 0


class Prog:
    ENGINES = ("sync", "tensor", "vector", "scalar", "gpsimd")

    def __init__(self):
        self.ops = {e: [] for e in self.ENGINES}

    def op(self, eng, fn, waits=(), inc=None, k=1):
        tgts = tuple((s.h, int(v)) for (s, v) in waits if v > 0)
        if inc is not None:
            inc.n += k
        inch = inc.h if inc is not None else None

        def run(E, fn=fn, tgts=tgts, inch=inch, k=k):
            for h, v in tgts:
                E.wait_ge(h, v)
            ins = fn(E)
            if inch is not None:
                ins.then_inc(inch, k)
        self.ops[eng].append(run)
        return inc.n if inc is not None else None

    def wait(self, eng, waits):
        tgts = tuple((s.h, int(v)) for (s, v) in waits if v > 0)
        if not tgts:
            return

        def run(E, tgts=tgts):
            for h, v in tgts:
                E.wait_ge(h, v)
        self.ops[eng].append(run)


def build_program():
    nc = bass.Bass("TRN2", target_bir_lowering=False)

    x_d = nc.dram_tensor("x", [S, D], F32, kind="ExternalInput")
    xres_d = nc.dram_tensor("xres", [TOK, D], F32, kind="ExternalInput")
    wq_d = nc.dram_tensor("wq", [D, 512], F32, kind="ExternalInput")
    wk_d = nc.dram_tensor("wk", [D, 512], F32, kind="ExternalInput")
    wv_d = nc.dram_tensor("wv", [D, 512], F32, kind="ExternalInput")
    bq_d = nc.dram_tensor("bq", [128, 4], F32, kind="ExternalInput")
    bk_d = nc.dram_tensor("bk", [128, 4], F32, kind="ExternalInput")
    bv_d = nc.dram_tensor("bv", [128, 4], F32, kind="ExternalInput")
    wo_d = nc.dram_tensor("wo", [64, 1024], F32, kind="ExternalInput")
    bo_d = nc.dram_tensor("bo", [1, 1024], F32, kind="ExternalInput")
    gwt_d = nc.dram_tensor("gwt", [D, NE], F32, kind="ExternalInput")
    wsg_d = nc.dram_tensor("wsg", [D, M], F32, kind="ExternalInput")
    wsu_d = nc.dram_tensor("wsu", [D, M], F32, kind="ExternalInput")
    wsd_d = nc.dram_tensor("wsd", [M, D], F32, kind="ExternalInput")
    wgu_d = nc.dram_tensor("wgu", [NE * 2 * D * M], F32, kind="ExternalInput")
    wed_d = nc.dram_tensor("wed", [NE * M, D], F32, kind="ExternalInput")
    ident_d = nc.dram_tensor("ident", [128, 128], F32, kind="ExternalInput")
    iota_d = nc.dram_tensor("iota", [128, 16], F32, kind="ExternalInput")
    iota16_d = nc.dram_tensor("iota16", [128, 16], F32, kind="ExternalInput")
    tilest_d = nc.dram_tensor("tilest", [128, 32], F32, kind="ExternalInput")
    thresh_d = nc.dram_tensor("thresh", [128, 8], F32, kind="ExternalInput")
    masks_d = nc.dram_tensor("masks", [4, 128, 512], BF16, kind="ExternalInput")
    out_d = nc.dram_tensor("out", [TOK, D], F32, kind="ExternalOutput")

    xn1_d = nc.dram_tensor("xn1", [D, S], F32)
    ztm_d = nc.dram_tensor("ztm", [S, 64], F32)
    zhalf_d = nc.dram_tensor("zhalf", [TOK, 64], F32)
    xn2tm_d = nc.dram_tensor("xn2tm", [TOK, D], F32)
    tok_d = nc.dram_tensor("tok", [NSLOT, 1], F32)
    eo_d = nc.dram_tensor("eo", [NSLOT, D], F32)

    with ExitStack() as ctx:
        PB = nc.alloc_sbuf_tensor("PB", [128, NCOLS], F32)
        MB = nc.alloc_sbuf_tensor("MB", [128, 2048], BF16)
        U32 = nc.alloc_sbuf_tensor("U32", [128, 80], mybir.dt.uint32)
        pb = [nc.alloc_psum_tensor(f"pb{i}", [128, 512], F32) for i in range(8)]

        def mks(name):
            return Sem(ctx.enter_context(nc.semaphore(name)))

        s_pe, s_vec, s_act, s_cc = mks("s_pe"), mks("s_vec"), mks("s_act"), mks("s_cc")
        s_lc = mks("s_lc")
        s_sm = mks("s_sm")
        s_lx = [mks("s_lx0"), mks("s_lx1")]
        s_s1 = [mks("s_s10"), mks("s_s11")]
        s_lw = [mks("s_lw0"), mks("s_lw1")]
        s_lxn = [mks("s_lxn0"), mks("s_lxn1")]
        s_lzh = [mks("s_lzh0"), mks("s_lzh1")]
        s_lxr = [mks("s_lxr0"), mks("s_lxr1")]
        s_szt = [mks("s_szt0"), mks("s_szt1")]
        s_lwg, s_lwu, s_lwd = mks("s_lwg"), mks("s_lwu"), mks("s_lwd")
        s_wdg = mks("s_wdg")
        s_wgb = [mks(f"s_wgb{b}") for b in range(NBLK)]
        s_so = mks("s_so")
        s_sxn2 = [mks("s_sxn20"), mks("s_sxn21")]
        s_stk = mks("s_stk")
        s_ltk = [mks("s_ltk0"), mks("s_ltk1")]
        s_lxg = [mks("s_lxg0"), mks("s_lxg1")]
        s_seo = [mks(f"s_seo{i}") for i in range(4)]
        s_lgb = [mks("s_lgb0"), mks("s_lgb1")]
        block = ctx.enter_context(nc.Block())

        P = Prog()

        def col(base, a, b, r0=0, r1=128):
            return PB[r0:r1, base + a: base + b]

        P.op("sync", lambda E: E.dma_start(out=col(IDENT, 0, 128), in_=ident_d[:]), inc=s_lc, k=16)
        P.op("sync", lambda E: E.dma_start(out=col(WO_C, 0, 1024, 0, 64), in_=wo_d[:]), inc=s_lc, k=16)
        P.op("sync", lambda E: E.dma_start(out=col(WO_C, 0, 1024, 64, 65), in_=bo_d[:]), inc=s_lc, k=16)
        P.op("sync", lambda E: E.dma_start(
            out=bass.AP(PB, GWT, [[NCOLS, 128], [16, 8], [1, 16]]),
            in_=bass.AP(gwt_d, 0, [[16, 128], [2048, 8], [1, 16]])), inc=s_lc, k=16)
        for base, dd in ((IOTA, iota_d), (IOTA16B, iota16_d)):
            P.op("sync", lambda E, dd=dd, base=base: E.dma_start(out=col(base, 0, 16), in_=dd[:]),
                 inc=s_lc, k=16)
        P.op("sync", lambda E: E.dma_start(out=col(TILEST, 0, 32), in_=tilest_d[:]), inc=s_lc, k=16)
        P.op("sync", lambda E: E.dma_start(out=col(THRESH, 0, 8), in_=thresh_d[:]), inc=s_lc, k=16)
        for base, dd in ((BQ, bq_d), (BK, bk_d), (BV, bv_d)):
            P.op("sync", lambda E, dd=dd, base=base: E.dma_start(out=col(base, 0, 4), in_=dd[:]),
                 inc=s_lc, k=16)
        for r in range(4):
            P.op("gpsimd", lambda E, r=r: E.dma_start(out=MB[:, 512 * r: 512 * r + 512], in_=masks_d[r]),
                 inc=s_sm, k=16)
        ld_const = s_lc.n
        mask_ld = s_sm.n
        P.op("vector", lambda E: E.memset(col(ONES, 0, 128), 1.0), inc=s_vec)
        P.op("vector", lambda E: E.memset(
            bass.AP(PB, RA_V + 64, [[NCOLS, 128], [65, 64]]), 1.0), inc=s_vec)
        vones_set = s_vec.n
        P.op("vector", lambda E: E.memset(col(RC_ZF, 0, 256, 64, 65), 1.0), inc=s_vec)
        P.op("vector", lambda E: E.tensor_scalar(col(BQS, 0, 4), col(BQ, 0, 4), 0.125, None, ALU.mult),
             waits=[(s_lc, ld_const)], inc=s_vec)
        P.op("vector", lambda E: E.tensor_scalar(col(WO_C, 0, 1024, 64, 65), col(WO_C, 0, 1024, 64, 65),
                                                 16.0, None, ALU.mult),
             waits=[(s_lc, ld_const)], inc=s_vec)
        vec_const = s_vec.n

        def emit_ln(x_fn, dump_fn, xn_fn, sb, W, x_ready, dump_free=(), xn_free=()):
            st_ = lambda i: col(STAT, sb + i, sb + i + 1)
            v_rs = P.op("vector", lambda E: E.reduce_sum(st_(0), x_fn(), axis=AX.X),
                        waits=list(x_ready) + [(s_vec, s_vec.n)], inc=s_vec)
            a_sq = P.op("scalar", lambda E: E.activation(dump_fn(), x_fn(), AF.Square,
                                                         accum_out=st_(1)),
                        waits=list(x_ready) + list(dump_free) + [(s_act, s_act.n)], inc=s_act)
            v_m = P.op("vector", lambda E: E.tensor_scalar(st_(2), st_(0), 1.0 / W, None, ALU.mult),
                       waits=[(s_vec, v_rs)], inc=s_vec)
            v_e2 = P.op("vector", lambda E: E.tensor_scalar(st_(3), st_(1), 1.0 / W, EPS,
                                                            ALU.mult, ALU.add),
                        waits=[(s_act, a_sq)], inc=s_vec)
            v_ms = P.op("vector", lambda E: E.tensor_scalar(st_(4), st_(2), st_(2), None, ALU.mult),
                        waits=[(s_vec, v_m)], inc=s_vec)
            v_var = P.op("vector", lambda E: E.tensor_sub(st_(3), st_(3), st_(4)),
                         waits=[(s_vec, v_ms), (s_vec, v_e2)], inc=s_vec)
            a_sd = P.op("scalar", lambda E: E.activation(st_(5), st_(3), AF.Sqrt),
                        waits=[(s_vec, v_var)], inc=s_act)
            v_rd = P.op("vector", lambda E: E.reciprocal(st_(6), st_(5)),
                        waits=[(s_act, a_sd)], inc=s_vec)
            v_xn = P.op("vector", lambda E: E.tensor_scalar(xn_fn(), x_fn(), st_(2), st_(6),
                                                            ALU.subtract, ALU.mult),
                        waits=[(s_vec, v_rd), (s_act, a_sq)] + list(xn_free), inc=s_vec)
            return v_xn, a_sq

        blk_store_done = [0, 0]
        xtile_free = [0, 0]
        xntile_free = [0, 0]
        trb_free = [0, 0]
        for tb in range(8):
            rpar = tb % 2
            blk = RA + 4096 * rpar
            last_copy = 0
            for tt in range(4):
                t128 = tb * 4 + tt
                par = t128 % 2
                xap = lambda par=par: col(RC_X, 1024 * par, 1024 * par + 1024)
                xnap = lambda par=par: col(RC_XN, 1024 * par, 1024 * par + 1024)
                ldv = P.op("sync", lambda E, t128=t128, par=par: E.dma_start(
                    out=col(RC_X, 1024 * par, 1024 * par + 1024),
                    in_=x_d[128 * t128: 128 * t128 + 128, :]),
                    waits=[(s_vec, xtile_free[par])], inc=s_lx[par], k=16)
                v_xn, _ = emit_ln(xap, xnap, xnap, 16 * par, 1024,
                                  [(s_lx[par], ldv)],
                                  dump_free=[(s_pe, xntile_free[par])],
                                  xn_free=[(s_pe, xntile_free[par])])
                xtile_free[par] = v_xn
                for bk, dbase in ((0, 0), (1, 4)):
                    pt = 0
                    for dd in range(4):
                        dch = dbase + dd
                        pt = P.op("tensor", lambda E, par=par, dch=dch, dd=dd, bk=bk: E.transpose(
                            pb[bk][:, 128 * dd: 128 * dd + 128],
                            col(RC_XN, 1024 * par + 128 * dch, 1024 * par + 128 * dch + 128),
                            col(IDENT, 0, 128)),
                            waits=([(s_vec, v_xn), (s_vec, trb_free[bk]), (s_lc, ld_const),
                                    (s_s1[rpar], blk_store_done[rpar])]
                                   if dd == 0 else []),
                            inc=s_pe if dd == 3 else None)
                    vc = P.op("vector", lambda E, blk=blk, tt=tt, bk=bk, dbase=dbase: E.tensor_copy(
                        bass.AP(PB, blk + 512 * dbase + 128 * tt,
                                [[NCOLS, 128], [512, 4], [1, 128]]),
                        pb[bk][:, 0:512]),
                        waits=[(s_pe, pt)], inc=s_vec)
                    trb_free[bk] = vc
                    last_copy = vc
                xntile_free[par] = s_pe.n
            for dch in range(8):
                P.op("gpsimd", lambda E, blk=blk, dch=dch, tb=tb: E.dma_start(
                    out=xn1_d[128 * dch: 128 * dch + 128, 512 * tb: 512 * tb + 512],
                    in_=col(0, blk + 512 * dch, blk + 512 * dch + 512)),
                    waits=[(s_vec, last_copy)], inc=s_s1[rpar], k=16)
            blk_store_done[rpar] = s_s1[rpar].n
        ph1_st = [(s_s1[0], s_s1[0].n), (s_s1[1], s_s1[1].n)]
        ph1_vec = s_vec.n

        wload_done = [0, 0, 0, 0]
        proj_done = [0, 0, 0, 0]

        def load_wset(p):
            for mi, wd in enumerate((wq_d, wk_d, wv_d)):
                for dch in range(8):
                    P.op("sync", lambda E, wd=wd, mi=mi, p=p, dch=dch: E.dma_start(
                        out=col(RC_W + 3072 * (p % 2) + 1024 * mi,
                                128 * dch, 128 * dch + 128),
                        in_=wd[128 * dch: 128 * dch + 128, 128 * p: 128 * p + 128]),
                        waits=([(s_vec, ph1_vec if p < 2 else 0),
                                (s_pe, proj_done[p - 2] if p >= 2 else 0)]
                               if dch == 0 and mi == 0 else []),
                        inc=s_lw[p % 2], k=16)
            wload_done[p] = s_lw[p % 2].n

        load_wset(0)
        load_wset(1)

        xnc_free = [0, 0]
        vt_bank_free = [0, 0]
        exp_free = [[0, 0], [0, 0]]
        av_free3 = [[0, 0, 0], [0, 0, 0]]
        av_guard = [0, 0]
        z_cons = [0, 0]
        bc_cons = [0, 0]
        rd_free = [0, 0]
        sgq_free = [0, 0]
        vf_free = 0

        negc_done = 0
        zt_free = [0, 0]
        zt_vfree = [0, 0]
        NRM_ACC = ((0, 1), (5, 6))
        NRM_SCR = (2, 7)

        def emit_norm_step(ch, h, nco, base, first):
            r0, r1 = 64 * h, 64 * h + 64
            a_sq = P.op("scalar", lambda E, base=base, ch=ch, r0=r0, r1=r1: E.activation(
                PB[r0:r1, SGQ: SGQ + 512],
                PB[r0:r1, base + 512 * ch: base + 512 * ch + 512], AF.Square),
                waits=[(s_pe, sgq_free[h]), (s_act, s_act.n)], inc=s_act)
            p_n = P.op("tensor", lambda E, r0=r0, r1=r1, h=h: E.matmul(
                pb[6 + h][0:1, 0:512],
                PB[r0:r1, ONES: ONES + 1], PB[r0:r1, SGQ: SGQ + 512],
                start=True, stop=True),
                waits=[(s_act, a_sq), (s_vec, vt_bank_free[h])], inc=s_pe)
            sgq_free[h] = p_n
            sc = NRM_SCR[h]
            v_mx = P.op("vector", lambda E, sc=sc, h=h: E.reduce_max(
                col(NRM, sc, sc + 1, 0, 1), pb[6 + h][0:1, 0:512], axis=AX.X),
                waits=[(s_pe, p_n), (s_vec, s_vec.n)], inc=s_vec)
            ac = NRM_ACC[h][nco]
            if first:
                v_a = P.op("vector", lambda E, ac=ac, sc=sc: E.tensor_copy(
                    col(NRM, ac, ac + 1, 0, 1), col(NRM, sc, sc + 1, 0, 1)),
                    waits=[(s_vec, v_mx)], inc=s_vec)
            else:
                v_a = P.op("vector", lambda E, ac=ac, sc=sc: E.tensor_tensor(
                    col(NRM, ac, ac + 1, 0, 1), col(NRM, ac, ac + 1, 0, 1),
                    col(NRM, sc, sc + 1, 0, 1), ALU.max),
                    waits=[(s_vec, v_mx)], inc=s_vec)
            vt_bank_free[h] = v_a

        for p in range(4):
            pair_guard = (([(s_act, v) for hh in exp_free for v in hh] +
                           [(s_vec, av_guard[0]), (s_vec, av_guard[1]),
                            (s_vec, bc_cons[0]), (s_vec, bc_cons[1])])
                          if p > 0 else [])
            pg67 = [av_guard[1], av_guard[1]]
            evac_k = [0, 0]
            evac_v = [0, 0]
            evac_q = [0, 0]
            for tc in range(8):
                par = tc % 2
                pe_projs = 0
                wq_off = RC_W + 3072 * (p % 2)
                for dch in range(8):
                    ci = tc * 8 + dch
                    cpar = ci % 2
                    ldv = P.op("sync", lambda E, dch=dch, tc=tc, cpar=cpar: E.dma_start(
                        out=col(RC_XNC, 512 * cpar, 512 * cpar + 512),
                        in_=xn1_d[128 * dch: 128 * dch + 128, 512 * tc: 512 * tc + 512]),
                        waits=ph1_st + [(s_pe, xnc_free[cpar]),
                                        (s_lw[p % 2], wload_done[p])],
                        inc=s_lxn[cpar], k=16)
                    mv = lambda cpar=cpar: col(RC_XNC, 512 * cpar, 512 * cpar + 512)
                    first, last = dch == 0, dch == 7
                    w8 = [(s_lxn[cpar], ldv)]
                    if first:
                        w8 += [(s_act, evac_k[par]), (s_act, evac_v[par]), (s_act, evac_q[par])]
                        if tc < 2:
                            w8 += pair_guard
                    P.op("tensor", lambda E, wq_off=wq_off, dch=dch, par=par, mv=mv, first=first, last=last: E.matmul(
                        pb[0 + par][:], col(wq_off + 1024, 128 * dch, 128 * dch + 128), mv(),
                        start=first, stop=last), waits=w8)
                    P.op("tensor", lambda E, wq_off=wq_off, dch=dch, par=par, mv=mv, first=first, last=last: E.matmul(
                        pb[2 + par][:], col(wq_off + 2048, 128 * dch, 128 * dch + 128), mv(),
                        start=first, stop=last))
                    pe_projs = P.op("tensor", lambda E, wq_off=wq_off, dch=dch, par=par, mv=mv, first=first, last=last: E.matmul(
                        pb[4 + par][:], col(wq_off, 128 * dch, 128 * dch + 128), mv(),
                        start=first, stop=last), inc=s_pe)
                    xnc_free[cpar] = pe_projs
                evac_k[par] = P.op("scalar", lambda E, tc=tc, par=par, p=p: E.activation(
                    col(RA_K, 512 * tc, 512 * tc + 512), pb[0 + par][:],
                    AF.Identity, bias=col(BK, p, p + 1)),
                    waits=[(s_pe, pe_projs), (s_lc, ld_const), (s_act, s_act.n)], inc=s_act)
                evac_v[par] = P.op("scalar", lambda E, par=par, p=p: E.activation(
                    col(RB_VF, 0, 512), pb[2 + par][:],
                    AF.Identity, bias=col(BV, p, p + 1)),
                    waits=[(s_pe, pe_projs), (s_pe, vf_free)], inc=s_act)
                evac_q[par] = P.op("scalar", lambda E, tc=tc, par=par, p=p: E.activation(
                    col(RA_Q, 512 * tc, 512 * tc + 512), pb[4 + par][:],
                    AF.Identity, bias=col(BQS, p, p + 1), scale=0.125),
                    waits=[(s_pe, pe_projs)], inc=s_act)
                for j in range(4):
                    bk = 6 + j % 2
                    k4 = tc * 4 + j
                    pt = P.op("tensor", lambda E, j=j, bk=bk: E.transpose(
                        pb[bk][:, 0:128], col(RB_VF, 128 * j, 128 * j + 128), col(IDENT, 0, 128)),
                        waits=[(s_act, evac_v[par]), (s_vec, vt_bank_free[j % 2]),
                               (s_vec, pg67[j % 2])],
                        inc=s_pe)
                    P.op("vector", lambda E, k4=k4, bk=bk: E.tensor_copy(
                        col(RA_V, 130 * k4, 130 * k4 + 64), pb[bk][:, 0:64]),
                        waits=[(s_pe, pt), (s_vec, vones_set)], inc=s_vec)
                    vt_bank_free[j % 2] = P.op("vector", lambda E, k4=k4, bk=bk: E.tensor_copy(
                        col(RA_V, 130 * k4 + 65, 130 * k4 + 129), pb[bk][:, 64:128]),
                        inc=s_vec)
                vf_free = s_pe.n
                if tc > 0:
                    for nco, base in ((0, RA_Q), (1, RA_K)):
                        for h in range(2):
                            emit_norm_step(tc - 1, h, nco, base, tc == 1)
            proj_done[p] = s_pe.n
            evac_all = [(s_act, s_act.n), (s_vec, s_vec.n)]
            if p + 2 < 4:
                load_wset(p + 2)

            for nco, base in ((0, RA_Q), (1, RA_K)):
                for h in range(2):
                    emit_norm_step(7, h, nco, base, False)
            for h in range(2):
                a1 = P.op("scalar", lambda E, h=h: E.activation(
                    col(NRM, 3, 4, 0, 1), col(NRM, NRM_ACC[h][0], NRM_ACC[h][0] + 1, 0, 1),
                    AF.Sqrt),
                    waits=[(s_vec, s_vec.n), (s_act, s_act.n)], inc=s_act)
                a2 = P.op("scalar", lambda E, h=h: E.activation(
                    col(NRM, 4, 5, 0, 1), col(NRM, NRM_ACC[h][1], NRM_ACC[h][1] + 1, 0, 1),
                    AF.Sqrt),
                    inc=s_act)
                v1 = P.op("vector", lambda E: E.tensor_tensor(
                    col(NRM, 3, 4, 0, 1), col(NRM, 3, 4, 0, 1), col(NRM, 4, 5, 0, 1), ALU.mult),
                    waits=[(s_act, a2), (s_vec, s_vec.n)], inc=s_vec)
                v2 = P.op("vector", lambda E: E.tensor_scalar(
                    col(NRM, 3, 4, 0, 1), col(NRM, 3, 4, 0, 1), -1.0, None, ALU.mult),
                    waits=[(s_vec, v1)], inc=s_vec)
                p_bc = P.op("tensor", lambda E, h=h: E.matmul(
                    pb[6 + h][:, 0:1], PB[0:1, ONES: ONES + 128], col(NRM, 3, 4, 0, 1),
                    start=True, stop=True),
                    waits=[(s_vec, v2), (s_vec, vt_bank_free[h])], inc=s_pe)
                v_nc = P.op("vector", lambda E, h=h: E.tensor_copy(
                    col(NEGC, h, h + 1), pb[6 + h][:, 0:1]),
                    waits=[(s_pe, p_bc), (s_act, s_act.n), (s_vec, s_vec.n)], inc=s_vec)
                vt_bank_free[h] = v_nc
            negc_done = s_vec.n

            for qb in range(8):
                nkc = 4 * qb + 4
                e_slot = [[0, 0, 0], [0, 0, 0]]
                av_ab = [0, 0]
                for kc in range(nkc + 2):
                    if kc < nkc:
                        pr = kc % 2
                        sl = kc % 3
                        sb = (0, 1) if pr == 0 else (4, 5)
                        pe_ss = [0, 0]
                        for h in range(2):
                            r0, r1 = 64 * h, 64 * h + 64
                            w_s = [(s_act, exp_free[h][pr]), (s_vec, negc_done)]
                            if kc == 0:
                                w_s.append((s_vec, bc_cons[h]))
                            pe_ss[h] = P.op("tensor", lambda E, r0=r0, r1=r1, kc=kc, qb=qb, b=sb[h]: E.matmul(
                                pb[b][:], PB[r0:r1, RA_K + 128 * kc: RA_K + 128 * kc + 128],
                                PB[r0:r1, RA_Q + 512 * qb: RA_Q + 512 * qb + 512],
                                start=True, stop=True),
                                waits=w_s, inc=s_pe)
                        diag = kc >= 4 * qb
                        for h in range(2):
                            if diag:
                                r = kc - 4 * qb
                                v_m = P.op("vector", lambda E, b=sb[h], r=r: E.tensor_add(
                                    pb[b][:], pb[b][:], MB[:, 512 * r: 512 * r + 512]),
                                    waits=[(s_pe, pe_ss[h]), (s_sm, mask_ld), (s_vec, s_vec.n)],
                                    inc=s_vec)
                                w_exp = [(s_vec, v_m)]
                            else:
                                w_exp = [(s_pe, pe_ss[h])]
                            e_slot[h][sl] = P.op("scalar", lambda E, h=h, sl=sl, b=sb[h]: E.activation(
                                col(RB_A, 512 * (3 * h + sl), 512 * (3 * h + sl) + 512), pb[b][:],
                                AF.Exp, bias=col(NEGC, h, h + 1)),
                                waits=w_exp + [(s_pe, av_free3[h][sl])], inc=s_act)
                            exp_free[h][pr] = e_slot[h][sl]
                    if kc >= 2:
                        kp = kc - 2
                        slp = kp % 3
                        last = kp == nkc - 1
                        for h in range(2):
                            w_a = [(s_act, e_slot[h][slp])]
                            if kp == 0:
                                w_a += [(s_vec, av_guard[h]), (s_vec, negc_done)]
                            av_ab[h] = P.op("tensor", lambda E, kp=kp, h=h, slp=slp, last=last: E.matmul(
                                pb[2 + h][0:65, :],
                                col(RA_V, 130 * kp + 65 * h, 130 * kp + 65 * h + 65),
                                col(RB_A, 512 * (3 * h + slp), 512 * (3 * h + slp) + 512),
                                start=(kp == 0), stop=last),
                                waits=w_a, inc=s_pe)
                            av_free3[h][slp] = av_ab[h]
                for h in range(2):
                    v_rd = P.op("vector", lambda E, h=h: E.reciprocal(
                        col(RB_RD, 512 * h, 512 * h + 512, 64, 65), pb[2 + h][64:65, :]),
                        waits=[(s_pe, av_ab[h]), (s_pe, rd_free[h]), (s_vec, s_vec.n)],
                        inc=s_vec)
                    p_b = P.op("tensor", lambda E, h=h: E.matmul(
                        pb[h][0:64, :], PB[64:65, ONES: ONES + 64],
                        col(RB_RD, 512 * h, 512 * h + 512, 64, 65),
                        start=True, stop=True),
                        waits=[(s_vec, v_rd), (s_act, exp_free[h][0])], inc=s_pe)
                    rd_free[h] = p_b
                    v_bc = P.op("vector", lambda E, h=h: E.tensor_copy(
                        col(RB_BC, 512 * h, 512 * h + 512, 0, 64), pb[h][0:64, :]),
                        waits=[(s_pe, p_b), (s_vec, s_vec.n)], inc=s_vec)
                    bc_cons[h] = v_bc
                    v_zt = P.op("vector", lambda E, h=h: E.tensor_tensor(
                        col(RB_ZTMP, 0, 512, 0, 64), pb[2 + h][0:64, :],
                        col(RB_BC, 512 * h, 512 * h + 512, 0, 64), ALU.mult),
                        waits=[(s_vec, s_vec.n)], inc=s_vec)
                    av_guard[h] = v_zt
                    if p == 0 and h == 0:
                        z_cons[h] = P.op("vector", lambda E, qb=qb: E.tensor_copy(
                            col(RB_ZS, 512 * qb, 512 * qb + 512, 0, 64),
                            col(RB_ZTMP, 0, 512, 0, 64)),
                            waits=[(s_vec, v_zt)], inc=s_vec)
                    else:
                        z_cons[h] = P.op("vector", lambda E, qb=qb: E.tensor_add(
                            col(RB_ZS, 512 * qb, 512 * qb + 512, 0, 64),
                            col(RB_ZS, 512 * qb, 512 * qb + 512, 0, 64),
                            col(RB_ZTMP, 0, 512, 0, 64)),
                            waits=[(s_vec, v_zt), (s_vec, s_vec.n)], inc=s_vec)
                if p == 3:
                    for jj in range(4):
                        j = 4 * qb + jj
                        jp = jj % 2
                        ptz = P.op("tensor", lambda E, j=j, bk6=6 + jp: E.transpose(
                            pb[bk6][:, 0:64], col(RB_ZS, 128 * j, 128 * j + 128, 0, 64),
                            col(IDENT, 0, 64, 0, 64)),
                            waits=[(s_vec, z_cons[1]), (s_vec, zt_vfree[jp]),
                                   (s_szt[jp], zt_free[jp])], inc=s_pe)
                        vcz = P.op("vector", lambda E, jp=jp, bk6=6 + jp: E.tensor_copy(
                            col(RC_ZT, 64 * jp, 64 * jp + 64), pb[bk6][:, 0:64]),
                            waits=[(s_pe, ptz), (s_szt[jp], zt_free[jp]),
                                   (s_vec, s_vec.n)], inc=s_vec)
                        zt_vfree[jp] = vcz
                        zt_free[jp] = P.op("gpsimd", lambda E, j=j, jp=jp: E.dma_start(
                            out=ztm_d[128 * j: 128 * j + 128, :],
                            in_=col(RC_ZT, 64 * jp, 64 * jp + 64)),
                            waits=[(s_vec, vcz)], inc=s_szt[jp], k=16)
        zsum_done = s_vec.n

        pbf = {i: ("vec", 0) for i in range(8)}

        def bw(i):
            eng, v = pbf[i]
            return (s_vec if eng == "vec" else (s_act if eng == "act" else s_pe), v)

        rt = lambda a, b, r0=0, r1=128: col(RT_T, a, b, r0, r1)
        pbf[6] = ("vec", zt_vfree[0])
        pbf[7] = ("vec", zt_vfree[1])

        def emit_routing_chunk(chn):
            pe_sc = 0
            for dch in range(8):
                first, last = dch == 0, dch == 7
                w0_ = ([(s_vec, s_vec.n), bw(7), (s_lc, ld_const)] if first else [])
                pe_sc = P.op("tensor", lambda E, chn=chn, dch=dch, first=first, last=last: E.matmul(
                    pb[7][0:16, 0:128], col(GWT, 16 * dch, 16 * dch + 16),
                    col(RA_XN2, 2048 * dch + 128 * chn, 2048 * dch + 128 * chn + 128),
                    start=first, stop=last),
                    waits=w0_, inc=s_pe if last else None)
            a_sig = P.op("scalar", lambda E: E.activation(
                rt(RT_SIG, RT_SIG + 128, 0, 16), pb[7][0:16, 0:128], AF.Sigmoid),
                waits=[(s_pe, pe_sc), (s_act, s_act.n)], inc=s_act)
            pt = P.op("tensor", lambda E: E.transpose(
                pb[7][:, 0:16], rt(RT_SIG, RT_SIG + 128, 0, 16), col(IDENT, 0, 16, 0, 16)),
                waits=[(s_act, a_sig)], inc=s_pe)
            v_sct = P.op("vector", lambda E, chn=chn: E.tensor_copy(
                col(RTS, 16 * chn, 16 * chn + 16), pb[7][:, 0:16]),
                waits=[(s_pe, pt), (s_vec, s_vec.n)], inc=s_vec)
            P.op("vector", lambda E, chn=chn: E.max(out=rt(RT_M8, RT_M8 + 8),
                                                    in_=col(RTS, 16 * chn, 16 * chn + 16)),
                 waits=[(s_vec, v_sct)], inc=s_vec)
            P.op("vector", lambda E: E.scalar_tensor_tensor(
                rt(RT_DEN, RT_DEN + 1), rt(RT_M8, RT_M8 + 1), 1e-20,
                rt(RT_M8 + 1, RT_M8 + 2), ALU.add, ALU.add),
                waits=[(s_vec, s_vec.n)], inc=s_vec)
            P.op("vector", lambda E: E.reciprocal(rt(RT_RDN, RT_RDN + 1), rt(RT_DEN, RT_DEN + 1)),
                 waits=[(s_vec, s_vec.n)], inc=s_vec)
            P.op("vector", lambda E, chn=chn: E.tensor_scalar(
                rt(RT_V + 4 * chn, RT_V + 4 * chn + 2), rt(RT_M8, RT_M8 + 2),
                rt(RT_RDN, RT_RDN + 1), 2.5, ALU.mult, ALU.mult),
                waits=[(s_vec, s_vec.n)], inc=s_vec)
            P.op("vector", lambda E, chn=chn: E.tensor_scalar(
                rt(RT_IND, RT_IND + 16), col(RTS, 16 * chn, 16 * chn + 16),
                rt(RT_M8 + 1, RT_M8 + 2), None, ALU.is_ge),
                waits=[(s_vec, s_vec.n)], inc=s_vec)
            pt2 = P.op("tensor", lambda E: E.transpose(
                pb[7][0:16, 0:128], rt(RT_IND, RT_IND + 16), col(IDENT, 0, 128)),
                waits=[(s_vec, s_vec.n)], inc=s_pe)
            v_ifm = P.op("vector", lambda E, chn=chn: E.tensor_copy(
                col(INDP, 128 * chn, 128 * chn + 128, 0, 16), pb[7][0:16, 0:128]),
                waits=[(s_pe, pt2), (s_vec, s_vec.n)], inc=s_vec)
            pbf[7] = ("vec", v_ifm)

        ztm_stores = [(s_szt[0], s_szt[0].n), (s_szt[1], s_szt[1].n)]
        v_z = P.op("vector", lambda E: E.memset(col(RC_XNC, 0, NSLOT // 128), 0.0),
                   waits=[(s_pe, proj_done[3]), (s_vec, s_vec.n)], inc=s_vec)
        P.op("gpsimd", lambda E: E.dma_start(out=tok_d[:], in_=col(RC_XNC, 0, NSLOT // 128)),
             waits=[(s_vec, v_z)], inc=s_stk, k=16)
        stk_zero = s_stk.n
        P.op("gpsimd", lambda E: E.collective_compute(
            "ReduceScatter", ALU.add,
            replica_groups=[[0, 1], [2, 3], [4, 5], [6, 7]],
            ins=[ztm_d[:]], outs=[zhalf_d[:]]),
            waits=ztm_stores, inc=s_cc)

        epi_zh_free = [0, 0]
        epi_xr_free = [[], []]
        epi_r1_free = [[], []]
        epi_zf_free = [0, 0]
        bank_free_v = {2: 0, 3: 0, 4: 0, 5: 0}
        tr01_free = [0, 0]
        for j in range(16):
            par = j % 2
            ld_zh = P.op("sync", lambda E, j=j, par=par: E.dma_start(
                out=col(RC_ZH, 64 * par, 64 * par + 64), in_=zhalf_d[128 * j: 128 * j + 128, :]),
                waits=[(s_cc, 1), (s_pe, epi_zh_free[par])], inc=s_lzh[par], k=16)
            ld_xr = P.op("sync", lambda E, j=j, par=par: E.dma_start(
                out=col(RC_XR, 1024 * par, 1024 * par + 1024),
                in_=xres_d[128 * j: 128 * j + 128, :]),
                waits=epi_xr_free[par], inc=s_lxr[par], k=16)
            pt = P.op("tensor", lambda E, par=par: E.transpose(
                pb[par][0:64, 0:128], col(RC_ZH, 64 * par, 64 * par + 64), col(IDENT, 0, 128)),
                waits=[(s_lzh[par], ld_zh), (s_vec, tr01_free[par]),
                       (s_act, max(exp_free[0][par], exp_free[1][par]))],
                inc=s_pe)
            epi_zh_free[par] = pt
            v_zf = P.op("vector", lambda E, par=par: E.tensor_copy(
                col(RC_ZF, 128 * par, 128 * par + 128, 0, 64), pb[par][0:64, 0:128]),
                waits=[(s_pe, pt), (s_pe, epi_zf_free[par]), (s_vec, s_vec.n)], inc=s_vec)
            tr01_free[par] = v_zf
            pe_o = []
            for half in range(2):
                bk = 2 + half
                pe_o.append(P.op("tensor", lambda E, par=par, half=half, bk=bk: E.matmul(
                    pb[bk][:], col(RC_ZF, 128 * par, 128 * par + 128, 0, 65),
                    col(WO_C, 512 * half, 512 * half + 512, 0, 65),
                    start=True, stop=True),
                    waits=[(s_vec, v_zf), (s_vec, bank_free_v[bk]), (s_vec, vec_const)],
                    inc=s_pe))
            epi_zf_free[par] = pe_o[-1]
            v_r1 = []
            for half in range(2):
                v_r1.append(P.op("vector", lambda E, par=par, half=half, bk=2 + half: E.tensor_add(
                    col(RC_R1, 1024 * par + 512 * half, 1024 * par + 512 * half + 512),
                    pb[bk][:],
                    col(RC_XR, 1024 * par + 512 * half, 1024 * par + 512 * half + 512)),
                    waits=[(s_pe, pe_o[half]), (s_lxr[par], ld_xr),
                           (s_vec, s_vec.n)] + epi_r1_free[par],
                    inc=s_vec))
            bank_free_v[2] = v_r1[0]
            bank_free_v[3] = v_r1[1]
            v_acc = P.op("vector", lambda E, j=j, par=par: E.tensor_copy(
                col(RB_ACC, 1024 * j, 1024 * j + 1024),
                col(RC_R1, 1024 * par, 1024 * par + 1024)),
                waits=[(s_vec, v_r1[1])], inc=s_vec)
            v_xn2, a_sq2 = emit_ln(
                lambda par=par: col(RC_R1, 1024 * par, 1024 * par + 1024),
                lambda par=par: col(RC_XR, 1024 * par, 1024 * par + 1024),
                lambda par=par: col(RC_R1, 1024 * par, 1024 * par + 1024),
                16 * par, 1024,
                [(s_vec, v_r1[1])],
                dump_free=[(s_vec, v_r1[1])],
                xn_free=[(s_vec, v_acc)])
            epi_xr_free[par] = [(s_act, a_sq2), (s_vec, v_r1[1])]
            st_x2 = P.op("gpsimd", lambda E, j=j, par=par: E.dma_start(
                out=xn2tm_d[128 * j: 128 * j + 128, :],
                in_=col(RC_R1, 1024 * par, 1024 * par + 1024)),
                waits=[(s_vec, v_xn2)], inc=s_sxn2[par], k=16)
            pt3 = 0
            for bk, dbase in ((4, 0), (5, 4)):
                for dd in range(4):
                    dch = dbase + dd
                    pt3 = P.op("tensor", lambda E, par=par, dch=dch, dd=dd, bk=bk: E.transpose(
                        pb[bk][:, 128 * dd: 128 * dd + 128],
                        col(RC_R1, 1024 * par + 128 * dch, 1024 * par + 128 * dch + 128),
                        col(IDENT, 0, 128)),
                        waits=([(s_vec, v_xn2), (s_vec, bank_free_v[bk])]
                               if dd == 0 else []),
                        inc=s_pe if dd == 3 else None)
                vc3 = P.op("vector", lambda E, j=j, bk=bk, dbase=dbase: E.tensor_copy(
                    bass.AP(PB, RA_XN2 + 2048 * dbase + 128 * j,
                            [[NCOLS, 128], [2048, 4], [1, 128]]),
                    pb[bk][:, 0:512]),
                    waits=[(s_pe, pt3), (s_vec, s_vec.n)], inc=s_vec)
                bank_free_v[bk] = vc3
            epi_r1_free[par] = [(s_pe, pt3), (s_sxn2[par], st_x2)]
            emit_routing_chunk(j)
        xn2_done = s_vec.n
        epi_pe = s_pe.n
        xn2_st = [(s_sxn2[0], s_sxn2[0].n), (s_sxn2[1], s_sxn2[1].n)]

        epi_clear = [(s_vec, xn2_done), (s_pe, epi_pe)] + xn2_st
        wg_ld = wu_ld = wd_ld = 0

        def load_w(gt, ut, dt_, e, w_g, w_u, w_d):
            nonlocal wg_ld, wu_ld, wd_ld
            for dch in range(8):
                wg_ld = P.op("sync", lambda E, gt=gt, e=e, dch=dch: E.dma_start(
                    out=col(RC_WG, 512 * dch, 512 * dch + 512),
                    in_=(gt[128 * dch: 128 * dch + 128, :] if e is None
                         else gt[e, 128 * dch: 128 * dch + 128, :])),
                    waits=w_g if dch == 0 else [],
                    inc=s_lwg, k=16)
            for dch in range(8):
                wu_ld = P.op("sync", lambda E, ut=ut, e=e, dch=dch: E.dma_start(
                    out=col(RC_WU, 512 * dch, 512 * dch + 512),
                    in_=(ut[128 * dch: 128 * dch + 128, :] if e is None
                         else ut[e, 128 * dch: 128 * dch + 128, :])),
                    waits=w_u if dch == 0 else [],
                    inc=s_lwu, k=16)
            for mc in range(4):
                wd_ld = P.op("sync", lambda E, dt_=dt_, e=e, mc=mc: E.dma_start(
                    out=col(RC_WD, 1024 * mc, 1024 * mc + 1024),
                    in_=(dt_[128 * mc: 128 * mc + 128, :] if e is None
                         else dt_[e, 128 * mc: 128 * mc + 128, :])),
                    waits=w_d if mc == 0 else [],
                    inc=s_lwd, k=16)

        load_w(wsg_d, wsu_d, wsd_d, None,
               [(s_pe, proj_done[3])],
               [(s_pe, proj_done[3]), (s_stk, stk_zero)],
               epi_clear + ztm_stores)

        v_scan = P.op("vector", lambda E: E.tensor_tensor_scan(
            col(INDP, 0, 2048, 0, 16), col(INDP, 0, 2048, 0, 16),
            col(INDP, 0, 2048, 0, 16), 0.0, ALU.add, ALU.bypass),
            waits=[(s_vec, s_vec.n)], inc=s_vec)

        P.op("vector", lambda E: E.tensor_scalar(
            col(PBASE, 2, 10, 0, 16), col(THRESH, 0, 8, 0, 16),
            col(INDP, 2047, 2048, 0, 16), None, ALU.is_lt),
            waits=[(s_vec, v_scan), (s_lc, ld_const)], inc=s_vec)
        P.op("vector", lambda E: E.reduce_sum(col(PBASE, 0, 1, 0, 16),
                                              col(PBASE, 2, 10, 0, 16), axis=AX.X),
             waits=[(s_vec, s_vec.n)], inc=s_vec)
        v_pad = P.op("vector", lambda E: E.tensor_scalar(
            col(PBASE, 1, 2, 0, 16), col(PBASE, 0, 1, 0, 16), float(BLK), None, ALU.mult),
            waits=[(s_vec, s_vec.n)], inc=s_vec)
        p_tp = P.op("tensor", lambda E: E.transpose(
            pb[6][0:1, 0:16], col(PBASE, 1, 2, 0, 16), col(IDENT, 0, 16, 0, 16)),
            waits=[(s_vec, v_pad), bw(6)], inc=s_pe)
        v_pr = P.op("vector", lambda E: E.tensor_copy(col(PBASE, 18, 34, 0, 1), pb[6][0:1, 0:16]),
                    waits=[(s_pe, p_tp), (s_vec, s_vec.n)], inc=s_vec)
        pbf[6] = ("vec", v_pr)
        v_in = P.op("vector", lambda E: E.tensor_tensor_scan(
            col(PBASE, 2, 18, 0, 1), col(PBASE, 18, 34, 0, 1), col(PBASE, 18, 34, 0, 1),
            0.0, ALU.add, ALU.bypass),
            waits=[(s_vec, v_pr)], inc=s_vec)
        v_ex = P.op("vector", lambda E: E.tensor_sub(
            col(PBASE, 2, 18, 0, 1), col(PBASE, 2, 18, 0, 1), col(PBASE, 18, 34, 0, 1)),
            waits=[(s_vec, v_in)], inc=s_vec)
        p_bc2 = P.op("tensor", lambda E: E.matmul(
            pb[6][:, 0:16], PB[0:1, ONES: ONES + 128], col(PBASE, 2, 18, 0, 1),
            start=True, stop=True),
            waits=[(s_vec, v_ex), bw(6)], inc=s_pe)
        v_btm = P.op("vector", lambda E: E.tensor_copy(col(BASE_TM, 0, 16), pb[6][:, 0:16]),
                     waits=[(s_pe, p_bc2), (s_vec, s_vec.n)], inc=s_vec)
        pbf[6] = ("vec", v_btm)
        p_tb = P.op("tensor", lambda E: E.transpose(
            pb[6][0:16, 0:1], col(PBASE, 2, 18, 0, 1), col(IDENT, 0, 1, 0, 1)),
            waits=[(s_vec, v_btm)], inc=s_pe)
        v_bc3 = P.op("vector", lambda E: E.tensor_copy(col(PBASE, 34, 35, 0, 16), pb[6][0:16, 0:1]),
                     waits=[(s_pe, p_tb), (s_vec, s_vec.n)], inc=s_vec)
        pbf[6] = ("vec", v_bc3)
        v_m = P.op("vector", lambda E: E.tensor_scalar(
            col(PBASE, 2, 34, 0, 16), col(TILEST, 0, 32, 0, 16),
            col(PBASE, 34, 35, 0, 16), None, ALU.is_ge),
            waits=[(s_vec, v_bc3), (s_lc, ld_const)], inc=s_vec)
        p_te = P.op("tensor", lambda E: E.matmul(
            pb[6][0:1, 0:32], PB[0:16, ONES: ONES + 1], col(PBASE, 2, 34, 0, 16),
            start=True, stop=True),
            waits=[(s_vec, v_m)], inc=s_pe)
        v_te = P.op("vector", lambda E: E.tensor_copy(col(PBASE, 2, 34, 0, 1), pb[6][0:1, 0:32]),
                    waits=[(s_pe, p_te), (s_vec, s_vec.n)], inc=s_vec)
        pbf[6] = ("vec", v_te)
        v_te2 = P.op("vector", lambda E: E.tensor_scalar(
            col(PBASE, 2, 34, 0, 1), col(PBASE, 2, 34, 0, 1), -1.0, None, ALU.add),
            waits=[(s_vec, v_te)], inc=s_vec)
        tilexp_v = P.op("vector", lambda E: E.tensor_copy(
            U32[0:1, 48:80], col(PBASE, 2, 34, 0, 1)),
            waits=[(s_vec, v_te2)], inc=s_vec)
        p_eb = P.op("tensor", lambda E: E.matmul(
            pb[6][:, 0:32], PB[0:1, ONES: ONES + 128], col(PBASE, 2, 34, 0, 1),
            start=True, stop=True),
            waits=[(s_vec, v_te2)], inc=s_pe)
        v_ebs2 = P.op("vector", lambda E: E.tensor_scalar(
            col(GWT, 0, 32), pb[6][:, 0:32], 512.0, None, ALU.mult),
            waits=[(s_pe, p_eb), (s_vec, s_vec.n)], inc=s_vec)
        pbf[6] = ("vec", v_ebs2)

        pt_all = 0
        for chn in range(16):
            pt_all = P.op("tensor", lambda E, chn=chn: E.transpose(
                pb[7][:, 16 * chn: 16 * chn + 16],
                col(INDP, 128 * chn, 128 * chn + 128, 0, 16),
                col(IDENT, 0, 16, 0, 16)),
                waits=([(s_vec, v_scan), bw(7)] if chn == 0 else []),
                inc=s_pe if chn == 15 else None)
        v_pos = P.op("vector", lambda E: E.tensor_copy(col(RTR, 0, 256), pb[7][:, 0:256]),
                     waits=[(s_pe, pt_all), (s_vec, s_vec.n)], inc=s_vec)
        pbf[7] = ("vec", v_pos)

        def emit_pass2_chunk(chn):
            P.op("vector", lambda E, chn=chn: E.tensor_add(
                rt(RT_SCT, RT_SCT + 16), col(RTR, 16 * chn, 16 * chn + 16),
                col(BASE_TM, 0, 16)),
                waits=[(s_vec, s_vec.n)], inc=s_vec)
            P.op("vector", lambda E, chn=chn: E.max(out=rt(RT_M8, RT_M8 + 8),
                                                    in_=col(RTS, 16 * chn, 16 * chn + 16)),
                 waits=[(s_vec, s_vec.n)], inc=s_vec)
            P.op("vector", lambda E, chn=chn: E.tensor_scalar(
                rt(RT_OH0, RT_OH0 + 16), col(RTS, 16 * chn, 16 * chn + 16),
                rt(RT_M8, RT_M8 + 1), None, ALU.is_ge),
                waits=[(s_vec, s_vec.n)], inc=s_vec)
            P.op("vector", lambda E, chn=chn: E.tensor_scalar(
                rt(RT_IND, RT_IND + 16), col(RTS, 16 * chn, 16 * chn + 16),
                rt(RT_M8 + 1, RT_M8 + 2), None, ALU.is_ge),
                waits=[(s_vec, s_vec.n)], inc=s_vec)
            P.op("vector", lambda E: E.tensor_sub(rt(RT_OH1, RT_OH1 + 16),
                                                  rt(RT_IND, RT_IND + 16),
                                                  rt(RT_OH0, RT_OH0 + 16)),
                 waits=[(s_vec, s_vec.n)], inc=s_vec)
            u32v = 0
            for r, oh in ((0, RT_OH0), (1, RT_OH1)):
                P.op("vector", lambda E, oh=oh: E.tensor_mul(
                    rt(RT_TMP, RT_TMP + 16), rt(oh, oh + 16), rt(RT_SCT, RT_SCT + 16)),
                    waits=[(s_vec, s_vec.n)], inc=s_vec)
                P.op("vector", lambda E, r=r: E.reduce_sum(
                    rt(RT_P + r, RT_P + r + 1), rt(RT_TMP, RT_TMP + 16), axis=AX.X),
                    waits=[(s_vec, s_vec.n)], inc=s_vec)
                u32v = P.op("vector", lambda E, chn=chn, r=r: E.tensor_scalar(
                    U32[:, 2 * chn + r: 2 * chn + r + 1], rt(RT_P + r, RT_P + r + 1),
                    -1.0, None, ALU.add),
                    waits=[(s_vec, s_vec.n)], inc=s_vec)
            v_val = P.op("vector", lambda E, chn=chn: E.tensor_copy(
                rt(RT_VAL, RT_VAL + 1), col(IOTA, chn, chn + 1)),
                waits=[(s_vec, s_vec.n), (s_lc, ld_const), (s_stk, s_stk.n)], inc=s_vec)
            for r in range(2):
                P.op("gpsimd", lambda E, chn=chn, r=r: E.indirect_dma_start(
                    out=tok_d[:],
                    out_offset=bass.IndirectOffsetOnAxis(U32[:, 2 * chn + r: 2 * chn + r + 1], 0),
                    in_=rt(RT_VAL, RT_VAL + 1), in_offset=None),
                    waits=[(s_vec, u32v), (s_vec, v_val), (s_stk, stk_zero)],
                    inc=s_stk, k=16)
        accv = [[xn2_done, xn2_done] for _ in range(16)]
        down_prev = 0
        glast = ulast = dlast = 0
        for t4 in range(4):
            for mc in range(4):
                emit_pass2_chunk(4 * t4 + mc)
                gbk, ubk = mc % 2, 2 + mc % 2
                pe_g = pe_u = 0
                for dch in range(8):
                    first, last = dch == 0, dch == 7
                    wgu = ([(s_lwg, wg_ld), (s_vec, xn2_done), bw(gbk)] if first else [])
                    pe_g = P.op("tensor", lambda E, mc=mc, dch=dch, t4=t4, gbk=gbk, first=first, last=last: E.matmul(
                        pb[gbk][:],
                        col(RC_WG, 512 * dch + 128 * mc, 512 * dch + 128 * mc + 128),
                        col(RA_XN2, 2048 * dch + 512 * t4, 2048 * dch + 512 * t4 + 512),
                        start=first, stop=last),
                        waits=wgu, inc=s_pe if last else None)
                for dch in range(8):
                    first, last = dch == 0, dch == 7
                    wuu = ([(s_lwu, wu_ld), bw(ubk)] if first else [])
                    pe_u = P.op("tensor", lambda E, mc=mc, dch=dch, t4=t4, ubk=ubk, first=first, last=last: E.matmul(
                        pb[ubk][:],
                        col(RC_WU, 512 * dch + 128 * mc, 512 * dch + 128 * mc + 128),
                        col(RA_XN2, 2048 * dch + 512 * t4, 2048 * dch + 512 * t4 + 512),
                        start=first, stop=last),
                        waits=wuu, inc=s_pe if last else None)
                glast, ulast = pe_g, pe_u
                a_sg = P.op("scalar", lambda E, gbk=gbk: E.activation(
                    col(SGQ, 0, 512), pb[gbk][:], AF.Silu),
                    waits=[(s_pe, pe_g), (s_act, s_act.n), (s_vec, s_vec.n)], inc=s_act)
                pbf[gbk] = ("act", a_sg)
                v_hm = P.op("vector", lambda E, mc=mc, ubk=ubk: E.tensor_tensor(
                    col(HMID, 512 * mc, 512 * mc + 512), col(SGQ, 0, 512), pb[ubk][:],
                    ALU.mult),
                    waits=[(s_act, a_sg), (s_pe, pe_u), (s_pe, down_prev), (s_vec, s_vec.n)],
                    inc=s_vec)
                pbf[ubk] = ("vec", v_hm)
            hm_all = s_vec.n
            for tt in range(4):
                jg = 4 * t4 + tt
                for half in range(2):
                    bk = 4 + (tt * 2 + half) % 4
                    pe_d = 0
                    for mc in range(4):
                        first, last = mc == 0, mc == 3
                        wdd = ([(s_lwd, wd_ld), (s_vec, hm_all), bw(bk)] if first else [])
                        pe_d = P.op("tensor", lambda E, tt=tt, half=half, mc=mc, bk=bk, first=first, last=last: E.matmul(
                            pb[bk][:],
                            PB[:, HMID + 512 * mc + 128 * tt: HMID + 512 * mc + 128 * tt + 128],
                            col(RC_WD, 1024 * mc + 512 * half, 1024 * mc + 512 * half + 512),
                            start=first, stop=last),
                            waits=wdd, inc=s_pe if last else None)
                    dlast = pe_d
                    v_add = P.op("vector", lambda E, jg=jg, half=half, bk=bk: E.tensor_add(
                        col(RB_ACC, 1024 * jg + 512 * half, 1024 * jg + 512 * half + 512),
                        col(RB_ACC, 1024 * jg + 512 * half, 1024 * jg + 512 * half + 512),
                        pb[bk][:]),
                        waits=[(s_pe, pe_d), (s_vec, accv[jg][half]), (s_vec, s_vec.n)],
                        inc=s_vec)
                    accv[jg][half] = v_add
                    pbf[bk] = ("vec", v_add)
            down_prev = dlast
        routing_v = s_vec.n
        stk_all = s_stk.n

        sh_pe = s_pe.n
        wregs = {}

        def wload(b, free_waits):
            sem = s_wgb[b]
            eng = "sync" if b % 2 == 0 else "scalar"

            def f(E, b=b, sem=sem, eng=eng):
                re = wregs.get(eng)
                if re is None:
                    re = wregs[eng] = E.alloc_register(f"rw_{eng}")
                E.reg_load(re, U32[0:1, 48 + b: 49 + b])
                E.reg_mul(re, re, 2 * D * M)
                E.dma_start(
                    out=bass.AP(PB, RC_WG, [[NCOLS, 128], [512, 16], [1, 512]]),
                    in_=bass.AP(wgu_d, re, [[512, 128], [65536, 16], [1, 512]]),
                ).then_inc(sem.h, 16)
            P.op(eng, f, waits=free_waits + [(s_vec, tilexp_v)], inc=None)
            sem.n += 16
            return sem

        wdu_free = [0, 0]

        def wd_load(b, free_waits):
            par = b % 2
            P.op("vector", lambda E, b=b: E.tensor_scalar(
                rt(RT_P + 4, RT_P + 8), col(IOTA, 0, 4), col(GWT, b, b + 1),
                None, ALU.add),
                waits=[(s_vec, s_vec.n)], inc=s_vec)
            v_wdu = P.op("vector", lambda E, par=par: E.tensor_copy(
                U32[:, 40 + 4 * par: 44 + 4 * par], rt(RT_P + 4, RT_P + 8)),
                waits=[(s_vec, s_vec.n), (s_wdg, wdu_free[par])], inc=s_vec)
            ld = 0
            for mc in range(4):
                ld = P.op("gpsimd", lambda E, par=par, mc=mc: E.indirect_dma_start(
                    out=col(RC_WD, 1024 * mc, 1024 * mc + 1024),
                    out_offset=None, in_=wed_d[:],
                    in_offset=bass.IndirectOffsetOnAxis(
                        U32[:, 40 + 4 * par + mc: 41 + 4 * par + mc], 0)),
                    waits=(free_waits + [(s_vec, v_wdu)]) if mc == 0 else [],
                    inc=s_wdg, k=16)
            wdu_free[par] = ld
            return ld

        tid_free = [0, 0]
        xgtm_free = [sh_pe, sh_pe]
        st_xg = {}

        def stage_tok(b, first=False):
            par = b % 2
            ld_tid = 0
            for t in range(2):
                ld_tid = P.op("sync", lambda E, b=b, t=t, par=par: E.dma_start(
                    out=rt(RT_TID + 4 * par + t, RT_TID + 4 * par + t + 1),
                    in_=bass.AP(tok_d, BLK * b + 128 * t, [[1, 128], [1, 1]])),
                    waits=([(s_stk, stk_all), (s_vec, tid_free[par])] if t == 0 else []),
                    inc=s_ltk[par], k=16)
            v_u32 = P.op("vector", lambda E, par=par: E.tensor_copy(
                U32[:, 32 + 4 * par: 34 + 4 * par],
                rt(RT_TID + 4 * par, RT_TID + 4 * par + 2)),
                waits=[(s_ltk[par], ld_tid), (s_vec, s_vec.n)], inc=s_vec)
            tid_free[par] = v_u32
            xg = 0
            for t in range(2):
                w_g = [(s_vec, v_u32), (s_pe, xgtm_free[par])] if t == 0 else []
                if t == 0 and first:
                    w_g = w_g + xn2_st
                xg = P.op("gpsimd", lambda E, par=par, t=t: E.indirect_dma_start(
                    out=col(RA, XG_TM + 2048 * par + 1024 * t,
                            XG_TM + 2048 * par + 1024 * t + 1024),
                    out_offset=None, in_=xn2tm_d[:],
                    in_offset=bass.IndirectOffsetOnAxis(
                        U32[:, 32 + 4 * par + t: 33 + 4 * par + t], 0)),
                    waits=w_g, inc=s_lxg[par], k=16)
            st_xg[b] = xg

        wgu_cur = wload(0, [(s_pe, ulast)])
        stage_tok(0, first=True)
        wd_cur = wd_load(0, [(s_pe, dlast)])
        gu_prev = sh_pe
        seo_last = [0, 0, 0, 0]
        for b in range(NBLK):
            par = b % 2
            wgl, wdl = wgu_cur, wd_cur
            if b + 1 < NBLK:
                stage_tok(b + 1)
            cnt = 0
            pt_last = 0
            for t in range(2):
                for dch in range(8):
                    bk = 6 + cnt % 2
                    cnt += 1
                    pt_last = P.op("tensor", lambda E, par=par, t=t, dch=dch, bk=bk: E.transpose(
                        pb[bk][:, 0:128],
                        col(RA, XG_TM + 2048 * par + 1024 * t + 128 * dch,
                            XG_TM + 2048 * par + 1024 * t + 128 * dch + 128),
                        col(IDENT, 0, 128)),
                        waits=[(s_lxg[par], st_xg[b]), bw(bk)], inc=s_pe)
                    vcp = P.op("vector", lambda E, t=t, dch=dch, bk=bk: E.tensor_copy(
                        col(RA, XG_FM + 256 * dch + 128 * t,
                            XG_FM + 256 * dch + 128 * t + 128),
                        pb[bk][:, 0:128]),
                        waits=[(s_pe, pt_last), (s_pe, gu_prev), (s_vec, s_vec.n)], inc=s_vec)
                    pbf[bk] = ("vec", vcp)
            xgtm_free[par] = pt_last
            xgfm_v = s_vec.n
            for mc in range(4):
                gbk, ubk = mc % 2, 2 + mc % 2
                pe_g = pe_u = 0
                for dch in range(8):
                    first, last = dch == 0, dch == 7
                    wgu = ([(wgl, 16), (s_vec, xgfm_v), bw(gbk)] if first else [])
                    pe_g = P.op("tensor", lambda E, mc=mc, dch=dch, gbk=gbk, first=first, last=last: E.matmul(
                        pb[gbk][:, 0:256],
                        col(RC_WG, 512 * dch + 128 * mc, 512 * dch + 128 * mc + 128),
                        col(RA, XG_FM + 256 * dch, XG_FM + 256 * dch + 256),
                        start=first, stop=last),
                        waits=wgu, inc=s_pe if last else None)
                for dch in range(8):
                    first, last = dch == 0, dch == 7
                    wuu = ([(wgl, 16), bw(ubk)] if first else [])
                    pe_u = P.op("tensor", lambda E, mc=mc, dch=dch, ubk=ubk, first=first, last=last: E.matmul(
                        pb[ubk][:, 0:256],
                        col(RC_WU, 512 * dch + 128 * mc, 512 * dch + 128 * mc + 128),
                        col(RA, XG_FM + 256 * dch, XG_FM + 256 * dch + 256),
                        start=first, stop=last),
                        waits=wuu, inc=s_pe if last else None)
                glast, ulast = pe_g, pe_u
                a_sg = P.op("scalar", lambda E, gbk=gbk: E.activation(
                    col(SGQ, 0, 256), pb[gbk][:, 0:256], AF.Silu),
                    waits=[(s_pe, pe_g), (s_act, s_act.n), (s_vec, s_vec.n)], inc=s_act)
                pbf[gbk] = ("act", a_sg)
                v_hm = P.op("vector", lambda E, mc=mc, ubk=ubk: E.tensor_tensor(
                    col(HMID, 256 * mc, 256 * mc + 256), col(SGQ, 0, 256),
                    pb[ubk][:, 0:256], ALU.mult),
                    waits=[(s_act, a_sg), (s_pe, pe_u), (s_pe, down_prev), (s_vec, s_vec.n)],
                    inc=s_vec)
                pbf[ubk] = ("vec", v_hm)
            gu_prev = s_pe.n
            hm_all = s_vec.n
            if b + 1 < NBLK:
                wgu_cur = wload(b + 1, [(s_pe, ulast)])
            for sc in range(2):
                for half in range(2):
                    slot = sc * 2 + half
                    bk = 4 + slot
                    pe_d = 0
                    for mc in range(4):
                        first, last = mc == 0, mc == 3
                        wdd = ([(s_wdg, wdl), (s_vec, hm_all), bw(bk)] if first else [])
                        pe_d = P.op("tensor", lambda E, sc=sc, half=half, mc=mc, bk=bk, first=first, last=last: E.matmul(
                            pb[bk][:],
                            PB[:, HMID + 256 * mc + 128 * sc: HMID + 256 * mc + 128 * sc + 128],
                            col(RC_WD, 1024 * mc + 512 * half, 1024 * mc + 512 * half + 512),
                            start=first, stop=last),
                            waits=wdd, inc=s_pe if last else None)
                    dlast = pe_d
                    vcp2 = P.op("vector", lambda E, slot=slot, bk=bk: E.tensor_copy(
                        col(RA, EOB + 512 * slot, EOB + 512 * slot + 512), pb[bk][:]),
                        waits=[(s_pe, pe_d), (s_seo[slot], seo_last[slot]), (s_vec, s_vec.n)],
                        inc=s_vec)
                    seo_last[slot] = P.op("gpsimd", lambda E, b=b, sc=sc, half=half, slot=slot: E.dma_start(
                        out=eo_d[BLK * b + 128 * sc: BLK * b + 128 * sc + 128,
                                 512 * half: 512 * half + 512],
                        in_=col(RA, EOB + 512 * slot, EOB + 512 * slot + 512)),
                        waits=[(s_vec, vcp2)], inc=s_seo[slot], k=16)
                    pbf[bk] = ("vec", vcp2)
            down_prev = dlast
            if b + 1 < NBLK:
                wd_cur = wd_load(b + 1, [(s_pe, dlast)])

        eo_all = [(s_seo[i], s_seo[i].n) for i in range(4)]
        gb_free = [0, 0]
        for chn in range(16):
            for r in range(2):
                par = (2 * chn + r) % 2
                g_v = P.op("gpsimd", lambda E, chn=chn, r=r, par=par: E.indirect_dma_start(
                    out=col(RA, GB + 1024 * par, GB + 1024 * par + 1024),
                    out_offset=None, in_=eo_d[:],
                    in_offset=bass.IndirectOffsetOnAxis(
                        U32[:, 2 * chn + r: 2 * chn + r + 1], 0)),
                    waits=eo_all + [(s_vec, routing_v), (s_vec, gb_free[par])],
                    inc=s_lgb[par], k=16)
                v_a = P.op("vector", lambda E, chn=chn, r=r, par=par: E.scalar_tensor_tensor(
                    col(RB_ACC, 1024 * chn, 1024 * chn + 1024),
                    col(RA, GB + 1024 * par, GB + 1024 * par + 1024),
                    rt(RT_V + 4 * chn + r, RT_V + 4 * chn + r + 1),
                    col(RB_ACC, 1024 * chn, 1024 * chn + 1024),
                    ALU.mult, ALU.add),
                    waits=[(s_lgb[par], g_v), (s_vec, accv[chn][0]), (s_vec, accv[chn][1]),
                           (s_vec, s_vec.n)],
                    inc=s_vec)
                gb_free[par] = v_a
                accv[chn][0] = v_a
                accv[chn][1] = v_a
        for j in range(16):
            P.op("gpsimd", lambda E, j=j: E.dma_start(
                out=out_d[128 * j: 128 * j + 128, :],
                in_=col(RB_ACC, 1024 * j, 1024 * j + 1024)),
                waits=[(s_vec, accv[j][0])], inc=s_so, k=16)
        P.wait("gpsimd", [(s_so, s_so.n)])

        @block.sync
        def _(E):
            for fn in P.ops["sync"]:
                fn(E)

        @block.tensor
        def _(E):
            for fn in P.ops["tensor"]:
                fn(E)

        @block.vector
        def _(E):
            for fn in P.ops["vector"]:
                fn(E)

        @block.scalar
        def _(E):
            for fn in P.ops["scalar"]:
                fn(E)

        @block.gpsimd
        def _(E):
            for fn in P.ops["gpsimd"]:
                fn(E)

    return nc


_NC_CACHE = None


def _get_program():
    global _NC_CACHE
    if _NC_CACHE is None:
        _NC_CACHE = build_program()
    return _NC_CACHE


def make_in_maps(inputs):
    x = np.asarray(inputs["L0_attn_x"], dtype=np.float32)
    WQ = np.asarray(inputs["L0_attn_WQ"], dtype=np.float32)
    bQ = np.asarray(inputs["L0_attn_bQ"], dtype=np.float32)
    WK = np.asarray(inputs["L0_attn_WK"], dtype=np.float32)
    bK = np.asarray(inputs["L0_attn_bK"], dtype=np.float32)
    WV = np.asarray(inputs["L0_attn_WV"], dtype=np.float32)
    bV = np.asarray(inputs["L0_attn_bV"], dtype=np.float32)
    WO = np.asarray(inputs["L0_attn_WO"], dtype=np.float32)
    bO = np.asarray(inputs["L0_attn_bO"], dtype=np.float32)
    gw = np.asarray(inputs["L0_ffn_gate_w"], dtype=np.float32)
    weg = np.asarray(inputs["L0_ffn_w_gate"], dtype=np.float32)
    weu = np.asarray(inputs["L0_ffn_w_up"], dtype=np.float32)
    wed = np.asarray(inputs["L0_ffn_w_down"], dtype=np.float32)
    wsg = np.asarray(inputs["L0_ffn_ws_gate"], dtype=np.float32)
    wsu = np.asarray(inputs["L0_ffn_ws_up"], dtype=np.float32)
    wsd = np.asarray(inputs["L0_ffn_ws_down"], dtype=np.float32)

    wgu_f = np.ascontiguousarray(np.concatenate(
        [weg.reshape(NE, D * M), weu.reshape(NE, D * M)], axis=1).ravel())
    wed_f = np.ascontiguousarray(wed.reshape(NE * M, D))
    ident = np.eye(128, dtype=np.float32)
    iota = (np.arange(128)[:, None] + 128 * np.arange(16)[None, :]).astype(np.float32)
    iota16 = np.broadcast_to(np.arange(16, dtype=np.float32), (128, 16)).copy()
    tilest = np.broadcast_to(256.0 * np.arange(32, dtype=np.float32), (128, 32)).copy()
    thresh = np.broadcast_to(256.0 * np.arange(8, dtype=np.float32), (128, 8)).copy()
    masks = np.stack([
        np.where(np.arange(128)[:, None] + 128 * r <= np.arange(512)[None, :],
                 0.0, -100000.0)
        for r in range(4)]).astype(ml_dtypes.bfloat16)
    in_maps = []
    for c in range(8):
        b, g = c // 2, c % 2
        hs = slice(512 * g, 512 * (g + 1))
        in_maps.append({
            "x": np.ascontiguousarray(x[b]),
            "xres": np.ascontiguousarray(x[b, TOK * g: TOK * (g + 1)]),
            "wq": np.ascontiguousarray(WQ[:, hs]),
            "wk": np.ascontiguousarray(WK[:, hs]),
            "wv": np.ascontiguousarray(WV[:, hs]),
            "bq": np.ascontiguousarray(bQ[hs].reshape(4, 128).T),
            "bk": np.ascontiguousarray(bK[hs].reshape(4, 128).T),
            "bv": np.ascontiguousarray(bV[hs].reshape(4, 128).T),
            "wo": WO,
            "bo": np.ascontiguousarray(bO.reshape(1, 1024)),
            "gwt": np.ascontiguousarray(gw.T),
            "wsg": wsg, "wsu": wsu, "wsd": wsd,
            "wgu": wgu_f, "wed": wed_f,
            "ident": ident, "iota": iota, "iota16": iota16, "masks": masks,
            "tilest": tilest, "thresh": thresh,
        })
    return in_maps


def kernel(**inputs):
    nc = _get_program()
    in_maps = make_in_maps(inputs)
    res = run_bass_kernel_spmd(nc, in_maps, core_ids=list(range(8)))
    out = np.empty((B, S, D), dtype=np.float32)
    for c in range(8):
        b, g = c // 2, c % 2
        out[b, TOK * g: TOK * (g + 1), :] = np.asarray(res.results[c]["out"])
    return out
